# version 09
# baseline (speedup 1.0000x reference)
import numpy as np
import ml_dtypes
import concourse.bass as bass
import concourse.mybir as mybir
from concourse.bass_utils import run_bass_kernel_spmd

F32 = mybir.dt.float32
BF16 = mybir.dt.bfloat16
I32 = mybir.dt.int32
AF = mybir.ActivationFunctionType
ALU = mybir.AluOpType
AX = mybir.AxisListType
BF = ml_dtypes.bfloat16

NCORES = 8
SEQ = 4096
D = 4096
NT = 8
TPC = NT * 128
NSLOT = 8
H, KV, HD = 32, 8, 128
GROUPS = H // KV
SYS_LEN, IMG_LEN = 35, 576
ENH, SUP = 1.5, 0.5
ROPE_BASE = 10000.0
LN_EPS = 1e-5
ENGINES = ("sync", "scalar", "vector", "tensor", "gpsimd")


def tile_of(j, i):
    return 4 * i + ((j + i) % 4)



class _Op:
    __slots__ = ("eng", "dma", "fn", "deps", "inc_needed", "kidx", "mark")

    def __init__(self, eng, dma, fn):
        self.eng = eng
        self.dma = dma
        self.fn = fn
        self.deps = []
        self.inc_needed = False
        self.kidx = -1
        self.mark = -1


class Buf:
    __slots__ = ("name", "writer", "readers")

    def __init__(self, name):
        self.name = name
        self.writer = None
        self.readers = []


class Prog:
    def __init__(self, nc):
        self.nc = nc
        self.ops = {e: [] for e in ENGINES}

    def op(self, eng, fn, reads=(), writes=(), dma=False):
        o = _Op(eng, dma, fn)
        deps = []
        for b in reads:
            if b.writer is not None:
                deps.append(b.writer)
        for b in writes:
            if b.writer is not None:
                deps.append(b.writer)
            deps.extend(b.readers)
        seen = set()
        for d in deps:
            if d is o or id(d) in seen:
                continue
            seen.add(id(d))
            if d.eng == eng and not d.dma and not dma and eng == "tensor":
                continue
            o.deps.append(d)
        self.ops[eng].append(o)
        for b in writes:
            b.writer = o
            b.readers = []
        for b in reads:
            b.readers.append(o)
        return o

    def barrier(self):
        lasts = {e: (self.ops[e][-1] if self.ops[e] else None) for e in ENGINES}
        for e in ENGINES:
            if not self.ops[e]:
                continue
            o = _Op(e, False, None)
            for d in lasts.values():
                if d is not None and (d.eng != e or d.dma):
                    o.deps.append(d)
            self.ops[e].append(o)

    def finalize(self, ctx):
        nc = self.nc
        self.barrier()
        for e in ENGINES:
            for o in self.ops[e]:
                if o.dma:
                    o.inc_needed = True
                for d in o.deps:
                    d.inc_needed = True
        sems = {}
        for e in ENGINES:
            nd = nc_ = 0
            dm = cm = 0
            for o in self.ops[e]:
                if o.fn is None:
                    continue
                if o.dma:
                    o.kidx = nd
                    nd += 1
                    if o.inc_needed:
                        dm += 1
                    o.mark = dm
                else:
                    o.kidx = nc_
                    nc_ += 1
                    if o.inc_needed:
                        cm += 1
                    o.mark = cm
            if dm:
                sems[(e, True)] = ctx.enter_context(nc.semaphore(f"sem_{e}_d"))
            if cm:
                sems[(e, False)] = ctx.enter_context(nc.semaphore(f"sem_{e}_c"))
        block = ctx.enter_context(nc.Block())
        for e in ENGINES:
            if not self.ops[e]:
                continue
            ops = self.ops[e]

            def mk(eng, ops):
                def body(engine):
                    floor = {}
                    for o in ops:
                        need = {}
                        for d in o.deps:
                            key = (d.eng, d.dma)
                            need[key] = max(need.get(key, 0), d.mark)
                        for key, m in sorted(need.items()):
                            unit = 16 if key[1] else 1
                            if floor.get(key, 0) >= m:
                                continue
                            floor[key] = m
                            engine.wait_ge(sems[key], unit * m)
                        if o.fn is None:
                            continue
                        inst = o.fn(engine)
                        if o.inc_needed:
                            if o.dma:
                                inst.then_inc(sems[(o.eng, True)], 16)
                            else:
                                inst.then_inc(sems[(o.eng, False)], 1)
                return body

            getattr(block, e)(mk(e, ops))



class Arena:

    def __init__(self, nc, ctx):
        self.nc = nc
        self.ctx = ctx
        self.stack = [ctx]
        self.names = [[]]
        self.t = {}
        self.b = {}

    def push(self, es):
        self.stack.append(es)
        self.names.append([])

    def pop(self):
        for n in self.names.pop():
            self.t.pop(n, None)
            self.b.pop(n, None)
        self.stack.pop()

    def sbuf(self, name, shape, dtype, nbuf=None):
        if name in self.t:
            return self.t[name]
        self._uid = getattr(self, "_uid", 0) + 1
        h = self.stack[-1].enter_context(self.nc.sbuf_tensor(f"sb{self._uid}_" + name, shape, dtype))
        self.t[name] = h
        self.b[name] = Buf(name)
        self.names[-1].append(name)
        return h

    def psum(self, name, shape, dtype=F32):
        if name in self.t:
            return self.t[name]
        per_bank = 512 if dtype == F32 else 1024
        shape = [shape[0], max(shape[1], per_bank)]
        assert shape[1] % per_bank == 0, (name, shape, dtype)
        self._uid = getattr(self, "_uid", 0) + 1
        h = self.stack[-1].enter_context(self.nc.psum_tensor(f"ps{self._uid}_" + name, shape, dtype))
        self.t[name] = h
        self.b[name] = Buf(name)
        self.names[-1].append(name)
        return h

    def buf(self, name):
        b = Buf(name)
        self.b[name] = b
        return b

    def alias(self, newname, oldname):
        self.t[newname] = self.t[oldname]
        self.b[newname] = self.b[oldname]
        self.names[-1].append(newname)

    def const_col(self, p, value):
        key = ("cc", float(value))
        if key not in self.t:
            name = f"cc{len([k for k in self.t if isinstance(k, tuple)])}"
            h = self.sbuf(name, [128, 1], F32)
            b = self.b[name]
            p.op("vector", lambda e: e.memset(h[:, :], float(value)), writes=[b])
            self.t[key] = (h, b)
        return self.t[key]



def emit_ln(p, ar, pref, xin_ap, xin_buf, xout_ap, xout_buf, width=D,
            scratch=None):
    nc_ = (width + 511) // 512
    if scratch is None:
        st = ar.sbuf(pref + "_st", [128, 6 * nc_], F32)
        mv = ar.sbuf(pref + "_mv", [128, 2], F32)
        rstd = ar.sbuf(pref + "_rs", [128, 1], F32)
        nmr = ar.sbuf(pref + "_nm", [128, 1], F32)
        bst, bmv, brs, bnm = (ar.b[pref + "_st"], ar.b[pref + "_mv"],
                              ar.b[pref + "_rs"], ar.b[pref + "_nm"])
    else:
        (st, bst), (mv, bmv), (rstd, brs), (nmr, bnm) = scratch
    for c in range(nc_):
        w = min(512, width - 512 * c)
        p.op("vector", lambda e, c=c, w=w: e.bn_stats(st[:, 6 * c:6 * c + 6],
                                                      xin_ap[:, 512 * c:512 * c + w]),
             reads=[xin_buf], writes=[bst])
    p.op("vector", lambda e: e.bn_aggr(mv[:, :], st[:, :]), reads=[bst], writes=[bmv])
    eps_h, eps_b = ar.const_col(p, LN_EPS)
    nh_h, nh_b = ar.const_col(p, -0.5)
    p.op("scalar", lambda e: e.activation(rstd[:, :], mv[:, 1:2], AF.Ln,
                                          bias=eps_h[:, :]),
         reads=[bmv, eps_b], writes=[brs])
    p.op("scalar", lambda e: e.activation(rstd[:, :], rstd[:, :], AF.Exp,
                                          scale=nh_h[:, :]),
         reads=[brs, nh_b], writes=[brs])
    p.op("vector", lambda e: e.tensor_scalar(nmr[:, :], mv[:, 0:1], rstd[:, :], -1.0,
                                             ALU.mult, ALU.mult),
         reads=[bmv, brs], writes=[bnm])
    p.op("vector", lambda e: e.tensor_scalar(xout_ap, xin_ap, rstd[:, :], nmr[:, :],
                                             ALU.mult, ALU.add),
         reads=[xin_buf, bnm, brs], writes=[xout_buf])



def emit_ph0(p, ar, aps, dbg_blocks=None, dbg_tiles=None, dbg_notr=False, dbg_noln=False):
    nc = p.nc
    xt = [ar.sbuf(f"p0_x{i}", [128, D], F32) for i in range(2)]
    xln = [ar.sbuf(f"p0_l{i}", [128, D], BF16) for i in range(2)]
    xlnT = ar.sbuf("xlnT", [128, 32 * 1024], BF16)
    bxlnT = [[ar.buf(f"xlnT_{c}_{t}") for t in range(NT)] for c in range(32)]
    wblk = [ar.sbuf(f"p0_w{i}", [128, 16384], BF16) for i in range(2)]
    cosb = ar.sbuf("p0_cos", [128, NT * 128], F32)
    sinb = ar.sbuf("p0_sin", [128, NT * 128], F32)
    cosqb = ar.sbuf("p0_cosq", [128, NT * 128], F32)
    sinqb = ar.sbuf("p0_sinq", [128, NT * 128], F32)
    eye = ar.sbuf("eye", [128, 128], BF16)
    ropa = ar.sbuf("p0_ra", [128, 512], F32)
    ropb = ar.sbuf("p0_rb", [128, 512], F32)
    qout = ar.sbuf("p0_qo", [128, 512], BF16)
    qsq = ar.sbuf("p0_qsq", [128, 512], F32)
    qn2 = ar.sbuf("qn2", [128, NT * 32], F32)
    kn2 = ar.sbuf("p0_kn2", [128, NT * 8], F32)
    stg = [ar.sbuf(f"p0_s{i}", [128, 512], BF16) for i in range(2)]
    pq = [ar.psum(f"p0_pq{i}", [128, 512], F32) for i in range(2)]
    ptr = [ar.psum(f"p0_pt{i}", [128, 512], BF16) for i in range(2)]

    bxt = [ar.b[f"p0_x{i}"] for i in range(2)]
    bxln = [ar.b[f"p0_l{i}"] for i in range(2)]
    bwblk = [ar.b[f"p0_w{i}"] for i in range(2)]
    bcos, bsin, beye = ar.b["p0_cos"], ar.b["p0_sin"], ar.b["eye"]
    bcosq, bsinq = ar.b["p0_cosq"], ar.b["p0_sinq"]
    bra, brb, bqo = ar.b["p0_ra"], ar.b["p0_rb"], ar.b["p0_qo"]
    bqsq, bqn2, bkn2 = ar.b["p0_qsq"], ar.b["qn2"], ar.b["p0_kn2"]
    bstg = [ar.b[f"p0_s{i}"] for i in range(2)]
    bpq = [ar.b[f"p0_pq{i}"] for i in range(2)]
    bptr = [ar.b[f"p0_pt{i}"] for i in range(2)]

    p.op("sync", lambda e: e.dma_start(eye[:, :], aps["eye"]), writes=[beye], dma=True)
    p.op("sync", lambda e: e.dma_start(
        cosb[:, :].rearrange("p (t n) -> p t n", n=128),
        aps["cos"].rearrange("(t p) n -> p t n", p=128)),
        writes=[bcos], dma=True)
    p.op("sync", lambda e: e.dma_start(
        sinb[:, :].rearrange("p (t n) -> p t n", n=128),
        aps["sin"].rearrange("(t p) n -> p t n", p=128)),
        writes=[bsin], dma=True)
    p.op("sync", lambda e: e.dma_start(
        cosqb[:, :].rearrange("p (t n) -> p t n", n=128),
        aps["cosq"].rearrange("(t p) n -> p t n", p=128)),
        writes=[bcosq], dma=True)
    p.op("sync", lambda e: e.dma_start(
        sinqb[:, :].rearrange("p (t n) -> p t n", n=128),
        aps["sinq"].rearrange("(t p) n -> p t n", p=128)),
        writes=[bsinq], dma=True)

    for t in (range(NT) if dbg_tiles is None else dbg_tiles):
        xb, lb = xt[t % 2], xln[t % 2]
        bxb, blb = bxt[t % 2], bxln[t % 2]
        p.op("sync", lambda e, t=t, xb=xb: e.dma_start(
            xb[:, :], aps["x0s"][128 * t:128 * (t + 1), :]),
            writes=[bxb], dma=True)
        if not dbg_noln:
            emit_ln(p, ar, f"p0ln{t}", xb[:, :], bxb, lb[:, :], blb)
        else:
            p.op("vector", lambda e, xb=xb, lb=lb: e.tensor_copy(lb[:, :], xb[:, :]), reads=[bxb], writes=[blb])
        for c in (range(32) if not dbg_notr else []):
            pb, bpb = ptr[c % 2], bptr[c % 2]
            p.op("tensor", lambda e, lb=lb, c=c, pb=pb: e.transpose(
                pb[:, 0:128], lb[:, 128 * c:128 * (c + 1)], eye[:, :]),
                reads=[blb, beye], writes=[bpb])
            eng = "vector" if c % 2 == 0 else "scalar"
            dst = xlnT[:, 1024 * c + 128 * t:1024 * c + 128 * (t + 1)]
            if eng == "vector":
                p.op(eng, lambda e, dst=dst, pb=pb: e.tensor_copy(dst, pb[:, 0:128]),
                     reads=[bpb], writes=[bxlnT[c][t]])
            else:
                p.op(eng, lambda e, dst=dst, pb=pb: e.copy(dst, pb[:, 0:128]),
                     reads=[bpb], writes=[bxlnT[c][t]])

    for bblk in (range(12) if dbg_blocks is None else dbg_blocks):
        wb, bwb = wblk[bblk % 2], bwblk[bblk % 2]
        p.op("gpsimd", lambda e, bblk=bblk, wb=wb: e.dma_start(
            wb[:, :], aps["w0blk"][bblk]),
            writes=[bwb], dma=True)
        is_q = bblk < 8
        is_k = bblk in (8, 9)
        is_v = bblk >= 10
        for t in range(NT):
            pb, bpb = pq[t % 2], bpq[t % 2]
            for c in range(32):
                p.op("tensor", lambda e, c=c, t=t, pb=pb, wb=wb: e.matmul(
                    pb[:, :], xlnT[:, 1024 * c + 128 * t:1024 * c + 128 * (t + 1)],
                    wb[:, 512 * c:512 * (c + 1)],
                    start=(c == 0), stop=(c == 31)),
                    reads=[bxlnT[c][t], bwb], writes=[bpb])
            if is_v:
                sb, bsb = stg[t % 2], bstg[t % 2]
                p.op("scalar", lambda e, sb=sb, pb=pb: e.copy(sb[:, :], pb[:, :]),
                     reads=[bpb], writes=[bsb])
                co = 1024 + 512 * (bblk - 10)
                p.op("sync", lambda e, t=t, co=co, sb=sb: e.dma_start(
                    aps["kvloc"][128 * t:128 * (t + 1), co:co + 512], sb[:, :]),
                    reads=[bsb], dma=True)
            else:
                cs = (cosqb if is_q else cosb)[:, 128 * t:128 * (t + 1)]
                sn = (sinqb if is_q else sinb)[:, 128 * t:128 * (t + 1)]
                p.op("scalar", lambda e, pb=pb: e.activation(qsq[:, :], pb[:, :], AF.Square),
                     reads=[bpb], writes=[bqsq])
                for hh in range(4):
                    sl = slice(128 * hh, 128 * hh + 128)
                    if is_q:
                        h = 4 * bblk + hh
                        dst = qn2[:, 32 * t + h:32 * t + h + 1]
                        p.op("vector", lambda e, sl=sl, dst=dst: e.reduce_sum(
                            dst, qsq[:, sl], axis=AX.X), reads=[bqsq], writes=[bqn2])
                    else:
                        g = 4 * (bblk - 8) + hh
                        dst = kn2[:, 8 * t + g:8 * t + g + 1]
                        p.op("vector", lambda e, sl=sl, dst=dst: e.reduce_sum(
                            dst, qsq[:, sl], axis=AX.X), reads=[bqsq], writes=[bkn2])
                for hh in range(4):
                    sl = slice(128 * hh, 128 * hh + 128)
                    lo = slice(128 * hh, 128 * hh + 64)
                    hi = slice(128 * hh + 64, 128 * hh + 128)
                    p.op("vector", lambda e, sl=sl, cs=cs, pb=pb: e.tensor_tensor(
                        ropa[:, sl], pb[:, sl], cs, ALU.mult),
                        reads=[bpb, bcos, bcosq], writes=[bra])
                    p.op("vector", lambda e, sl=sl, sn=sn, pb=pb: e.tensor_tensor(
                        ropb[:, sl], pb[:, sl], sn, ALU.mult),
                        reads=[bpb, bsin, bsinq], writes=[brb])
                    p.op("vector", lambda e, lo=lo, hi=hi: e.tensor_tensor(
                        qout[:, lo], ropa[:, lo], ropb[:, hi], ALU.subtract),
                        reads=[bra, brb], writes=[bqo])
                    p.op("vector", lambda e, lo=lo, hi=hi: e.tensor_tensor(
                        qout[:, hi], ropa[:, hi], ropb[:, lo], ALU.add),
                        reads=[bra, brb], writes=[bqo])
                if is_k:
                    co = 512 * (bblk - 8)
                    p.op("sync", lambda e, t=t, co=co: e.dma_start(
                        aps["kvloc"][128 * t:128 * (t + 1), co:co + 512], qout[:, :]),
                        reads=[bqo], dma=True)
                else:
                    sb, bsb = stg[t % 2], bstg[t % 2]
                    for hh in range(4):
                        pt, bpt = ptr[hh % 2], bptr[hh % 2]
                        p.op("tensor", lambda e, hh=hh, pt=pt: e.transpose(
                            pt[:, 0:128], qout[:, 128 * hh:128 * (hh + 1)], eye[:, :]),
                            reads=[bqo, beye], writes=[bpt])
                        p.op("scalar", lambda e, hh=hh, pt=pt, sb=sb: e.copy(
                            sb[:, 128 * hh:128 * (hh + 1)], pt[:, 0:128]),
                            reads=[bpt], writes=[bsb])
                    dst = aps["qtT"][512 * bblk:512 * (bblk + 1),
                                     128 * t:128 * (t + 1)].rearrange(
                        "(h p) n -> p h n", p=128)
                    p.op("scalar", lambda e, dst=dst, sb=sb: e.dma_start(
                        dst, sb[:, :].rearrange("p (h n) -> p h n", n=128)),
                        reads=[bsb], dma=True)

    p.op("sync", lambda e: e.dma_start(
        aps["kn2"].rearrange("(t p) g -> p t g", p=128),
        kn2[:, :].rearrange("p (t g) -> p t g", g=8)),
        reads=[bkn2], dma=True)
    return {"xlnT": xlnT, "bxlnT": bxlnT, "qn2": qn2, "bqn2": bqn2}



CBIAS_L0 = 40.0

def kc_rowbase(cc):
    ii = cc // 4
    jj = (cc % 4 - ii) % 4
    return jj * 1024 + ii * 128


def emit_ph1_attn(p, ar, aps, dbg_units=None, dbg_noks=False, dbg_nomask=False, dbg_pt=None, dbg_nonorm=False, dbg_normmode=0, dbg_dump=None, dbg_delay=""):
    eye = ar.t.get("eye")
    if eye is None:
        eye = ar.sbuf("eye", [128, 128], BF16)
        p.op("sync", lambda e: e.dma_start(eye[:, :], aps["eye"]), writes=[ar.b["eye"]], dma=True)
    beye = ar.b["eye"]
    masks = ar.sbuf("p1_masks", [128, 32 * 256], BF16)
    bmasks = ar.b["p1_masks"]
    p.op("sync", lambda e: e.dma_start(
        masks[:, :].rearrange("p (m n) -> p m n", n=256),
        aps["masks"].rearrange("m p n -> p m n")), writes=[bmasks], dma=True)
    scale1 = ar.sbuf("p1_scl", [128, 5 * 256], F32)
    bscale1 = ar.b["p1_scl"]
    p.op("sync", lambda e: e.dma_start(
        scale1[:, :].rearrange("p (m n) -> p m n", n=256),
        aps["scale1"].rearrange("m p n -> p m n")), writes=[bscale1], dma=True)
    ksb = ar.sbuf("p1_ksb", [128, 640], BF16)
    bksb = ar.b["p1_ksb"]
    p.op("sync", lambda e: e.dma_start(ksb[:, :], aps["ksb"]), writes=[bksb], dma=True)
    ones_bf = ar.sbuf("ones_bf", [128, 1], BF16)
    bones_bf = ar.b["ones_bf"]
    p.op("vector", lambda e: e.memset(ones_bf[:, :], 1.0), writes=[bones_bf])
    ones_f = ar.sbuf("ones_f", [1, 128], F32)
    bones_f = ar.b["ones_f"]
    p.op("vector", lambda e: e.memset(ones_f[:, :], 1.0), writes=[bones_f])
    cneg_h, cneg_b = ar.const_col(p, -CBIAS_L0)

    aoT = ar.sbuf("aoT", [128, 32 * 1024], BF16)
    baoT = [[ar.buf(f"aoT_{h}_{pr}") for pr in range(4)] for h in range(32)]

    vsb = ar.sbuf("p1_v", [128, 32 * 128], BF16)
    bvsb = ar.b["p1_v"]
    kT2 = [ar.sbuf(f"p1_kT{i}", [128, 4096], BF16) for i in range(2)]
    bkT2 = [ar.b[f"p1_kT{i}"] for i in range(2)]
    kTs2 = [ar.sbuf(f"p1_kTs{i}", [128, 640], BF16) for i in range(2)]
    bkTs2 = [ar.b[f"p1_kTs{i}"] for i in range(2)]
    qh = [ar.sbuf(f"p1_qh{i}", [128, 1024], BF16) for i in range(2)]
    bqh = [ar.b[f"p1_qh{i}"] for i in range(2)]
    PT = [ar.sbuf(f"p1_PT{i}", [128, 512], BF16) for i in range(2)]
    bPT = [ar.b[f"p1_PT{i}"] for i in range(2)]
    PTm = [ar.sbuf(f"p1_PTm{i}", [128, 512], BF16) for i in range(2)]
    bPTm = [ar.b[f"p1_PTm{i}"] for i in range(2)]
    sscr = [ar.sbuf(f"p1_sscr{i}", [128, 256], F32) for i in range(2)]
    bsscr = [ar.b[f"p1_sscr{i}"] for i in range(2)]
    recb = ar.sbuf("p1_recb", [1, 256], F32)
    brecb = ar.b["p1_recb"]
    recb16 = ar.sbuf("p1_recb16", [1, 256], BF16)
    brecb16 = ar.b["p1_recb16"]
    onesbr2 = ar.t.get("ones_bfr")
    if onesbr2 is None:
        onesbr2 = ar.sbuf("ones_bfr", [1, 128], BF16)
        p.op("vector", lambda e: e.memset(onesbr2[:, :], 1.0), writes=[ar.b["ones_bfr"]])
    bonesbr2 = ar.b["ones_bfr"]
    rbb = ar.sbuf("p1_rbb", [128, 256], F32)
    brbb = ar.b["p1_rbb"]
    if dbg_delay:
        dly1 = ar.sbuf("p1_dly1", [1, 256], F32)
        bdly1 = ar.b["p1_dly1"]
        dly2 = ar.sbuf("p1_dly2", [1, 256], F32)
        bdly2 = ar.b["p1_dly2"]
        bdlyA = ar.buf("p1_dlyA")
        pdly = ar.psum("p1_pdly", [128, 512], F32)
        p.op("vector", lambda e: e.memset(dly1[:, :], 1.0), writes=[bdly1])

    psc = [ar.psum(f"p1_sc{i}", [128, 512], F32) for i in range(2)]
    bpsc = [ar.b[f"p1_sc{i}"] for i in range(2)]
    ptr = [ar.psum(f"p1_tr{i}", [128, 1024], BF16) for i in range(2)]
    bptr = [ar.b[f"p1_tr{i}"] for i in range(2)]
    pout = ar.psum("p1_out", [128, 512], F32)
    bpout = ar.b["p1_out"]
    prs = ar.psum("p1_rs", [128, 512], F32)
    bprs = ar.b["p1_rs"]
    prb = ar.psum("p1_rb", [128, 512], F32)
    bprb = ar.b["p1_rb"]

    for g in range(KV):
        if dbg_units is not None and not any(u[0] == g for u in dbg_units):
            continue
        gb = g % 2
        kT, bkT = kT2[gb], bkT2[gb]
        kTs, bkTs = kTs2[gb], bkTs2[gb]
        kvsr = [aps["kvs_buf"]] if "kvs_buf" in aps else []
        p.op("sync", lambda e, g=g, kT=kT: e.dma_start(
            kT[:, :], aps["kvs"][:, 128 * g:128 * (g + 1)], transpose=True),
            reads=kvsr, writes=[bkT], dma=True)
        p.op("sync", lambda e, g=g, vsb=vsb: e.dma_start(
            vsb[:, :].rearrange("p (cc n) -> p cc n", n=128),
            aps["kvs"][:, 1024 + 128 * g:1024 + 128 * (g + 1)].rearrange(
                "(cc p) n -> p cc n", p=128)),
            reads=kvsr, writes=[bvsb], dma=True)
        for cc in range(5):
            s = kc_rowbase(cc) // 128
            p.op("vector", lambda e, s=s, cc=cc, kTs=kTs, kT=kT: e.tensor_tensor(
                kTs[:, 128 * cc:128 * (cc + 1)], kT[:, 128 * s:128 * (s + 1)],
                ksb[:, 128 * cc:128 * (cc + 1)], ALU.mult),
                reads=[bkT, bksb], writes=[bkTs])

        for h4 in range(4):
            h = 4 * g + h4
            if dbg_units is not None and not any(u[0] == g and u[1] == h4 for u in dbg_units):
                continue
            qb, bqb = qh[h % 2], bqh[h % 2]
            p.op("sync", lambda e, h=h, qb=qb: e.dma_start(
                qb[:, :], aps["qtT"][128 * h:128 * (h + 1), :]),
                writes=[bqb], dma=True)
            for pr in range(4):
                if dbg_units is not None and (g, h4, pr) not in dbg_units:
                    continue
                ncc = 8 * (pr + 1)
                qsl = qb[:, 256 * pr:256 * (pr + 1)]
                first = {"o": True}
                for cp in range(ncc // 2):
                    pb, bpb = psc[cp % 2], bpsc[cp % 2]
                    pt, bpt = PT[cp % 2], bPT[cp % 2]
                    for half in range(2):
                        cc = 2 * cp + half
                        use_ks = (pr >= 1 and cc <= 4) and not dbg_noks
                        s = kc_rowbase(cc) // 128
                        lhs = kTs[:, 128 * cc:128 * (cc + 1)] if use_ks \
                            else kT[:, 128 * s:128 * (s + 1)]
                        lhsr = [bkTs] if use_ks else [bkT]
                        p.op("tensor", lambda e, lhs=lhs, qsl=qsl, pb=pb, half=half: e.matmul(
                            pb[:, 256 * half:256 * (half + 1)], lhs, qsl,
                            start=True, stop=True),
                            reads=lhsr + [bqb], writes=[bpb])
                    if pr == 0 and cp <= 2:
                        for half in range(2):
                            cc = 2 * cp + half
                            hs = slice(256 * half, 256 * (half + 1))
                            if cc <= 4:
                                ss, bss = sscr[half], bsscr[half]
                                p.op("vector", lambda e, cc=cc, pb=pb, hs=hs, ss=ss: e.tensor_tensor(
                                    ss[:, :], pb[:, hs], scale1[:, 256 * cc:256 * (cc + 1)],
                                    ALU.mult), reads=[bpb, bscale1], writes=[bss])
                                p.op("scalar", lambda e, pt=pt, hs=hs, ss=ss: e.activation(
                                    pt[:, hs], ss[:, :], AF.Exp, bias=cneg_h[:, :]),
                                    reads=[bss, cneg_b], writes=[bpt])
                            else:
                                p.op("scalar", lambda e, pb=pb, pt=pt, hs=hs: e.activation(
                                    pt[:, hs], pb[:, hs], AF.Exp, bias=cneg_h[:, :]),
                                    reads=[bpb, cneg_b], writes=[bpt])
                    else:
                        p.op("scalar", lambda e, pb=pb, pt=pt: e.activation(
                            pt[:, :], pb[:, :], AF.Exp, bias=cneg_h[:, :]),
                            reads=[bpb, cneg_b], writes=[bpt])
                    if dbg_pt is not None and dbg_pt.get("unit") == (g, h4, pr, cp):
                        p.op("sync", lambda e, pt=pt: e.dma_start(dbg_pt["ap"], pt[:, :]),
                             reads=[bpt], dma=True)
                    for half in range(2):
                        cc = 2 * cp + half
                        hs = slice(256 * half, 256 * (half + 1))
                        src_t, src_b = pt, bpt
                        if cc >= 8 * pr and not dbg_nomask:
                            ptm, bptm = PTm[cp % 2], bPTm[cp % 2]
                            p.op("vector", lambda e, cc=cc, pt=pt, ptm=ptm, hs=hs: e.tensor_tensor(
                                ptm[:, hs], pt[:, hs], masks[:, 256 * cc:256 * (cc + 1)],
                                ALU.mult), reads=[bpt, bmasks], writes=[bptm])
                            src_t, src_b = ptm, bptm
                        s = kc_rowbase(cc) // 128
                        last = (cc == ncc - 1)
                        st_ = first["o"]
                        p.op("tensor", lambda e, s=s, src_t=src_t, hs=hs, last=last, st_=st_, vsb=vsb: e.matmul(
                            pout[:, 0:256], vsb[:, 128 * s:128 * (s + 1)], src_t[:, hs],
                            start=st_, stop=last),
                            reads=[bvsb, src_b], writes=[bpout])
                        p.op("tensor", lambda e, src_t=src_t, hs=hs, last=last, st_=st_: e.matmul(
                            prs[0:1, 0:256], ones_bf[:, :], src_t[:, hs],
                            start=st_, stop=last),
                            reads=[bones_bf, src_b], writes=[bprs])
                        first["o"] = False
                dst = aoT[:, 1024 * h + 256 * pr:1024 * h + 256 * (pr + 1)]
                if dbg_nonorm:
                    p.op("vector", lambda e, dst=dst: e.tensor_copy(dst, pout[:, 0:256]),
                         reads=[bpout], writes=[baoT[h][pr]])
                elif dbg_normmode == 3:
                    p.op("vector", lambda e: e.reciprocal(recb[:, :], prs[0:1, 0:256]),
                         reads=[bprs], writes=[brecb])
                    p.op("tensor", lambda e: e.matmul(
                        prb[:, 0:256], ones_f[:, :], recb[:, :], start=True, stop=True),
                        reads=[bones_f, brecb], writes=[bprb])
                    p.op("scalar", lambda e: e.copy(rbb[:, :], prb[:, 0:256]),
                         reads=[bprb], writes=[brbb])
                    p.op("vector", lambda e, dst=dst: e.tensor_copy(dst, pout[:, 0:256]),
                         reads=[bpout, brbb], writes=[baoT[h][pr]])
                elif dbg_normmode == 4:
                    p.op("vector", lambda e: e.reciprocal(recb[:, :], prs[0:1, 0:256]),
                         reads=[bprs], writes=[brecb])
                    p.op("vector", lambda e, dst=dst: e.tensor_copy(dst, pout[:, 0:256]),
                         reads=[bpout, brecb], writes=[baoT[h][pr]])
                elif dbg_normmode == 1:
                    p.op("vector", lambda e: e.reciprocal(recb[:, :], prs[0:1, 0:256]),
                         reads=[bprs], writes=[brecb])
                    p.op("vector", lambda e: e.tensor_copy(recb16[:, :], recb[:, :]),
                         reads=[brecb], writes=[brecb16])
                    p.op("tensor", lambda e: e.matmul(
                        prb[:, 0:256], onesbr2[:, :], recb16[:, :], start=True, stop=True),
                        reads=[bonesbr2, brecb16], writes=[bprb])
                    p.op("scalar", lambda e: e.copy(rbb[:, :], prb[:, 0:256]),
                         reads=[bprb], writes=[brbb])
                    p.op("vector", lambda e, dst=dst: e.tensor_tensor(
                        dst, pout[:, 0:256], rbb[:, :], ALU.mult),
                        reads=[bpout, brbb], writes=[baoT[h][pr]])
                else:
                    rd_rs = [bprs]
                    if "A" in dbg_delay:
                        p.op("tensor", lambda e: e.matmul(
                            pdly[:, 0:256], ones_f[:, :], dly1[:, :],
                            start=True, stop=True), reads=[bones_f, bdly1], writes=[bdlyA])
                        rd_rs = [bprs, bdlyA]
                    p.op("vector", lambda e: e.reciprocal(recb[:, :], prs[0:1, 0:256]),
                         reads=rd_rs, writes=[brecb])
                    rd_mm = [bones_f, brecb]
                    if "B" in dbg_delay:
                        p.op("vector", lambda e: e.tensor_copy(recb16[:, :], recb[:, :]),
                             reads=[brecb], writes=[brecb16])
                        rd_mm = [bones_f, brecb, brecb16]
                    p.op("tensor", lambda e: e.matmul(
                        prb[:, 0:256], ones_f[:, :], recb[:, :], start=True, stop=True),
                        reads=rd_mm, writes=[bprb])
                    rd_cp = [bprb]
                    if "C" in dbg_delay:
                        p.op("tensor", lambda e: e.matmul(
                            pdly[:, 0:256], ones_f[:, :], dly1[:, :],
                            start=True, stop=True), reads=[bones_f, bdly1], writes=[bdlyA])
                        rd_cp = [bprb, bdlyA]
                    p.op("scalar", lambda e: e.copy(rbb[:, :], prb[:, 0:256]),
                         reads=rd_cp, writes=[brbb])
                    rd_tt = [bpout, brbb]
                    if "D" in dbg_delay:
                        p.op("scalar", lambda e: e.activation(
                            dly2[:, :], dly1[:, :], AF.Copy), reads=[bdly1], writes=[bdly2])
                        rd_tt = [bpout, brbb, bdly2]
                    if dbg_dump is not None:
                        u = 4 * h + pr
                        p.op("sync", lambda e, u=u: e.dma_start(
                            dbg_dump["recb"][u:u + 1, :], recb[:, :]),
                            reads=[brecb], dma=True)
                        p.op("sync", lambda e, u=u: e.dma_start(
                            dbg_dump["rbb"][u], rbb[:, :]),
                            reads=[brbb], dma=True)
                    p.op("vector", lambda e, dst=dst: e.tensor_tensor(
                        dst, pout[:, 0:256], rbb[:, :], ALU.mult),
                        reads=rd_tt, writes=[baoT[h][pr]])
    flat = [b for row in baoT for b in row]
    return {"aoT": aoT, "baoT": flat}


def emit_ph1_wo(p, ar, aps, aoT, baoT):
    wb = [ar.sbuf(f"p1w_{i}", [128, 32 * 256], BF16) for i in range(2)]
    bwb = [ar.b[f"p1w_{i}"] for i in range(2)]
    stg = [ar.sbuf(f"p1ws_{i}", [128, 256], F32) for i in range(2)]
    bstg = [ar.b[f"p1ws_{i}"] for i in range(2)]
    psc = [ar.t["p1_sc0"], ar.t["p1_sc1"]]
    bpsc = [ar.b["p1_sc0"], ar.b["p1_sc1"]]
    for B in range(16):
        w, bw = wb[B % 2], bwb[B % 2]
        p.op("gpsimd", lambda e, B=B, w=w: e.dma_start(w[:, :], aps["w1blk"][B]),
             writes=[bw], dma=True)
        for t in range(NT):
            pb, bpb = psc[t % 2], bpsc[t % 2]
            for c in range(32):
                p.op("tensor", lambda e, c=c, t=t, pb=pb, w=w: e.matmul(
                    pb[:, 0:256], aoT[:, 1024 * c + 128 * t:1024 * c + 128 * (t + 1)],
                    w[:, 256 * c:256 * (c + 1)],
                    start=(c == 0), stop=(c == 31)),
                    reads=baoT + [bw], writes=[bpb])
            sb, bsb = stg[t % 2], bstg[t % 2]
            p.op("scalar", lambda e, sb=sb, pb=pb: e.copy(sb[:, :], pb[:, 0:256]),
                 reads=[bpb], writes=[bsb])
            p.op("sync", lambda e, t=t, B=B, sb=sb: e.dma_start(
                aps["aout"][128 * t:128 * (t + 1), 256 * B:256 * (B + 1)], sb[:, :]),
                reads=[bsb], dma=True)


def emit_lnres(p, ar, pref, yap, resap, xsap, xltap, bias_bcast=None,
               xlt_buf=None, xs_buf=None):
    eye = ar.t["eye"]
    beye = ar.b["eye"]
    yb = ar.sbuf(pref + "_y", [128, D], F32)
    by = ar.b[pref + "_y"]
    ylb = ar.sbuf(pref + "_yl", [128, D], F32)
    byl = ar.b[pref + "_yl"]
    if xltap is not None:
        l2 = ar.sbuf(pref + "_l2", [128, D], BF16)
        bl2 = ar.b[pref + "_l2"]
        sg = ar.sbuf(pref + "_sg", [128, 512], BF16)
        bsg = ar.b[pref + "_sg"]
        if "p1_tr0" not in ar.t:
            for i in range(2):
                ar.psum(f"p1_tr{i}", [128, 1024], BF16)
        ptr = [ar.t["p1_tr0"], ar.t["p1_tr1"]]
        bptr = [ar.b["p1_tr0"], ar.b["p1_tr1"]]
    for t in range(NT):
        p.op("sync", lambda e, t=t: e.dma_start(yb[:, :], yap[128 * t:128 * (t + 1), :]),
             writes=[by], dma=True)
        if bias_bcast is not None:
            bb, bbb = bias_bcast
            p.op("vector", lambda e: e.tensor_tensor(yb[:, :], yb[:, :], bb[:, :], ALU.add),
                 reads=[by, bbb], writes=[by])
        emit_ln(p, ar, f"{pref}ln1_{t}", yb[:, :], by, ylb[:, :], byl)
        p.op("sync", lambda e, t=t: e.dma_start(yb[:, :], resap[128 * t:128 * (t + 1), :]),
             writes=[by], dma=True)
        p.op("vector", lambda e: e.tensor_tensor(ylb[:, :], ylb[:, :], yb[:, :], ALU.add),
             reads=[byl, by], writes=[byl])
        p.op("sync", lambda e, t=t: e.dma_start(xsap[128 * t:128 * (t + 1), :], ylb[:, :]),
             reads=[byl], writes=([xs_buf] if xs_buf is not None else []), dma=True)
        if xltap is None:
            continue
        emit_ln(p, ar, f"{pref}ln2_{t}", ylb[:, :], byl, l2[:, :], bl2)
        for q4 in range(8):
            for hh in range(4):
                c = 4 * q4 + hh
                pb, bpb = ptr[hh % 2], bptr[hh % 2]
                p.op("tensor", lambda e, c=c, pb=pb: e.transpose(
                    pb[:, 0:128], l2[:, 128 * c:128 * (c + 1)], eye[:, :]),
                    reads=[bl2, beye], writes=[bpb])
                p.op("scalar", lambda e, hh=hh, pb=pb: e.copy(
                    sg[:, 128 * hh:128 * (hh + 1)], pb[:, 0:128]),
                    reads=[bpb], writes=[bsg])
            dst = xltap[512 * q4:512 * (q4 + 1), 128 * t:128 * (t + 1)].rearrange(
                "(h p) n -> p h n", p=128)
            p.op("scalar", lambda e, dst=dst: e.dma_start(
                dst, sg[:, :].rearrange("p (h n) -> p h n", n=128)),
                reads=[bsg], writes=([xlt_buf] if xlt_buf is not None else []), dma=True)





def emit_bcast_row(p, ar, name, row_ap, width, row_buf=None, scale=None):
    nc = p.nc
    if "ones_f" not in ar.t:
        onf = ar.sbuf("ones_f", [1, 128], F32)
        p.op("vector", lambda e: e.memset(onf[:, :], 1.0), writes=[ar.b["ones_f"]])
    onf, bonf = ar.t["ones_f"], ar.b["ones_f"]
    lw = max(width, 128)
    row = ar.sbuf(name + "_r", [1, lw], F32)
    brow = ar.b[name + "_r"]
    out = ar.sbuf(name, [128, width], F32)
    bout = ar.b[name]
    if "pbc" not in ar.t:
        ar.psum("pbc", [128, 512], F32)
    pbc, bpbc = ar.t["pbc"], ar.b["pbc"]
    p.op("sync", lambda e: e.dma_start(row[:, :], row_ap), writes=[brow], dma=True)
    if scale is not None:
        p.op("vector", lambda e: e.tensor_scalar_mul(row[:, :], row[:, :], scale),
             reads=[brow], writes=[brow])
    for c in range((width + 511) // 512):
        w = min(512, width - 512 * c)
        p.op("tensor", lambda e, c=c, w=w: e.matmul(
            pbc[:, 0:w], onf[:, :], row[:, 512 * c:512 * c + w], start=True, stop=True),
            reads=[bonf, brow], writes=[bpbc])
        p.op("scalar", lambda e, c=c, w=w: e.copy(out[:, 512 * c:512 * c + w], pbc[:, 0:w]),
             reads=[bpbc], writes=[bout])
    return out, bout


def emit_ph2(p, ar, aps, dbg_stop=99):
    if "eye" not in ar.t:
        eye = ar.sbuf("eye", [128, 128], BF16)
        p.op("sync", lambda e: e.dma_start(eye[:, :], aps["eye"]), writes=[ar.b["eye"]], dma=True)
    x1T = ar.sbuf("p2_x1T", [128, 32 * 1024], BF16)
    bx1T = ar.b["p2_x1T"]
    p.op("sync", lambda e: e.dma_start(
        x1T[:, :].rearrange("p (c n) -> p c n", n=1024),
        aps["xln1T"].rearrange("(c p) n -> p c n", p=128)), writes=[bx1T], dma=True)
    if "ones_bfr" not in ar.t:
        onesr0 = ar.sbuf("ones_bfr", [1, 128], BF16)
        p.op("vector", lambda e: e.memset(onesr0[:, :], 1.0), writes=[ar.b["ones_bfr"]])
    onesr, bonesr = ar.t["ones_bfr"], ar.b["ones_bfr"]
    brow = ar.sbuf("p2_brow", [1, 4096], BF16)
    bbrow = ar.b["p2_brow"]
    p.op("gpsimd", lambda e: e.dma_start(brow[:, :], aps["bias0"]), writes=[bbrow], dma=True)
    lsi_i = ar.sbuf("p2_lsii", [1, 128], I32)
    blsii = ar.b["p2_lsii"]
    p.op("sync", lambda e: e.dma_start(lsi_i[:, :], aps["lsi"]), writes=[blsii], dma=True)
    lsf = ar.sbuf("p2_lsf", [1, 4], F32)
    blsf = ar.b["p2_lsf"]
    p.op("vector", lambda e: e.tensor_copy(lsf[:, :], lsi_i[:, 0:4]), reads=[blsii], writes=[blsf])
    bidx = ar.sbuf("p2_bidx", [1, 128], F32)
    bbidx = ar.b["p2_bidx"]
    p.op("sync", lambda e: e.dma_start(bidx[:, :], aps["bidx"]), writes=[bbidx], dma=True)
    sm = ar.sbuf("p2_sm", [1, 8], F32)
    bsm = ar.b["p2_sm"]
    p.op("vector", lambda e: e.tensor_scalar(sm[:, 0:4], lsf[:, :], bidx[:, 0:1], None, ALU.is_le),
         reads=[blsf, bbidx], writes=[bsm])
    p.op("vector", lambda e: e.reduce_sum(sm[:, 4:5], sm[:, 0:4], axis=AX.X),
         reads=[bsm], writes=[bsm])
    p.op("vector", lambda e: e.tensor_scalar_add(sm[:, 4:5], sm[:, 4:5], -1.0),
         reads=[bsm], writes=[bsm])
    p.op("vector", lambda e: e.tensor_scalar(sm[:, 5:6], sm[:, 4:5], 0.5, None, ALU.is_gt),
         reads=[bsm], writes=[bsm])
    p.op("vector", lambda e: e.tensor_scalar(sm[:, 6:7], sm[:, 4:5], -1.0, 0.0, ALU.add, ALU.max),
         reads=[bsm], writes=[bsm])
    p.op("vector", lambda e: e.tensor_scalar(sm[:, 6:7], sm[:, 6:7], 2.0, None, ALU.min),
         reads=[bsm], writes=[bsm])
    erow = ar.sbuf("p2_erow", [1, 128], F32)
    berow = ar.b["p2_erow"]
    p.op("sync", lambda e: e.dma_start(erow[:, :], aps["erow3"]), writes=[berow], dma=True)
    alpha = ar.sbuf("p2_alpha", [1, 3], F32)
    balpha = ar.b["p2_alpha"]
    p.op("vector", lambda e: e.tensor_scalar(alpha[:, :], erow[:, 0:3], sm[:, 6:7], None, ALU.is_equal),
         reads=[berow, bsm], writes=[balpha])
    p.op("vector", lambda e: e.tensor_scalar(alpha[:, :], alpha[:, :], sm[:, 5:6], None, ALU.mult),
         reads=[balpha, bsm], writes=[balpha])
    p.op("vector", lambda e: e.tensor_copy(sm[:, 0:3], alpha[:, :]),
         reads=[balpha], writes=[bsm])
    p.barrier()
    if "ones_f" not in ar.t:
        onf0 = ar.sbuf("ones_f", [1, 128], F32)
        p.op("vector", lambda e: e.memset(onf0[:, :], 1.0), writes=[ar.b["ones_f"]])
    onf, bonf = ar.t["ones_f"], ar.b["ones_f"]
    if "pbc" not in ar.t:
        ar.psum("pbc", [128, 512], F32)
    pbc, bpbc = ar.t["pbc"], ar.b["pbc"]
    ab = ar.sbuf("p2_ab", [128, 3], F32)
    bab = ar.b["p2_ab"]
    p.op("tensor", lambda e: e.matmul(pbc[:, 0:3], onf[:, :], alpha[:, :], start=True, stop=True),
         reads=[bonf, balpha], writes=[bpbc])
    p.op("scalar", lambda e: e.copy(ab[:, :], pbc[:, 0:3]), reads=[bpbc], writes=[bab])
    if dbg_stop <= 1:
        return
    a0 = ar.sbuf("p2_a0", [128, 3 * 512], BF16)
    ba0 = ar.b["p2_a0"]
    p.op("gpsimd", lambda e: e.dma_start(
        a0[:, :].rearrange("p (e c r) -> p e c r", e=3, r=16),
        aps["a0t"].rearrange("e (c p) r -> p e c r", p=128)), writes=[ba0], dma=True)
    aeffb = ar.sbuf("p2_aeffb", [128, 512], BF16)
    baeffb = ar.b["p2_aeffb"]
    p.op("vector", lambda e: e.tensor_scalar(aeffb[:, :], a0[:, 0:512], ab[:, 0:1], None, ALU.mult),
         reads=[ba0, bab], writes=[baeffb])
    p.op("vector", lambda e: e.scalar_tensor_tensor(aeffb[:, :], a0[:, 512:1024], ab[:, 1:2],
                                                    aeffb[:, :], ALU.mult, ALU.add),
         reads=[ba0, bab, baeffb], writes=[baeffb])
    p.op("vector", lambda e: e.scalar_tensor_tensor(aeffb[:, :], a0[:, 1024:1536], ab[:, 2:3],
                                                    aeffb[:, :], ALU.mult, ALU.add),
         reads=[ba0, bab, baeffb], writes=[baeffb])
    b0 = ar.sbuf("p2_b0", [16, 3 * 4096], BF16)
    bb0 = ar.b["p2_b0"]
    p.op("gpsimd", lambda e: e.dma_start(
        b0[:, :].rearrange("p (e n) -> p e n", e=3),
        aps["b0t"].rearrange("e p n -> p e n")), writes=[bb0], dma=True)
    beffb = ar.sbuf("p2_beffb", [16, 4096], BF16)
    bbeffb = ar.b["p2_beffb"]
    p.op("vector", lambda e: e.tensor_scalar(beffb[:, :], b0[:, 0:4096], ab[0:16, 0:1], None, ALU.mult),
         reads=[bb0, bab], writes=[bbeffb])
    p.op("vector", lambda e: e.scalar_tensor_tensor(beffb[:, :], b0[:, 4096:8192], ab[0:16, 1:2],
                                                    beffb[:, :], ALU.mult, ALU.add),
         reads=[bb0, bab, bbeffb], writes=[bbeffb])
    p.op("vector", lambda e: e.scalar_tensor_tensor(beffb[:, :], b0[:, 8192:12288], ab[0:16, 2:3],
                                                    beffb[:, :], ALU.mult, ALU.add),
         reads=[bb0, bab, bbeffb], writes=[bbeffb])
    if dbg_stop <= 2:
        return
    eye, beye = ar.t["eye"], ar.b["eye"]
    tT = ar.sbuf("p2_tT", [16, NT * 128], BF16)
    btT = ar.b["p2_tT"]
    tst = ar.sbuf("p2_tst", [128, 16], BF16)
    btst = ar.b["p2_tst"]
    plo = ar.psum("p2_plo", [128, 512], F32)
    bplo = ar.b["p2_plo"]
    ptt = ar.psum("p2_ptt", [128, 1024], BF16)
    bptt = ar.b["p2_ptt"]
    for t in range(NT):
        for c in range(32):
            p.op("tensor", lambda e, c=c, t=t: e.matmul(
                plo[:, 0:16], x1T[:, 1024 * c + 128 * t:1024 * c + 128 * (t + 1)],
                aeffb[:, 16 * c:16 * (c + 1)], start=(c == 0), stop=(c == 31)),
                reads=[bx1T, baeffb], writes=[bplo])
        p.op("scalar", lambda e: e.copy(tst[:, :], plo[:, 0:16]), reads=[bplo], writes=[btst])
        p.op("tensor", lambda e: e.transpose(ptt[0:16, 0:128], tst[:, :], eye[:, :]),
             reads=[btst, beye], writes=[bptt])
        p.op("vector", lambda e, t=t: e.tensor_copy(tT[:, 128 * t:128 * (t + 1)], ptt[0:16, 0:128]),
             reads=[bptt], writes=[btT])
    if dbg_stop <= 3:
        return
    wb = [ar.sbuf(f"p2w_{i}", [128, 32 * 128], BF16) for i in range(2)]
    bwb = [ar.b[f"p2w_{i}"] for i in range(2)]
    stg = [ar.sbuf(f"p2s_{i}", [128, 128], F32) for i in range(2)]
    bstg = [ar.b[f"p2s_{i}"] for i in range(2)]
    pmm = [ar.psum(f"p2_mm{i}", [128, 512], F32) for i in range(2)]
    bpmm = [ar.b[f"p2_mm{i}"] for i in range(2)]
    for B in range(32):
        w, bw = wb[B % 2], bwb[B % 2]
        p.op("gpsimd", lambda e, B=B, w=w: e.dma_start(w[:, :], aps["w2blk"][B]),
             writes=[bw], dma=True)
        for t in range(NT):
            pb, bpb = pmm[t % 2], bpmm[t % 2]
            for c in range(32):
                p.op("tensor", lambda e, c=c, t=t, pb=pb, w=w: e.matmul(
                    pb[:, 0:128], x1T[:, 1024 * c + 128 * t:1024 * c + 128 * (t + 1)],
                    w[:, 128 * c:128 * (c + 1)], start=(c == 0), stop=False),
                    reads=[bx1T, bw], writes=[bpb])
            p.op("tensor", lambda e, t=t, B=B, pb=pb: e.matmul(
                pb[:, 0:128], tT[:, 128 * t:128 * (t + 1)],
                beffb[:, 128 * B:128 * (B + 1)], start=False, stop=False),
                reads=[btT, bbeffb], writes=[bpb])
            p.op("tensor", lambda e, B=B, pb=pb: e.matmul(
                pb[:, 0:128], onesr[:, :], brow[:, 128 * B:128 * (B + 1)],
                start=False, stop=True), reads=[bonesr, bbrow], writes=[bpb])
            sb, bsb = stg[t % 2], bstg[t % 2]
            p.op("scalar", lambda e, sb=sb, pb=pb: e.copy(sb[:, :], pb[:, 0:128]),
                 reads=[bpb], writes=[bsb])
            p.op("sync", lambda e, t=t, B=B, sb=sb: e.dma_start(
                aps["yffn"][128 * t:128 * (t + 1), 128 * B:128 * (B + 1)], sb[:, :]),
                reads=[bsb], dma=True)


def emit_l1pre(p, ar, aps, xlt_ap, xlt_buf):
    eye, beye = ar.t["eye"], ar.b["eye"]
    xl = [ar.sbuf(f"l1p_x{i}", [128, 4096], BF16) for i in range(2)]
    bxl = [ar.b[f"l1p_x{i}"] for i in range(2)]
    wqd = ar.sbuf("l1p_wqd", [128, 32 * 12], BF16)
    bwqd = ar.b["l1p_wqd"]
    p.op("gpsimd", lambda e: e.dma_start(
        wqd[:, :].rearrange("p (c r) -> p c r", r=12),
        aps["wq_d"].rearrange("(c p) r -> p c r", p=128)), writes=[bwqd], dma=True)
    wkvd = ar.sbuf("l1p_wkvd", [128, 32 * 4], BF16)
    bwkvd = ar.b["l1p_wkvd"]
    p.op("gpsimd", lambda e: e.dma_start(
        wkvd[:, :].rearrange("p (c r) -> p c r", r=4),
        aps["wkv_d"].rearrange("(c p) r -> p c r", p=128)), writes=[bwkvd], dma=True)
    wqk = ar.sbuf("l1p_wqk", [12, 128], BF16)
    bwqk = ar.b["l1p_wqk"]
    p.op("gpsimd", lambda e: e.dma_start(wqk[:, :], aps["w_qk"]), writes=[bwqk], dma=True)
    bqd = ar.sbuf("l1p_bqd", [1, 128], BF16)
    bbqd = ar.b["l1p_bqd"]
    p.op("gpsimd", lambda e: e.dma_start(bqd[:, :], aps["bq_d"]), writes=[bbqd], dma=True)
    bkvd = ar.sbuf("l1p_bkvd", [1, 128], BF16)
    bbkvd = ar.b["l1p_bkvd"]
    p.op("gpsimd", lambda e: e.dma_start(bkvd[:, :], aps["bkv_d"]), writes=[bbkvd], dma=True)
    bqk = ar.sbuf("l1p_bqk", [1, 128], BF16)
    bbqk = ar.b["l1p_bqk"]
    p.op("gpsimd", lambda e: e.dma_start(bqk[:, :], aps["b_qk"]), writes=[bbqk], dma=True)
    onesr = ar.t.get("ones_bfr")
    if onesr is None:
        onesr = ar.sbuf("ones_bfr", [1, 128], BF16)
        p.op("vector", lambda e: e.memset(onesr[:, :], 1.0), writes=[ar.b["ones_bfr"]])
    bonesr = ar.b["ones_bfr"]
    cq = ar.sbuf("l1p_cq", [128, 16], F32)
    bcq = ar.b["l1p_cq"]
    cqb = ar.sbuf("l1p_cqb", [128, 16], BF16)
    bcqb = ar.b["l1p_cqb"]
    ckv = ar.sbuf("l1p_ckv", [128, 128], F32)
    bckv = ar.b["l1p_ckv"]
    p.op("vector", lambda e: e.memset(ckv[:, :], 0.0), writes=[bckv])
    pq = ar.psum("l1p_pq", [128, 512], F32)
    bpq = ar.b["l1p_pq"]
    pt2 = ar.psum("l1p_pt", [128, 1024], BF16)
    bpt2 = ar.b["l1p_pt"]
    pqt = ar.psum("l1p_pqt", [128, 512], F32)
    bpqt = ar.b["l1p_pqt"]
    cqT = ar.sbuf("l1p_cqT", [12, 128], BF16)
    bcqT = ar.b["l1p_cqT"]
    q1s = ar.sbuf("l1p_q1s", [4, 128], BF16)
    bq1s = ar.b["l1p_q1s"]
    for t in range(NT):
        x, bx = xl[t % 2], bxl[t % 2]
        p.op("sync", lambda e, t=t, x=x: e.dma_start(
            x[:, :].rearrange("p (c n) -> p c n", n=128),
            xlt_ap[:, 128 * t:128 * (t + 1)].rearrange("(c p) n -> p c n", p=128)),
            reads=[xlt_buf], writes=[bx], dma=True)
        for c in range(32):
            p.op("tensor", lambda e, c=c, x=x: e.matmul(
                pq[:, 0:12], x[:, 128 * c:128 * (c + 1)], wqd[:, 12 * c:12 * (c + 1)],
                start=(c == 0), stop=False), reads=[bx, bwqd], writes=[bpq])
            p.op("tensor", lambda e, c=c, x=x: e.matmul(
                pq[:, 16:20], x[:, 128 * c:128 * (c + 1)], wkvd[:, 4 * c:4 * (c + 1)],
                start=False, stop=(c == 31)), reads=[bx, bwkvd], writes=[bpq])
        p.op("tensor", lambda e: e.matmul(pq[:, 0:12], onesr[:, :], bqd[:, 0:12],
                                          start=False, stop=False),
             reads=[bonesr, bbqd], writes=[bpq])
        p.op("tensor", lambda e: e.matmul(pq[:, 16:20], onesr[:, :], bkvd[:, 0:4],
                                          start=False, stop=True),
             reads=[bonesr, bbkvd], writes=[bpq])
        p.op("vector", lambda e: e.tensor_copy(ckv[:, 0:4], pq[:, 16:20]),
             reads=[bpq], writes=[bckv])
        ckvw = [aps["ckvloc_buf"]] if "ckvloc_buf" in aps else []
        p.op("sync", lambda e, t=t: e.dma_start(
            aps["ckvloc"][128 * t:128 * (t + 1), :], ckv[:, :]), reads=[bckv],
            writes=ckvw, dma=True)
        p.op("scalar", lambda e: e.copy(cqb[:, 0:12], pq[:, 0:12]), reads=[bpq], writes=[bcqb])
        p.op("tensor", lambda e: e.transpose(pt2[0:16, 0:128], cqb[:, 0:16], eye[:, :]),
             reads=[bcqb, beye], writes=[bpt2])
        p.op("vector", lambda e: e.tensor_copy(cqT[:, :], pt2[0:12, 0:128]),
             reads=[bpt2], writes=[bcqT])
        for h in range(H):
            p.op("tensor", lambda e, h=h: e.matmul(
                pqt[0:4, 0:128], wqk[:, 4 * h:4 * (h + 1)], cqT[:, :],
                start=True, stop=False), reads=[bwqk, bcqT], writes=[bpqt])
            p.op("tensor", lambda e, h=h: e.matmul(
                pqt[0:4, 0:128], bqk[:, 4 * h:4 * (h + 1)], onesr[:, :],
                start=False, stop=True), reads=[bbqk, bonesr], writes=[bpqt])
            p.op("vector", lambda e: e.tensor_copy(q1s[:, :], pqt[0:4, 0:128]),
                 reads=[bpqt], writes=[bq1s])
            p.op("sync", lambda e, h=h, t=t: e.dma_start(
                aps["q1T"][:, 1024 * h + 128 * t:1024 * h + 128 * (t + 1)], q1s[:, :]),
                reads=[bq1s], dma=True)


CBIAS_L1 = 30.0

def emit_ph3(p, ar, aps):
    if "eye" not in ar.t:
        eye = ar.sbuf("eye", [128, 128], BF16)
        p.op("sync", lambda e: e.dma_start(eye[:, :], aps["eye"]), writes=[ar.b["eye"]], dma=True)
    eye, beye = ar.t["eye"], ar.b["eye"]
    if "ones_f" not in ar.t:
        onf = ar.sbuf("ones_f", [1, 128], F32)
        p.op("vector", lambda e: e.memset(onf[:, :], 1.0), writes=[ar.b["ones_f"]])
    onf, bonf = ar.t["ones_f"], ar.b["ones_f"]
    qhb = [ar.sbuf(f"p3_qh{i}", [4, 1024], BF16) for i in range(2)]
    bqhb = [ar.b[f"p3_qh{i}"] for i in range(2)]
    ckvf = ar.sbuf("p3_ckvf", [128, 32 * 128], F32)
    bckvf = ar.b["p3_ckvf"]
    ckvgr = [aps["ckvg_buf"]] if "ckvg_buf" in aps else []
    p.op("sync", lambda e: e.dma_start(
        ckvf[:, :].rearrange("p (c r) -> p c r", r=128),
        aps["ckvg"].rearrange("(c p) r -> p c r", p=128)), reads=ckvgr, writes=[bckvf], dma=True)
    wv5 = ar.sbuf("p3_wv5", [5, 4096], BF16)
    bwv5 = ar.b["p3_wv5"]
    p.op("gpsimd", lambda e: e.dma_start(wv5[:, :], aps["wv5"]), writes=[bwv5], dma=True)
    ckv5 = ar.sbuf("p3_ckv5", [128, 32 * 5], BF16)
    bckv5 = ar.b["p3_ckv5"]
    p.op("vector", lambda e: e.memset(ckv5[:, :], 1.0), writes=[bckv5])
    for c in range(32):
        p.op("vector", lambda e, c=c: e.tensor_copy(ckv5[:, 5 * c:5 * c + 4],
                                                    ckvf[:, 128 * c:128 * c + 4]),
             reads=[bckvf], writes=[bckv5])
    ckvb = ar.sbuf("p3_ckvb", [128, 4], BF16)
    bckvb = ar.b["p3_ckvb"]
    ckvT = ar.sbuf("p3_ckvT", [4, 4096], BF16)
    bckvT = ar.b["p3_ckvT"]
    ptc = ar.psum("p3_ptc", [128, 1024], BF16)
    bptc = ar.b["p3_ptc"]
    for c in range(32):
        p.op("vector", lambda e, c=c: e.tensor_copy(ckvb[:, :], ckvf[:, 128 * c:128 * c + 4]),
             reads=[bckvf], writes=[bckvb])
        p.op("tensor", lambda e, c=c: e.transpose(ptc[0:4, 0:128], ckvb[:, :], eye[:, :]),
             reads=[bckvb, beye], writes=[bptc])
        p.op("scalar", lambda e, c=c: e.copy(ckvT[:, 128 * c:128 * (c + 1)], ptc[0:4, 0:128]),
             reads=[bptc], writes=[bckvT])
    cneg_h, cneg_b = ar.const_col(p, -CBIAS_L1)
    aoT2 = ar.sbuf("aoT2", [128, 32 * 1024], BF16)
    baoT2 = [[ar.buf(f"aoT2_{h}_{gq}") for gq in range(2)] for h in range(H)]
    PT = [ar.sbuf(f"p3_PT{i}", [128, 512], BF16) for i in range(2)]
    bPT = [ar.b[f"p3_PT{i}"] for i in range(2)]
    ctx5 = ar.sbuf("p3_ctx5", [5, 512], BF16)
    bctx5 = ar.b["p3_ctx5"]
    sel5 = ar.sbuf("p3_sel5", [5, 128], BF16)
    bsel5 = ar.b["p3_sel5"]
    p.op("gpsimd", lambda e: e.dma_start(sel5[:, :], aps["sel5"]), writes=[bsel5], dma=True)
    recb = ar.sbuf("p3_recb", [1, 512], F32)
    brecb = ar.b["p3_recb"]
    rbb = ar.sbuf("p3_rbb", [128, 512], F32)
    brbb = ar.b["p3_rbb"]
    psc = [ar.psum(f"p3_sc{i}", [128, 512], F32) for i in range(2)]
    bpsc = [ar.b[f"p3_sc{i}"] for i in range(2)]
    pctx = ar.psum("p3_ctx", [128, 512], F32)
    bpctx = ar.b["p3_ctx"]
    pov = ar.psum("p3_ov", [128, 512], F32)
    bpov = ar.b["p3_ov"]
    prb = ar.psum("p3_rb", [128, 512], F32)
    bprb = ar.b["p3_rb"]
    pdn = ar.psum("p3_dn", [128, 512], F32)
    bpdn = ar.b["p3_dn"]
    for h in range(H):
        qhc, bqhc = qhb[h % 2], bqhb[h % 2]
        p.op("sync", lambda e, h=h, qhc=qhc: e.dma_start(
            qhc[:, :], aps["q1T"][:, 1024 * h:1024 * (h + 1)]),
            writes=[bqhc], dma=True)
        for gq in range(2):
            qsl = qhc[:, 512 * gq:512 * (gq + 1)]
            first = {"c": True}
            for c in range(32):
                pb, bpb = psc[c % 2], bpsc[c % 2]
                pt, bpt = PT[c % 2], bPT[c % 2]
                p.op("tensor", lambda e, c=c, qsl=qsl, pb=pb: e.matmul(
                    pb[:, :], ckvT[:, 128 * c:128 * (c + 1)], qsl, start=True, stop=True),
                    reads=[bckvT, bqhc], writes=[bpb])
                p.op("scalar", lambda e, pb=pb, pt=pt: e.activation(
                    pt[:, :], pb[:, :], AF.Exp, bias=cneg_h[:, :]),
                    reads=[bpb, cneg_b], writes=[bpt])
                st_ = first["c"]
                p.op("tensor", lambda e, c=c, pt=pt, st_=st_: e.matmul(
                    pctx[0:5, :], ckv5[:, 5 * c:5 * (c + 1)], pt[:, :],
                    start=st_, stop=(c == 31)), reads=[bckv5, bpt], writes=[bpctx])
                first["c"] = False
            p.op("scalar", lambda e: e.copy(ctx5[:, :], pctx[0:5, :]),
                 reads=[bpctx], writes=[bctx5])
            p.op("tensor", lambda e: e.matmul(pdn[0:1, :], sel5[:, 0:1], ctx5[:, :],
                                              start=True, stop=True),
                 reads=[bsel5, bctx5], writes=[bpdn])
            p.op("vector", lambda e: e.reciprocal(recb[:, :], pdn[0:1, :]),
                 reads=[bpdn], writes=[brecb])
            p.op("tensor", lambda e: e.matmul(prb[:, :], onf[:, :], recb[:, :],
                                              start=True, stop=True),
                 reads=[bonf, brecb], writes=[bprb])
            p.op("scalar", lambda e: e.copy(rbb[:, :], prb[:, :]),
                 reads=[bprb], writes=[brbb])
            p.op("tensor", lambda e, h=h: e.matmul(
                pov[:, :], wv5[:, 128 * h:128 * (h + 1)], ctx5[:, :],
                start=True, stop=True), reads=[bwv5, bctx5], writes=[bpov])
            dst = aoT2[:, 1024 * h + 512 * gq:1024 * h + 512 * (gq + 1)]
            p.op("vector", lambda e, dst=dst: e.tensor_tensor(
                dst, pov[:, :], rbb[:, :], ALU.mult),
                reads=[bpov, brbb], writes=[baoT2[h][gq]])
    flat = [b for row in baoT2 for b in row]
    return {"aoT2": aoT2, "baoT2": flat}


def emit_wo_generic(p, ar, pref, aps_wblk, aoT, baoT, yout_ap, nblk=16,
                    bias_row=None):
    if bias_row is not None:
        if "ones_bfr" not in ar.t:
            onesr0 = ar.sbuf("ones_bfr", [1, 128], BF16)
            p.op("vector", lambda e: e.memset(onesr0[:, :], 1.0), writes=[ar.b["ones_bfr"]])
        onesr, bonesr = ar.t["ones_bfr"], ar.b["ones_bfr"]
        brw = ar.sbuf(pref + "_brw", [1, 4096], BF16)
        bbrw = ar.b[pref + "_brw"]
        p.op("gpsimd", lambda e: e.dma_start(brw[:, :], bias_row), writes=[bbrw], dma=True)
    wb = [ar.sbuf(f"{pref}w_{i}", [128, 32 * 256], BF16) for i in range(2)]
    bwb = [ar.b[f"{pref}w_{i}"] for i in range(2)]
    stg = [ar.sbuf(f"{pref}s_{i}", [128, 256], F32) for i in range(2)]
    bstg = [ar.b[f"{pref}s_{i}"] for i in range(2)]
    pmm = [ar.psum(f"{pref}_mm{i}", [128, 512], F32) for i in range(2)]
    bpmm = [ar.b[f"{pref}_mm{i}"] for i in range(2)]
    for B in range(nblk):
        w, bw = wb[B % 2], bwb[B % 2]
        p.op("gpsimd", lambda e, B=B, w=w: e.dma_start(w[:, :], aps_wblk[B]),
             writes=[bw], dma=True)
        for t in range(NT):
            pb, bpb = pmm[t % 2], bpmm[t % 2]
            for c in range(32):
                p.op("tensor", lambda e, c=c, t=t, pb=pb, w=w: e.matmul(
                    pb[:, 0:256], aoT[:, 1024 * c + 128 * t:1024 * c + 128 * (t + 1)],
                    w[:, 256 * c:256 * (c + 1)], start=(c == 0),
                    stop=(c == 31 and bias_row is None)),
                    reads=baoT + [bw], writes=[bpb])
            if bias_row is not None:
                p.op("tensor", lambda e, B=B, pb=pb: e.matmul(
                    pb[:, 0:256], onesr[:, :], brw[:, 256 * B:256 * (B + 1)],
                    start=False, stop=True), reads=[bonesr, bbrw], writes=[bpb])
            sb, bsb = stg[t % 2], bstg[t % 2]
            p.op("scalar", lambda e, sb=sb, pb=pb: e.copy(sb[:, :], pb[:, 0:256]),
                 reads=[bpb], writes=[bsb])
            p.op("sync", lambda e, t=t, B=B, sb=sb: e.dma_start(
                yout_ap[128 * t:128 * (t + 1), 256 * B:256 * (B + 1)], sb[:, :]),
                reads=[bsb], dma=True)





def emit_ph4(p, ar, aps, with_counts_out, with_main, fused_cc=False):
    if "eye" not in ar.t:
        eye = ar.sbuf("eye", [128, 128], BF16)
        p.op("sync", lambda e: e.dma_start(eye[:, :], aps["eye"]), writes=[ar.b["eye"]], dma=True)
    eye, beye = ar.t["eye"], ar.b["eye"]
    if "ones_f" not in ar.t:
        onf = ar.sbuf("ones_f", [1, 128], F32)
        p.op("vector", lambda e: e.memset(onf[:, :], 1.0), writes=[ar.b["ones_f"]])
    onf, bonf = ar.t["ones_f"], ar.b["ones_f"]
    if "ones_bfr" not in ar.t:
        onesr = ar.sbuf("ones_bfr", [1, 128], BF16)
        p.op("vector", lambda e: e.memset(onesr[:, :], 1.0), writes=[ar.b["ones_bfr"]])
    onesr, bonesr = ar.t["ones_bfr"], ar.b["ones_bfr"]
    if "ones_bf" not in ar.t:
        oc = ar.sbuf("ones_bf", [128, 1], BF16)
        p.op("vector", lambda e: e.memset(oc[:, :], 1.0), writes=[ar.b["ones_bf"]])
    onesc, bonesc = ar.t["ones_bf"], ar.b["ones_bf"]
    xlt_buf = aps.get("xlt3_buf") or ar.buf("p4_xlt3")

    gw1 = ar.sbuf("p4_gw1", [128, 32 * 32], BF16)
    bgw1 = ar.b["p4_gw1"]
    p.op("gpsimd", lambda e: e.dma_start(
        gw1[:, :].rearrange("p (c r) -> p c r", r=32),
        aps["gw1t"].rearrange("(c p) r -> p c r", p=128)), writes=[bgw1], dma=True)
    gw2 = ar.sbuf("p4_gw2", [32, 128], BF16)
    bgw2 = ar.b["p4_gw2"]
    p.op("gpsimd", lambda e: e.dma_start(gw2[:, :], aps["gw2t"]), writes=[bgw2], dma=True)
    gb1 = ar.sbuf("p4_gb1", [1, 128], BF16)
    bgb1 = ar.b["p4_gb1"]
    p.op("gpsimd", lambda e: e.dma_start(gb1[:, :], aps["gb1"]), writes=[bgb1], dma=True)
    gb2 = ar.sbuf("p4_gb2", [1, 128], BF16)
    bgb2 = ar.b["p4_gb2"]
    p.op("gpsimd", lambda e: e.dma_start(gb2[:, :], aps["gb2"]), writes=[bgb2], dma=True)
    lt8 = ar.sbuf("p4_lt8", [8, 128], BF16)
    blt8 = ar.b["p4_lt8"]
    p.op("gpsimd", lambda e: e.dma_start(lt8[:, :], aps["lt8"]), writes=[blt8], dma=True)
    a1c = ar.sbuf("p4_a1c", [128, 32 * 128], BF16)
    ba1c = ar.b["p4_a1c"]
    p.op("gpsimd", lambda e: e.dma_start(
        a1c[:, :].rearrange("p (c r) -> p c r", r=128),
        aps["a1cat"].rearrange("(c p) r -> p c r", p=128)), writes=[ba1c], dma=True)
    gmb, bgmb = emit_bcast_row(p, ar, "p4_gmb", aps["lng"], 32)
    btb, bbtb = emit_bcast_row(p, ar, "p4_btb", aps["lnb"], 32)

    xall = ar.sbuf("p4_xall", [128, 32 * 1024], BF16)
    bxall = ar.b["p4_xall"]
    p.op("sync", lambda e: e.dma_start(
        xall[:, :].rearrange("p (c n) -> p c n", n=1024),
        aps["xln3T"].rearrange("(c p) n -> p c n", p=128)),
        reads=[xlt_buf], writes=[bxall], dma=True)
    hg = ar.sbuf("p4_hg", [128, 32], F32)
    bhg = ar.b["p4_hg"]
    hln = ar.sbuf("p4_hln", [128, 32], F32)
    bhln = ar.b["p4_hln"]
    h2b = ar.sbuf("p4_h2b", [128, 32], BF16)
    bh2b = ar.b["p4_h2b"]
    h2T = ar.sbuf("p4_h2T", [32, 128], BF16)
    bh2T = ar.b["p4_h2T"]
    gts = ar.sbuf("p4_gts", [128, 8], F32)
    bgts = ar.b["p4_gts"]
    mcol = ar.sbuf("p4_mcol", [128, 4], F32)
    bmcol = ar.b["p4_mcol"]
    dsc = ar.sbuf("p4_dsc", [128, 8], F32)
    bdsc = ar.b["p4_dsc"]
    eqf = ar.sbuf("p4_eqf", [128, 8], F32)
    beqf = ar.b["p4_eqf"]
    eqb = ar.sbuf("p4_eqb", [128, 8], BF16)
    beqb = ar.b["p4_eqb"]
    eqT = ar.sbuf("p4_eqT", [8, 128], BF16)
    beqT = ar.b["p4_eqT"]
    pfz = ar.sbuf("p4_pfz", [128, 8], F32)
    bpfz = ar.b["p4_pfz"]
    ohf = ar.sbuf("p4_ohf", [128, 8], F32)
    bohf = ar.b["p4_ohf"]
    ohs = ar.sbuf("p4_ohs", [128, 2 * 64], BF16)
    bohs = ar.b["p4_ohs"]
    gg = ar.sbuf("p4_gg", [128, 16], F32)
    bgg = ar.b["p4_gg"]
    tcat = ar.sbuf("p4_tcat", [128, NT * 128], BF16)
    btcat = ar.b["p4_tcat"]
    cnts = ar.sbuf("p4_cnts", [8, 128], F32)
    bcnts = ar.b["p4_cnts"]
    p.op("vector", lambda e: e.memset(cnts[:, :], 0.0), writes=[bcnts])
    pg = ar.psum("p4_pg", [128, 512], F32)
    bpg = ar.b["p4_pg"]
    ptr2 = ar.psum("p4_ptr", [128, 1024], BF16)
    bptr2 = ar.b["p4_ptr"]
    pcnt = ar.psum("p4_pcnt", [128, 512], F32)
    bpcnt = ar.b["p4_pcnt"]
    ptc = ar.psum("p4_ptc", [128, 512], F32)
    bptc = ar.b["p4_ptc"]

    for t in range(NT):
        for c in range(32):
            p.op("tensor", lambda e, c=c, t=t: e.matmul(
                pg[:, 0:32], xall[:, 1024 * c + 128 * t:1024 * c + 128 * (t + 1)],
                gw1[:, 32 * c:32 * (c + 1)],
                start=(c == 0), stop=False), reads=[bxall, bgw1], writes=[bpg])
        p.op("tensor", lambda e: e.matmul(pg[:, 0:32], onesr[:, :], gb1[:, 0:32],
                                          start=False, stop=True),
             reads=[bonesr, bgb1], writes=[bpg])
        p.op("vector", lambda e: e.tensor_copy(hg[:, :], pg[:, 0:32]), reads=[bpg], writes=[bhg])
        emit_ln(p, ar, f"p4ln_{t}", hg[:, :], bhg, hln[:, :], bhln, width=32)
        p.op("vector", lambda e: e.tensor_tensor(hln[:, :], hln[:, :], gmb[:, :], ALU.mult),
             reads=[bhln, bgmb], writes=[bhln])
        p.op("vector", lambda e: e.tensor_tensor(hln[:, :], hln[:, :], btb[:, :], ALU.add),
             reads=[bhln, bbtb], writes=[bhln])
        p.op("vector", lambda e: e.tensor_relu(h2b[:, :], hln[:, :]),
             reads=[bhln], writes=[bh2b])
        p.op("tensor", lambda e: e.transpose(ptr2[0:32, 0:128], h2b[:, :], eye[:, :]),
             reads=[bh2b, beye], writes=[bptr2])
        p.op("vector", lambda e: e.tensor_copy(h2T[:, :], ptr2[0:32, 0:128]),
             reads=[bptr2], writes=[bh2T])
        p.op("tensor", lambda e: e.matmul(pg[:, 64:72], h2T[:, :], gw2[:, 0:8],
                                          start=True, stop=False),
             reads=[bh2T, bgw2], writes=[bpg])
        p.op("tensor", lambda e: e.matmul(pg[:, 64:72], onesr[:, :], gb2[:, 0:8],
                                          start=False, stop=True),
             reads=[bonesr, bgb2], writes=[bpg])
        p.op("vector", lambda e: e.tensor_copy(gts[:, :], pg[:, 64:72]), reads=[bpg], writes=[bgts])
        for k in range(2):
            gsrc, bgsrc = (gts, bgts) if k == 0 else (dsc, bdsc)
            p.op("vector", lambda e, gsrc=gsrc, k=k: e.reduce_max(
                mcol[:, k:k + 1], gsrc[:, :], axis=AX.X), reads=[bgsrc], writes=[bmcol])
            p.op("vector", lambda e, gsrc=gsrc, k=k: e.tensor_scalar(
                eqf[:, :], gsrc[:, :], mcol[:, k:k + 1], None, ALU.is_ge),
                reads=[bgsrc, bmcol], writes=[beqf])
            p.op("vector", lambda e: e.tensor_copy(eqb[:, :], eqf[:, :]),
                 reads=[beqf], writes=[beqb])
            p.op("tensor", lambda e: e.transpose(ptr2[0:8, 0:128], eqb[:, :], eye[:, :]),
                 reads=[beqb, beye], writes=[bptr2])
            p.op("vector", lambda e: e.tensor_copy(eqT[:, :], ptr2[0:8, 0:128]),
                 reads=[bptr2], writes=[beqT])
            p.op("tensor", lambda e: e.matmul(pg[:, 96:104], eqT[:, :], lt8[:, 0:8],
                                              start=True, stop=True),
                 reads=[beqT, blt8], writes=[bpg])
            p.op("vector", lambda e: e.tensor_scalar(pfz[:, :], pg[:, 96:104], 0.5, None, ALU.is_le),
                 reads=[bpg], writes=[bpfz])
            p.op("vector", lambda e: e.tensor_tensor(ohf[:, :], eqf[:, :], pfz[:, :], ALU.mult),
                 reads=[beqf, bpfz], writes=[bohf])
            p.op("vector", lambda e, k=k, t=t: e.tensor_copy(
                ohs[:, 64 * k + 8 * t:64 * k + 8 * (t + 1)], ohf[:, :]),
                reads=[bohf], writes=[bohs])
            p.op("tensor", lambda e, k=k, t=t: e.matmul(
                pcnt[0:8, k:k + 1], ohs[:, 64 * k + 8 * t:64 * k + 8 * (t + 1)],
                onesc[:, :], start=(k == 0 and t == 0), stop=(k == 1 and t == NT - 1)),
                reads=[bohs, bonesc], writes=[bpcnt])
            if k == 0:
                p.op("vector", lambda e: e.scalar_tensor_tensor(
                    dsc[:, :], ohf[:, :], -1e30, gts[:, :], ALU.mult, ALU.add),
                    reads=[bohf, bgts], writes=[bdsc])
        p.op("vector", lambda e: e.tensor_tensor(mcol[:, 2:3], mcol[:, 1:2], mcol[:, 0:1],
                                                 ALU.subtract), reads=[bmcol], writes=[bmcol])
        p.op("scalar", lambda e: e.activation(mcol[:, 2:3], mcol[:, 2:3], AF.Exp),
             reads=[bmcol], writes=[bmcol])
        p.op("vector", lambda e: e.tensor_scalar_add(mcol[:, 2:3], mcol[:, 2:3], 1.0),
             reads=[bmcol], writes=[bmcol])
        p.op("vector", lambda e: e.reciprocal(mcol[:, 3:4], mcol[:, 2:3]),
             reads=[bmcol], writes=[bmcol])
        p.op("vector", lambda e, t=t: e.tensor_copy(gg[:, t:t + 1], mcol[:, 3:4]),
             reads=[bmcol], writes=[bgg])
        p.op("vector", lambda e, t=t: e.tensor_scalar(gg[:, 8 + t:9 + t], mcol[:, 3:4],
                                                      -1.0, 1.0, ALU.mult, ALU.add),
             reads=[bmcol], writes=[bgg])
        for c in range(32):
            p.op("tensor", lambda e, c=c, t=t: e.matmul(
                ptc[:, 0:128], xall[:, 1024 * c + 128 * t:1024 * c + 128 * (t + 1)],
                a1c[:, 128 * c:128 * (c + 1)],
                start=(c == 0), stop=(c == 31)), reads=[bxall, ba1c], writes=[bptc])
        p.op("vector", lambda e, t=t: e.tensor_copy(tcat[:, 128 * t:128 * (t + 1)], ptc[:, 0:128]),
             reads=[bptc], writes=[btcat])
    p.op("vector", lambda e: e.tensor_copy(cnts[:, 0:2], pcnt[0:8, 0:2]),
         reads=[bpcnt], writes=[bcnts])
    if with_counts_out:
        cloc_w = [aps["cntl_buf"]] if "cntl_buf" in aps else []
        p.op("sync", lambda e: e.dma_start(aps["cnt_loc"], cnts[:, :]),
             reads=[bcnts], writes=cloc_w, dma=True)
    if fused_cc:
        p.op("gpsimd", lambda e: e.collective_compute(
            "AllReduce", ALU.add, [list(range(NCORES))],
            [aps["cnt_loc"].opt()], [aps["cnts_all"].opt()]),
            reads=[aps["cntl_buf"]], writes=[aps["cnts_buf"]])
    if not with_main:
        return

    call = ar.sbuf("p4_call", [8, 128], F32)
    bcall = ar.b["p4_call"]
    car = [aps["cnts_buf"]] if "cnts_buf" in aps else []
    p.op("sync", lambda e: e.dma_start(call[:, :], aps["cnts_all"]), reads=car,
         writes=[bcall], dma=True)
    cg = ar.sbuf("p4_cg", [8, 2], F32)
    bcg = ar.b["p4_cg"]
    if fused_cc:
        p.op("vector", lambda e: e.tensor_copy(cg[:, :], call[:, 0:2]),
             reads=[bcall], writes=[bcg])
    else:
        p.op("vector", lambda e: e.reduce_sum(
            cg[:, :], call[:, 0:16].rearrange("p (c k) -> p k c", k=2), axis=AX.X),
            reads=[bcall], writes=[bcg])
    alw = ar.sbuf("p4_alw", [8, 2], BF16)
    balw = ar.b["p4_alw"]
    p.op("vector", lambda e: e.tensor_scalar(alw[:, :], cg[:, :], 3072.5, None, ALU.is_le),
         reads=[bcg], writes=[balw])
    alwT = ar.sbuf("p4_alwT", [1, 8], F32)
    balwT = ar.b["p4_alwT"]
    aw = ar.sbuf("p4_aw", [128, 16], F32)
    baw = ar.b["p4_aw"]
    pbc, bpbc = ar.t["pbc"], ar.b["pbc"]
    for k in range(2):
        p.op("tensor", lambda e, k=k: e.transpose(ptr2[0:1, 0:8], alw[:, k:k + 1],
                                                  eye[0:8, 0:8]),
             reads=[balw, beye], writes=[bptr2])
        p.op("scalar", lambda e: e.copy(alwT[:, :], ptr2[0:1, 0:8]), reads=[bptr2], writes=[balwT])
        p.op("tensor", lambda e, k=k: e.matmul(pbc[:, 0:8], onf[:, :], alwT[:, :],
                                               start=True, stop=True),
             reads=[bonf, balwT], writes=[bpbc])
        p.op("vector", lambda e, k=k: e.tensor_scalar(aw[:, 8 * k:8 * (k + 1)], pbc[:, 0:8],
                                                      1.0 / 16.0, None, ALU.mult),
             reads=[bpbc], writes=[baw])

    wfin = ar.sbuf("p4_wfin", [128, 8], F32)
    bwfin = ar.b["p4_wfin"]
    wtmp = ar.sbuf("p4_wtmp", [128, 8], F32)
    bwtmp = ar.b["p4_wtmp"]
    tT = ar.sbuf("p4_tT", [128, NT * 128], BF16)
    btT = ar.b["p4_tT"]
    tsc = ar.sbuf("p4_tsc", [128, 128], BF16)
    btsc = ar.b["p4_tsc"]
    b1s = ar.sbuf("p4_b1s", [128, 4096], BF16)
    bb1s = ar.b["p4_b1s"]
    p.op("gpsimd", lambda e: e.dma_start(b1s[:, :], aps["b1stack"]), writes=[bb1s], dma=True)
    for t in range(NT):
        p.op("vector", lambda e, t=t: e.tensor_scalar(
            wfin[:, :], ohs[:, 8 * t:8 * (t + 1)], gg[:, t:t + 1], None, ALU.mult),
            reads=[bohs, bgg], writes=[bwfin])
        p.op("vector", lambda e: e.tensor_tensor(wfin[:, :], wfin[:, :], aw[:, 0:8], ALU.mult),
             reads=[bwfin, baw], writes=[bwfin])
        p.op("vector", lambda e, t=t: e.tensor_scalar(
            wtmp[:, :], ohs[:, 64 + 8 * t:64 + 8 * (t + 1)], gg[:, 8 + t:9 + t], None, ALU.mult),
            reads=[bohs, bgg], writes=[bwtmp])
        p.op("vector", lambda e: e.tensor_tensor(wtmp[:, :], wtmp[:, :], aw[:, 8:16], ALU.mult),
             reads=[bwtmp, baw], writes=[bwtmp])
        p.op("vector", lambda e: e.tensor_tensor(wfin[:, :], wfin[:, :], wtmp[:, :], ALU.add),
             reads=[bwfin, bwtmp], writes=[bwfin])
        for e_ in range(8):
            p.op("vector", lambda e, t=t, e_=e_: e.tensor_scalar(
                tsc[:, 16 * e_:16 * (e_ + 1)], tcat[:, 128 * t + 16 * e_:128 * t + 16 * (e_ + 1)],
                wfin[:, e_:e_ + 1], None, ALU.mult),
                reads=[btcat, bwfin], writes=[btsc])
        p.op("tensor", lambda e: e.transpose(ptr2[:, 0:128], tsc[:, :], eye[:, :]),
             reads=[btsc, beye], writes=[bptr2])
        p.op("vector", lambda e, t=t: e.tensor_copy(tT[:, 128 * t:128 * (t + 1)], ptr2[:, 0:128]),
             reads=[bptr2], writes=[btT])
    wb = [ar.sbuf(f"p4w_{i}", [128, 32 * 256], BF16) for i in range(2)]
    bwb = [ar.b[f"p4w_{i}"] for i in range(2)]
    stg = [ar.sbuf(f"p4s_{i}", [128, 256], F32) for i in range(2)]
    bstg = [ar.b[f"p4s_{i}"] for i in range(2)]
    pmm = [ar.psum(f"p4_mm{i}", [128, 512], F32) for i in range(2)]
    bpmm = [ar.b[f"p4_mm{i}"] for i in range(2)]
    for B in range(16):
        w, bw = wb[B % 2], bwb[B % 2]
        p.op("gpsimd", lambda e, B=B, w=w: e.dma_start(w[:, :], aps["w4blk"][B]),
             writes=[bw], dma=True)
        for t in range(NT):
            pb, bpb = pmm[t % 2], bpmm[t % 2]
            for c in range(32):
                p.op("tensor", lambda e, c=c, t=t, pb=pb, w=w: e.matmul(
                    pb[:, 0:256], xall[:, 1024 * c + 128 * t:1024 * c + 128 * (t + 1)],
                    w[:, 256 * c:256 * (c + 1)], start=(c == 0), stop=False),
                    reads=[bxall, bw], writes=[bpb])
            p.op("tensor", lambda e, t=t, B=B, pb=pb: e.matmul(
                pb[:, 0:256], tT[:, 128 * t:128 * (t + 1)], b1s[:, 256 * B:256 * (B + 1)],
                start=False, stop=True), reads=[btT, bb1s], writes=[bpb])
            sb, bsb = stg[t % 2], bstg[t % 2]
            p.op("scalar", lambda e, sb=sb, pb=pb: e.copy(sb[:, :], pb[:, 0:256]),
                 reads=[bpb], writes=[bsb])
            p.op("sync", lambda e, t=t, B=B, sb=sb: e.dma_start(
                aps["yffn2"][128 * t:128 * (t + 1), 256 * B:256 * (B + 1)], sb[:, :]),
                reads=[bsb], dma=True)
    emit_lnres(p, ar, "p4f", aps["yffn2"], aps["xs3"], aps["out"], None)



def host_slot_rows(core):
    b, j = divmod(core, 4)
    rows = []
    for i in range(NSLOT):
        T = tile_of(j, i)
        base = b * SEQ + T * 128
        rows.extend(range(base, base + 128))
    return np.array(rows)


def host_rope_tables():
    inv_freq = 1.0 / ROPE_BASE ** (np.arange(0, HD, 2, dtype=np.float64) / HD)
    pos = np.arange(SEQ, dtype=np.float64)
    freqs = pos[:, None] * inv_freq[None, :]
    emb = np.concatenate([freqs, freqs], axis=-1)
    return np.cos(emb).astype(np.float32), np.sin(emb).astype(np.float32)


def host_wblk(wT, bw=512):
    ink, out = wT.shape
    assert ink == 4096 and out % bw == 0
    nb = out // bw
    v = wT.reshape(32, 128, nb, bw)
    return np.ascontiguousarray(v.transpose(2, 1, 0, 3).reshape(nb, 128, 32 * bw))


def host_w0blk(wq, wk, wv):
    return host_wblk(np.concatenate([wq.T, wk.T, wv.T], axis=1))


def host_key_scale_row():
    ks = np.ones(SEQ, np.float32)
    ks[:SYS_LEN] = SUP
    ks[SYS_LEN:SYS_LEN + IMG_LEN] = ENH
    return ks


def host_masks(core):
    j = core % 4
    out = np.zeros((32, 128, 256), np.float32)
    for cc in range(32):
        pr = cc // 8
        kpos = cc * 128 + np.arange(128)
        for hs in range(2):
            T = tile_of(j, 2 * pr + hs)
            qpos = T * 128 + np.arange(128)
            out[cc][:, 128 * hs:128 * (hs + 1)] = (kpos[:, None] <= qpos[None, :])
    return out.astype(BF)


def host_scale1(core):
    j = core % 4
    ks = host_key_scale_row()
    out = np.ones((5, 128, 256), np.float32)
    for cc in range(5):
        kpos = cc * 128 + np.arange(128)
        for hs in range(2):
            T = tile_of(j, hs)
            qpos = T * 128 + np.arange(128)
            row_sel = (qpos >= SYS_LEN + IMG_LEN)
            out[cc][:, 128 * hs:128 * (hs + 1)] = np.where(
                row_sel[None, :], ks[kpos][:, None], 1.0)
    return out


def host_ksb():
    return np.ascontiguousarray(
        np.broadcast_to(host_key_scale_row()[:640], (128, 640))).astype(BF)


def host_prep(inputs):
    pr = {}
    x0 = np.ascontiguousarray(np.asarray(inputs["L0_attn_hidden_states"]).reshape(8192, D))
    pr["x0"] = x0
    cos_t, sin_t = host_rope_tables()
    sq = 1.0 / np.sqrt(128.0)
    pr["cos_t"], pr["sin_t"], pr["sq"] = cos_t, sin_t, sq
    pr["w0blk"] = host_w0blk(np.asarray(inputs["L0_attn_wq"]),
                             np.asarray(inputs["L0_attn_wk"]),
                             np.asarray(inputs["L0_attn_wv"]))
    pr["w1blk"] = host_wblk(np.ascontiguousarray(np.asarray(inputs["L0_attn_wo"]).T), 256)
    pr["w2blk"] = host_wblk(np.ascontiguousarray(np.asarray(inputs["L0_ffn_weight"]).T), 128)
    pr["w3blk"] = host_wblk(np.ascontiguousarray(np.asarray(inputs["L1_attn_Wo"])), 256)
    pr["w4blk"] = host_wblk(np.ascontiguousarray(np.asarray(inputs["L1_ffn_weight"]).T), 256)
    pr["eye"] = np.eye(128, dtype=np.float32).astype(BF)
    pr["ksb"] = host_ksb()
    pr["bias0"] = np.asarray(inputs["L0_ffn_bias"]).reshape(1, D).astype(np.float32)
    pr["a0t"] = np.ascontiguousarray(np.asarray(inputs["L0_ffn_lora_A"]).transpose(0, 2, 1))
    pr["b0t"] = np.ascontiguousarray(np.asarray(inputs["L0_ffn_lora_B"]).transpose(0, 2, 1))
    lsi = np.zeros((1, 128), np.int32)
    lsi[0, :4] = np.asarray(inputs["L0_ffn_lora_start_indices"]).reshape(4)
    pr["lsi"] = lsi
    er = np.zeros((1, 128), np.float32)
    er[0, :3] = [0.0, 1.0, 2.0]
    pr["erow3"] = er
    pr["wq_d"] = np.ascontiguousarray(np.asarray(inputs["L1_attn_Wq_d"]).astype(np.float32))
    pr["wkv_d"] = np.ascontiguousarray(np.asarray(inputs["L1_attn_Wkv_d"]).astype(np.float32))
    pr["w_qk"] = np.ascontiguousarray(np.asarray(inputs["L1_attn_W_qk"]).astype(np.float32) * 0.5)
    t = np.zeros((1, 128), np.float32); t[0, :12] = np.asarray(inputs["L1_attn_bq_d"])
    pr["bq_d"] = t
    t = np.zeros((1, 128), np.float32); t[0, :4] = np.asarray(inputs["L1_attn_bkv_d"])
    pr["bkv_d"] = t
    pr["b_qk"] = (np.asarray(inputs["L1_attn_b_qk"]).reshape(1, 128).astype(np.float32) * 0.5)
    pr["wv5"] = np.ascontiguousarray(np.concatenate(
        [np.asarray(inputs["L1_attn_Wv_u"]).astype(np.float32),
         np.asarray(inputs["L1_attn_bv_u"]).reshape(1, D).astype(np.float32)], axis=0))
    t = np.zeros((5, 128), np.float32); t[4, 0] = 1.0
    pr["sel5"] = t
    pr["bo"] = np.asarray(inputs["L1_attn_bo"]).reshape(1, D).astype(np.float32)
    pr["gw1t"] = np.ascontiguousarray(np.asarray(inputs["L1_ffn_gw1"]).T.astype(np.float32))
    t = np.zeros((1, 128), np.float32); t[0, :32] = np.asarray(inputs["L1_ffn_gb1"])
    pr["gb1"] = t
    t = np.zeros((1, 128), np.float32); t[0, :32] = np.asarray(inputs["L1_ffn_ln_gamma"])
    pr["lng"] = t
    t = np.zeros((1, 128), np.float32); t[0, :32] = np.asarray(inputs["L1_ffn_ln_beta"])
    pr["lnb"] = t
    t = np.zeros((32, 128), np.float32); t[:, :8] = np.asarray(inputs["L1_ffn_gw2"]).T
    pr["gw2t"] = t
    t = np.zeros((1, 128), np.float32); t[0, :8] = np.asarray(inputs["L1_ffn_gb2"])
    pr["gb2"] = t
    t = np.zeros((8, 128), np.float32); t[:, :8] = np.triu(np.ones((8, 8)), 1)
    pr["lt8"] = t.astype(BF)
    pr["a1cat"] = np.ascontiguousarray(
        np.asarray(inputs["L1_ffn_lora_A"]).transpose(1, 0, 2).reshape(D, 128).astype(np.float32))
    pr["b1stack"] = np.ascontiguousarray(
        np.asarray(inputs["L1_ffn_lora_B"]).reshape(128, D).astype(np.float32))
    return pr


def _din(nc, name, shape, dt):
    return nc.dram_tensor(name, list(shape), dt, kind="ExternalInput").ap()


def _dout(nc, name, shape, dt):
    return nc.dram_tensor(name, list(shape), dt, kind="ExternalOutput").ap()


from contextlib import ExitStack


def build_A():
    nc = bass.Bass("TRN2", target_bir_lowering=False)
    aps = {}
    aps["x0s"] = _din(nc, "x0s", [1024, 4096], F32)
    aps["w0blk"] = _din(nc, "w0blk", [12, 128, 16384], F32)
    for n in ("cos", "sin", "cosq", "sinq"):
        aps[n] = _din(nc, n, [1024, 128], F32)
    aps["eye"] = _din(nc, "eye", [128, 128], BF16)
    aps["qtT"] = _dout(nc, "qtT", [4096, 1024], BF16)
    aps["kvloc"] = _dout(nc, "kvloc", [1024, 2048], BF16)
    aps["kn2"] = _dout(nc, "kn2", [1024, 8], F32)
    with ExitStack() as ctx:
        ar = Arena(nc, ctx)
        p = Prog(nc)
        emit_ph0(p, ar, aps)
        p.finalize(ctx)
    return nc


def build_B():
    nc = bass.Bass("TRN2", target_bir_lowering=False)
    aps = {}
    aps["x0s"] = _din(nc, "x0s", [1024, 4096], F32)
    aps["qtT"] = _din(nc, "qtT", [4096, 1024], BF16)
    aps["kvs"] = _din(nc, "kvs", [4096, 2048], BF16)
    aps["masks"] = _din(nc, "masks", [32, 128, 256], BF16)
    aps["scale1"] = _din(nc, "scale1", [5, 128, 256], F32)
    aps["ksb"] = _din(nc, "ksb", [128, 640], BF16)
    aps["eye"] = _din(nc, "eye", [128, 128], BF16)
    aps["w1blk"] = _din(nc, "w1blk", [16, 128, 32 * 256], F32)
    aps["aout"] = nc.dram_tensor("aout", [1024, 4096], F32, kind="Internal").ap()
    aps["xs1"] = _dout(nc, "xs1", [1024, 4096], F32)
    aps["xln1T"] = _dout(nc, "xln1T", [4096, 1024], BF16)
    with ExitStack() as ctx:
        ar = Arena(nc, ctx)
        p = Prog(nc)
        st = emit_ph1_attn(p, ar, aps)
        emit_ph1_wo(p, ar, aps, st["aoT"], st["baoT"])
        emit_lnres(p, ar, "p1c", aps["aout"], aps["x0s"], aps["xs1"], aps["xln1T"])
        p.finalize(ctx)
    return nc


def build_C():
    nc = bass.Bass("TRN2", target_bir_lowering=False)
    aps = {}
    aps["xln1T"] = _din(nc, "xln1T", [4096, 1024], BF16)
    aps["xs1"] = _din(nc, "xs1", [1024, 4096], F32)
    aps["w2blk"] = _din(nc, "w2blk", [32, 128, 32 * 128], F32)
    aps["bias0"] = _din(nc, "bias0", [1, 4096], F32)
    aps["a0t"] = _din(nc, "a0t", [3, 4096, 16], F32)
    aps["b0t"] = _din(nc, "b0t", [3, 16, 4096], F32)
    aps["lsi"] = _din(nc, "lsi", [1, 128], I32)
    aps["bidx"] = _din(nc, "bidx", [1, 128], F32)
    aps["erow3"] = _din(nc, "erow3", [1, 128], F32)
    aps["eye"] = _din(nc, "eye", [128, 128], BF16)
    aps["wq_d"] = _din(nc, "wq_d", [4096, 12], F32)
    aps["wkv_d"] = _din(nc, "wkv_d", [4096, 4], F32)
    aps["w_qk"] = _din(nc, "w_qk", [12, 128], F32)
    aps["bq_d"] = _din(nc, "bq_d", [1, 128], F32)
    aps["bkv_d"] = _din(nc, "bkv_d", [1, 128], F32)
    aps["b_qk"] = _din(nc, "b_qk", [1, 128], F32)
    aps["yffn"] = nc.dram_tensor("yffn", [1024, 4096], F32, kind="Internal").ap()
    aps["xln2T"] = nc.dram_tensor("xln2T", [4096, 1024], BF16, kind="Internal").ap()
    aps["xs2"] = _dout(nc, "xs2", [1024, 4096], F32)
    aps["ckvloc"] = _dout(nc, "ckvloc", [1024, 128], F32)
    aps["q1T"] = _dout(nc, "q1T", [4, 32 * 1024], BF16)
    with ExitStack() as ctx:
        ar = Arena(nc, ctx)
        p = Prog(nc)
        emit_ph2(p, ar, aps)
        ar.alias("p1_tr0", "p2_ptt")
        ar.alias("p1_tr1", "p2_ptt")
        xlt_buf = ar.buf("xlt2")
        emit_lnres(p, ar, "p2c", aps["yffn"], aps["xs1"], aps["xs2"], aps["xln2T"],
                   xlt_buf=xlt_buf)
        emit_l1pre(p, ar, aps, aps["xln2T"], xlt_buf)
        p.finalize(ctx)
    return nc


def build_D():
    nc = bass.Bass("TRN2", target_bir_lowering=False)
    aps = {}
    aps["q1T"] = _din(nc, "q1T", [4, 32 * 1024], BF16)
    aps["ckvg"] = _din(nc, "ckvg", [4096, 128], F32)
    aps["wv5"] = _din(nc, "wv5", [5, 4096], F32)
    aps["sel5"] = _din(nc, "sel5", [5, 128], F32)
    aps["eye"] = _din(nc, "eye", [128, 128], BF16)
    aps["w3blk"] = _din(nc, "w3blk", [16, 128, 32 * 256], F32)
    aps["bo"] = _din(nc, "bo", [1, 4096], F32)
    aps["xs2"] = _din(nc, "xs2", [1024, 4096], F32)
    aps["aout2"] = nc.dram_tensor("aout2", [1024, 4096], F32, kind="Internal").ap()
    aps["xs3"] = _dout(nc, "xs3", [1024, 4096], F32)
    aps["xln3T"] = _dout(nc, "xln3T", [4096, 1024], BF16)
    with ExitStack() as ctx:
        ar = Arena(nc, ctx)
        p = Prog(nc)
        st = emit_ph3(p, ar, aps)
        ar.alias("p1_tr0", "p3_ptc")
        ar.alias("p1_tr1", "p3_ptc")
        ar.alias("pbc", "p3_sc0")
        ar.alias("p3o_mm0", "p3_sc0")
        ar.alias("p3o_mm1", "p3_sc1")
        emit_wo_generic(p, ar, "p3o", aps["w3blk"], st["aoT2"], st["baoT2"], aps["aout2"],
                        bias_row=aps["bo"])
        emit_lnres(p, ar, "p3c", aps["aout2"], aps["xs2"], aps["xs3"], aps["xln3T"])
        p.finalize(ctx)
    return nc


def _ph4_aps(nc, with_main):
    aps = {}
    aps["xln3T"] = _din(nc, "xln3T", [4096, 1024], BF16)
    aps["gw1t"] = _din(nc, "gw1t", [4096, 32], F32)
    aps["gb1"] = _din(nc, "gb1", [1, 128], F32)
    aps["lng"] = _din(nc, "lng", [1, 128], F32)
    aps["lnb"] = _din(nc, "lnb", [1, 128], F32)
    aps["gw2t"] = _din(nc, "gw2t", [32, 128], F32)
    aps["gb2"] = _din(nc, "gb2", [1, 128], F32)
    aps["lt8"] = _din(nc, "lt8", [8, 128], BF16)
    aps["a1cat"] = _din(nc, "a1cat", [4096, 128], F32)
    aps["eye"] = _din(nc, "eye", [128, 128], BF16)
    if with_main:
        aps["b1stack"] = _din(nc, "b1stack", [128, 4096], F32)
        aps["w4blk"] = _din(nc, "w4blk", [16, 128, 32 * 256], F32)
        aps["cnts_all"] = _din(nc, "cnts_all", [8, 128], F32)
        aps["xs3"] = _din(nc, "xs3", [1024, 4096], F32)
        aps["yffn2"] = nc.dram_tensor("yffn2", [1024, 4096], F32, kind="Internal").ap()
        aps["out"] = _dout(nc, "out", [1024, 4096], F32)
    else:
        aps["cnt_loc"] = _dout(nc, "cnt_loc", [8, 128], F32)
    return aps


def build_E():
    nc = bass.Bass("TRN2", target_bir_lowering=False)
    aps = _ph4_aps(nc, False)
    with ExitStack() as ctx:
        ar = Arena(nc, ctx)
        p = Prog(nc)
        emit_ph4(p, ar, aps, with_counts_out=True, with_main=False)
        p.finalize(ctx)
    return nc


def build_F():
    nc = bass.Bass("TRN2", target_bir_lowering=False)
    aps = _ph4_aps(nc, True)
    with ExitStack() as ctx:
        ar = Arena(nc, ctx)
        p = Prog(nc)
        emit_ph4(p, ar, aps, with_counts_out=False, with_main=True)
        p.finalize(ctx)
    return nc





def build_fused():
    nc = bass.Bass("TRN2", target_bir_lowering=False)
    aps = {}
    aps["x0s"] = _din(nc, "x0s", [1024, 4096], F32)
    aps["w0blk"] = _din(nc, "w0blk", [12, 128, 16384], F32)
    for n in ("cos", "sin", "cosq", "sinq"):
        aps[n] = _din(nc, n, [1024, 128], F32)
    aps["eye"] = _din(nc, "eye", [128, 128], BF16)
    aps["masks"] = _din(nc, "masks", [32, 128, 256], BF16)
    aps["scale1"] = _din(nc, "scale1", [5, 128, 256], F32)
    aps["ksb"] = _din(nc, "ksb", [128, 640], BF16)
    aps["w1blk"] = _din(nc, "w1blk", [16, 128, 32 * 256], F32)
    aps["w2blk"] = _din(nc, "w2blk", [32, 128, 32 * 128], F32)
    aps["bias0"] = _din(nc, "bias0", [1, 4096], F32)
    aps["a0t"] = _din(nc, "a0t", [3, 4096, 16], F32)
    aps["b0t"] = _din(nc, "b0t", [3, 16, 4096], F32)
    aps["lsi"] = _din(nc, "lsi", [1, 128], I32)
    aps["bidx"] = _din(nc, "bidx", [1, 128], F32)
    aps["erow3"] = _din(nc, "erow3", [1, 128], F32)
    aps["wq_d"] = _din(nc, "wq_d", [4096, 12], F32)
    aps["wkv_d"] = _din(nc, "wkv_d", [4096, 4], F32)
    aps["w_qk"] = _din(nc, "w_qk", [12, 128], F32)
    aps["bq_d"] = _din(nc, "bq_d", [1, 128], F32)
    aps["bkv_d"] = _din(nc, "bkv_d", [1, 128], F32)
    aps["b_qk"] = _din(nc, "b_qk", [1, 128], F32)
    aps["wv5"] = _din(nc, "wv5", [5, 4096], F32)
    aps["sel5"] = _din(nc, "sel5", [5, 128], F32)
    aps["w3blk"] = _din(nc, "w3blk", [16, 128, 32 * 256], F32)
    aps["bo"] = _din(nc, "bo", [1, 4096], F32)
    aps["gw1t"] = _din(nc, "gw1t", [4096, 32], F32)
    aps["gb1"] = _din(nc, "gb1", [1, 128], F32)
    aps["lng"] = _din(nc, "lng", [1, 128], F32)
    aps["lnb"] = _din(nc, "lnb", [1, 128], F32)
    aps["gw2t"] = _din(nc, "gw2t", [32, 128], F32)
    aps["gb2"] = _din(nc, "gb2", [1, 128], F32)
    aps["lt8"] = _din(nc, "lt8", [8, 128], BF16)
    aps["a1cat"] = _din(nc, "a1cat", [4096, 128], F32)
    aps["b1stack"] = _din(nc, "b1stack", [128, 4096], F32)
    aps["w4blk"] = _din(nc, "w4blk", [16, 128, 32 * 256], F32)
    def _int(name, shape, dt):
        return nc.dram_tensor(name, list(shape), dt, kind="Internal").ap()
    aps["qtT"] = _int("qtT", [4096, 1024], BF16)
    aps["kvloc"] = _int("kvloc", [1024, 2048], BF16)
    aps["kn2"] = _int("kn2", [1024, 8], F32)
    aps["kvs"] = _int("kvs", [4096, 2048], BF16)
    aps["aout"] = _int("aout", [1024, 4096], F32)
    aps["xs1"] = _int("xs1", [1024, 4096], F32)
    aps["xln1T"] = _int("xln1T", [4096, 1024], BF16)
    aps["yffn"] = _int("yffn", [1024, 4096], F32)
    aps["xs2"] = _int("xs2", [1024, 4096], F32)
    aps["xln2T"] = _int("xln2T", [4096, 1024], BF16)
    aps["ckvloc"] = _int("ckvloc", [1024, 128], F32)
    aps["ckvg"] = _int("ckvg", [4096, 128], F32)
    aps["q1T"] = _int("q1T", [4, 32 * 1024], BF16)
    aps["aout2"] = _int("aout2", [1024, 4096], F32)
    aps["xs3"] = _int("xs3", [1024, 4096], F32)
    aps["xln3T"] = _int("xln3T", [4096, 1024], BF16)
    aps["yffn2"] = _int("yffn2", [1024, 4096], F32)
    aps["cnt_loc"] = _int("cnt_loc", [8, 128], F32)
    aps["cnts_all"] = _int("cnt_sum", [8, 128], F32)
    aps["out"] = _dout(nc, "out", [1024, 4096], F32)

    RG2 = [[0, 1, 2, 3], [4, 5, 6, 7]]
    with ExitStack() as ctx:
        ar = Arena(nc, ctx)
        p = Prog(nc)
        eye = ar.sbuf("eye", [128, 128], BF16)
        p.op("sync", lambda e: e.dma_start(eye[:, :], aps["eye"]), writes=[ar.b["eye"]], dma=True)
        onf = ar.sbuf("ones_f", [1, 128], F32)
        p.op("vector", lambda e: e.memset(onf[:, :], 1.0), writes=[ar.b["ones_f"]])
        onesr = ar.sbuf("ones_bfr", [1, 128], BF16)
        p.op("vector", lambda e: e.memset(onesr[:, :], 1.0), writes=[ar.b["ones_bfr"]])
        onesc = ar.sbuf("ones_bf", [128, 1], BF16)
        p.op("vector", lambda e: e.memset(onesc[:, :], 1.0), writes=[ar.b["ones_bf"]])
        for v in (-CBIAS_L0, -CBIAS_L1, LN_EPS, -0.5):
            ar.const_col(p, v)
        kvloc_buf = ar.buf("dram_kvloc")
        kvs_buf = ar.buf("dram_kvs")
        aps["kvloc_buf"] = kvloc_buf
        aps["kvs_buf"] = kvs_buf
        ckvl_buf = ar.buf("dram_ckvloc")
        ckvg_buf = ar.buf("dram_ckvg")
        aps["ckvloc_buf"] = ckvl_buf
        aps["ckvg_buf"] = ckvg_buf
        aps["cntl_buf"] = ar.buf("dram_cntl")
        aps["cnts_buf"] = ar.buf("dram_cnts")

        with ExitStack() as s:
            ar.push(s)
            emit_ph0(p, ar, aps)
            p.barrier()
            ar.pop()
        p.op("gpsimd", lambda e: e.collective_compute(
            "AllGather", ALU.bypass, RG2, [aps["kvloc"].opt()], [aps["kvs"].opt()]),
            writes=[kvs_buf])
        with ExitStack() as s:
            ar.push(s)
            st = emit_ph1_attn(p, ar, aps)
            emit_ph1_wo(p, ar, aps, st["aoT"], st["baoT"])
            emit_lnres(p, ar, "p1c", aps["aout"], aps["x0s"], aps["xs1"], aps["xln1T"])
            p.barrier()
            ar.pop()
        with ExitStack() as s:
            ar.push(s)
            emit_ph2(p, ar, aps)
            ar.alias("p1_tr0", "p2_ptt")
            ar.alias("p1_tr1", "p2_ptt")
            xlt_buf = ar.buf("xlt2")
            emit_lnres(p, ar, "p2c", aps["yffn"], aps["xs1"], aps["xs2"], aps["xln2T"],
                       xlt_buf=xlt_buf)
            emit_l1pre(p, ar, aps, aps["xln2T"], xlt_buf)
            p.barrier()
            ar.pop()
        p.op("gpsimd", lambda e: e.collective_compute(
            "AllGather", ALU.bypass, RG2, [aps["ckvloc"].opt()], [aps["ckvg"].opt()]),
            writes=[ckvg_buf])
        with ExitStack() as s:
            ar.push(s)
            st = emit_ph3(p, ar, aps)
            ar.alias("p1_tr0", "p3_ptc")
            ar.alias("p1_tr1", "p3_ptc")
            ar.alias("pbc", "p3_sc0")
            ar.alias("p3o_mm0", "p3_sc0")
            ar.alias("p3o_mm1", "p3_sc1")
            emit_wo_generic(p, ar, "p3o", aps["w3blk"], st["aoT2"], st["baoT2"],
                            aps["aout2"], bias_row=aps["bo"])
            emit_lnres(p, ar, "p3c", aps["aout2"], aps["xs2"], aps["xs3"], aps["xln3T"])
            p.barrier()
            ar.pop()
        with ExitStack() as s:
            ar.push(s)
            emit_ph4(p, ar, aps, with_counts_out=True, with_main=True, fused_cc=True)
            p.finalize(ctx)
            ar.pop()
    return nc


def kernel_fused(**inputs):
    pr = host_prep(inputs)
    x0 = pr["x0"]
    cos_t, sin_t, sq = pr["cos_t"], pr["sin_t"], pr["sq"]
    rows_all = [host_slot_rows(c) for c in range(NCORES)]
    nc = build_fused()
    in_maps = []
    for c in range(NCORES):
        rows = rows_all[c]
        pos = rows % SEQ
        in_maps.append({
            "x0s": np.ascontiguousarray(x0[rows]), "w0blk": pr["w0blk"],
            "cos": np.ascontiguousarray(cos_t[pos]),
            "sin": np.ascontiguousarray(sin_t[pos]),
            "cosq": np.ascontiguousarray(cos_t[pos]) * sq,
            "sinq": np.ascontiguousarray(sin_t[pos]) * sq,
            "eye": pr["eye"], "masks": host_masks(c), "scale1": host_scale1(c),
            "ksb": pr["ksb"], "w1blk": pr["w1blk"], "w2blk": pr["w2blk"],
            "bias0": pr["bias0"], "a0t": pr["a0t"], "b0t": pr["b0t"],
            "lsi": pr["lsi"], "bidx": np.full((1, 128), float(c // 4), dtype=np.float32),
            "erow3": pr["erow3"], "wq_d": pr["wq_d"], "wkv_d": pr["wkv_d"],
            "w_qk": pr["w_qk"], "bq_d": pr["bq_d"], "bkv_d": pr["bkv_d"],
            "b_qk": pr["b_qk"], "wv5": pr["wv5"], "sel5": pr["sel5"],
            "w3blk": pr["w3blk"], "bo": pr["bo"], "gw1t": pr["gw1t"],
            "gb1": pr["gb1"], "lng": pr["lng"], "lnb": pr["lnb"],
            "gw2t": pr["gw2t"], "gb2": pr["gb2"], "lt8": pr["lt8"],
            "a1cat": pr["a1cat"], "b1stack": pr["b1stack"], "w4blk": pr["w4blk"],
        })
    res = run_bass_kernel_spmd(nc, in_maps, core_ids=list(range(NCORES)))
    out = np.empty((8192, D), dtype=np.float32)
    for c in range(NCORES):
        out[rows_all[c]] = np.asarray(res.results[c]["out"])
    return out.reshape(2, SEQ, D)


def kernel_unfused(**inputs):
    pr = host_prep(inputs)
    x0 = pr["x0"]
    cos_t, sin_t, sq = pr["cos_t"], pr["sin_t"], pr["sq"]
    rows_all = [host_slot_rows(c) for c in range(NCORES)]

    ncA = build_A()
    inA = []
    for c in range(NCORES):
        rows = rows_all[c]
        pos = rows % SEQ
        inA.append({"x0s": np.ascontiguousarray(x0[rows]), "w0blk": pr["w0blk"],
                    "cos": np.ascontiguousarray(cos_t[pos]),
                    "sin": np.ascontiguousarray(sin_t[pos]),
                    "cosq": np.ascontiguousarray(cos_t[pos]) * sq,
                    "sinq": np.ascontiguousarray(sin_t[pos]) * sq,
                    "eye": pr["eye"]})
    rA = run_bass_kernel_spmd(ncA, inA, core_ids=list(range(NCORES)))

    ncB = build_B()
    inB = []
    for c in range(NCORES):
        b = c // 4
        kvs = np.ascontiguousarray(np.concatenate(
            [np.asarray(rA.results[4 * b + j]["kvloc"]) for j in range(4)], axis=0))
        inB.append({"x0s": inA[c]["x0s"], "qtT": np.asarray(rA.results[c]["qtT"]),
                    "kvs": kvs, "masks": host_masks(c), "scale1": host_scale1(c),
                    "ksb": pr["ksb"], "eye": pr["eye"], "w1blk": pr["w1blk"]})
    rB = run_bass_kernel_spmd(ncB, inB, core_ids=list(range(NCORES)))

    ncC = build_C()
    inC = []
    for c in range(NCORES):
        inC.append({"xln1T": np.asarray(rB.results[c]["xln1T"]),
                    "xs1": np.asarray(rB.results[c]["xs1"]),
                    "w2blk": pr["w2blk"], "bias0": pr["bias0"], "a0t": pr["a0t"],
                    "b0t": pr["b0t"], "lsi": pr["lsi"],
                    "bidx": np.full((1, 128), float(c // 4), dtype=np.float32),
                    "erow3": pr["erow3"], "eye": pr["eye"],
                    "wq_d": pr["wq_d"], "wkv_d": pr["wkv_d"], "w_qk": pr["w_qk"],
                    "bq_d": pr["bq_d"], "bkv_d": pr["bkv_d"], "b_qk": pr["b_qk"]})
    rC = run_bass_kernel_spmd(ncC, inC, core_ids=list(range(NCORES)))

    ncD = build_D()
    inD = []
    for c in range(NCORES):
        b = c // 4
        ckvg = np.ascontiguousarray(np.concatenate(
            [np.asarray(rC.results[4 * b + j]["ckvloc"]) for j in range(4)], axis=0))
        inD.append({"q1T": np.asarray(rC.results[c]["q1T"]), "ckvg": ckvg,
                    "wv5": pr["wv5"], "sel5": pr["sel5"], "eye": pr["eye"],
                    "w3blk": pr["w3blk"], "bo": pr["bo"],
                    "xs2": np.asarray(rC.results[c]["xs2"])})
    rD = run_bass_kernel_spmd(ncD, inD, core_ids=list(range(NCORES)))

    ncE = build_E()
    inE = []
    for c in range(NCORES):
        inE.append({"xln3T": np.asarray(rD.results[c]["xln3T"]),
                    "gw1t": pr["gw1t"], "gb1": pr["gb1"], "lng": pr["lng"],
                    "lnb": pr["lnb"], "gw2t": pr["gw2t"], "gb2": pr["gb2"],
                    "lt8": pr["lt8"], "a1cat": pr["a1cat"], "eye": pr["eye"]})
    rE = run_bass_kernel_spmd(ncE, inE, core_ids=list(range(NCORES)))
    cnts_all = np.zeros((8, 128), np.float32)
    cnts_all[:, 0:16] = np.concatenate(
        [np.asarray(rE.results[c]["cnt_loc"])[:, 0:2] for c in range(NCORES)], axis=1)

    ncF = build_F()
    inF = []
    for c in range(NCORES):
        inF.append({"xln3T": np.asarray(rD.results[c]["xln3T"]),
                    "gw1t": pr["gw1t"], "gb1": pr["gb1"], "lng": pr["lng"],
                    "lnb": pr["lnb"], "gw2t": pr["gw2t"], "gb2": pr["gb2"],
                    "lt8": pr["lt8"], "a1cat": pr["a1cat"], "eye": pr["eye"],
                    "b1stack": pr["b1stack"], "w4blk": pr["w4blk"],
                    "cnts_all": cnts_all,
                    "xs3": np.asarray(rD.results[c]["xs3"])})
    rF = run_bass_kernel_spmd(ncF, inF, core_ids=list(range(NCORES)))

    out = np.empty((8192, D), dtype=np.float32)
    for c in range(NCORES):
        out[rows_all[c]] = np.asarray(rF.results[c]["out"])
    return out.reshape(2, SEQ, D)


def kernel(**inputs):
    return kernel_fused(**inputs)


# version 10 (final)
# speedup vs baseline: 1.0143x; 1.0143x over previous
import numpy as np
import ml_dtypes
import concourse.bass as bass
import concourse.mybir as mybir
from concourse.bass_utils import run_bass_kernel_spmd

F32 = mybir.dt.float32
BF16 = mybir.dt.bfloat16
I32 = mybir.dt.int32
AF = mybir.ActivationFunctionType
ALU = mybir.AluOpType
AX = mybir.AxisListType
BF = ml_dtypes.bfloat16

NCORES = 8
SEQ = 4096
D = 4096
NT = 8
TPC = NT * 128
NSLOT = 8
H, KV, HD = 32, 8, 128
GROUPS = H // KV
SYS_LEN, IMG_LEN = 35, 576
ENH, SUP = 1.5, 0.5
ROPE_BASE = 10000.0
LN_EPS = 1e-5
ENGINES = ("sync", "scalar", "vector", "tensor", "gpsimd")


def tile_of(j, i):
    return 4 * i + ((j + i) % 4)



class _Op:
    __slots__ = ("eng", "dma", "fn", "deps", "inc_needed", "kidx", "mark")

    def __init__(self, eng, dma, fn):
        self.eng = eng
        self.dma = dma
        self.fn = fn
        self.deps = []
        self.inc_needed = False
        self.kidx = -1
        self.mark = -1


class Buf:
    __slots__ = ("name", "writer", "readers")

    def __init__(self, name):
        self.name = name
        self.writer = None
        self.readers = []


class Prog:
    def __init__(self, nc):
        self.nc = nc
        self.ops = {e: [] for e in ENGINES}

    def op(self, eng, fn, reads=(), writes=(), dma=False):
        o = _Op(eng, dma, fn)
        deps = []
        for b in reads:
            if b.writer is not None:
                deps.append(b.writer)
        for b in writes:
            if b.writer is not None:
                deps.append(b.writer)
            deps.extend(b.readers)
        seen = set()
        for d in deps:
            if d is o or id(d) in seen:
                continue
            seen.add(id(d))
            if d.eng == eng and not d.dma and not dma and eng == "tensor":
                continue
            o.deps.append(d)
        self.ops[eng].append(o)
        for b in writes:
            b.writer = o
            b.readers = []
        for b in reads:
            b.readers.append(o)
        return o

    def barrier(self):
        lasts = {e: (self.ops[e][-1] if self.ops[e] else None) for e in ENGINES}
        for e in ENGINES:
            if not self.ops[e]:
                continue
            o = _Op(e, False, None)
            for d in lasts.values():
                if d is not None and (d.eng != e or d.dma):
                    o.deps.append(d)
            self.ops[e].append(o)

    def finalize(self, ctx):
        nc = self.nc
        self.barrier()
        for e in ENGINES:
            for o in self.ops[e]:
                if o.dma:
                    o.inc_needed = True
                for d in o.deps:
                    d.inc_needed = True
        sems = {}
        for e in ENGINES:
            nd = nc_ = 0
            dm = cm = 0
            for o in self.ops[e]:
                if o.fn is None:
                    continue
                if o.dma:
                    o.kidx = nd
                    nd += 1
                    if o.inc_needed:
                        dm += 1
                    o.mark = dm
                else:
                    o.kidx = nc_
                    nc_ += 1
                    if o.inc_needed:
                        cm += 1
                    o.mark = cm
            if dm:
                sems[(e, True)] = ctx.enter_context(nc.semaphore(f"sem_{e}_d"))
            if cm:
                sems[(e, False)] = ctx.enter_context(nc.semaphore(f"sem_{e}_c"))
        block = ctx.enter_context(nc.Block())
        for e in ENGINES:
            if not self.ops[e]:
                continue
            ops = self.ops[e]

            def mk(eng, ops):
                def body(engine):
                    floor = {}
                    for o in ops:
                        need = {}
                        for d in o.deps:
                            key = (d.eng, d.dma)
                            need[key] = max(need.get(key, 0), d.mark)
                        for key, m in sorted(need.items()):
                            unit = 16 if key[1] else 1
                            if floor.get(key, 0) >= m:
                                continue
                            floor[key] = m
                            engine.wait_ge(sems[key], unit * m)
                        if o.fn is None:
                            continue
                        inst = o.fn(engine)
                        if o.inc_needed:
                            if o.dma:
                                inst.then_inc(sems[(o.eng, True)], 16)
                            else:
                                inst.then_inc(sems[(o.eng, False)], 1)
                return body

            getattr(block, e)(mk(e, ops))



class Arena:

    def __init__(self, nc, ctx):
        self.nc = nc
        self.ctx = ctx
        self.stack = [ctx]
        self.names = [[]]
        self.t = {}
        self.b = {}

    def push(self, es):
        self.stack.append(es)
        self.names.append([])

    def pop(self):
        for n in self.names.pop():
            self.t.pop(n, None)
            self.b.pop(n, None)
        self.stack.pop()

    def sbuf(self, name, shape, dtype, nbuf=None):
        if name in self.t:
            return self.t[name]
        self._uid = getattr(self, "_uid", 0) + 1
        h = self.stack[-1].enter_context(self.nc.sbuf_tensor(f"sb{self._uid}_" + name, shape, dtype))
        self.t[name] = h
        self.b[name] = Buf(name)
        self.names[-1].append(name)
        return h

    def psum(self, name, shape, dtype=F32):
        if name in self.t:
            return self.t[name]
        per_bank = 512 if dtype == F32 else 1024
        shape = [shape[0], max(shape[1], per_bank)]
        assert shape[1] % per_bank == 0, (name, shape, dtype)
        self._uid = getattr(self, "_uid", 0) + 1
        h = self.stack[-1].enter_context(self.nc.psum_tensor(f"ps{self._uid}_" + name, shape, dtype))
        self.t[name] = h
        self.b[name] = Buf(name)
        self.names[-1].append(name)
        return h

    def buf(self, name):
        b = Buf(name)
        self.b[name] = b
        return b

    def alias(self, newname, oldname):
        self.t[newname] = self.t[oldname]
        self.b[newname] = self.b[oldname]
        self.names[-1].append(newname)

    def const_col(self, p, value):
        key = ("cc", float(value))
        if key not in self.t:
            name = f"cc{len([k for k in self.t if isinstance(k, tuple)])}"
            h = self.sbuf(name, [128, 1], F32)
            b = self.b[name]
            p.op("vector", lambda e: e.memset(h[:, :], float(value)), writes=[b])
            self.t[key] = (h, b)
        return self.t[key]



def emit_ln(p, ar, pref, xin_ap, xin_buf, xout_ap, xout_buf, width=D,
            scratch=None):
    nc_ = (width + 511) // 512
    if scratch is None:
        st = ar.sbuf(pref + "_st", [128, 6 * nc_], F32)
        mv = ar.sbuf(pref + "_mv", [128, 2], F32)
        rstd = ar.sbuf(pref + "_rs", [128, 1], F32)
        nmr = ar.sbuf(pref + "_nm", [128, 1], F32)
        bst, bmv, brs, bnm = (ar.b[pref + "_st"], ar.b[pref + "_mv"],
                              ar.b[pref + "_rs"], ar.b[pref + "_nm"])
    else:
        (st, bst), (mv, bmv), (rstd, brs), (nmr, bnm) = scratch
    for c in range(nc_):
        w = min(512, width - 512 * c)
        p.op("vector", lambda e, c=c, w=w: e.bn_stats(st[:, 6 * c:6 * c + 6],
                                                      xin_ap[:, 512 * c:512 * c + w]),
             reads=[xin_buf], writes=[bst])
    p.op("vector", lambda e: e.bn_aggr(mv[:, :], st[:, :]), reads=[bst], writes=[bmv])
    eps_h, eps_b = ar.const_col(p, LN_EPS)
    nh_h, nh_b = ar.const_col(p, -0.5)
    p.op("scalar", lambda e: e.activation(rstd[:, :], mv[:, 1:2], AF.Ln,
                                          bias=eps_h[:, :]),
         reads=[bmv, eps_b], writes=[brs])
    p.op("scalar", lambda e: e.activation(rstd[:, :], rstd[:, :], AF.Exp,
                                          scale=nh_h[:, :]),
         reads=[brs, nh_b], writes=[brs])
    p.op("vector", lambda e: e.tensor_scalar(nmr[:, :], mv[:, 0:1], rstd[:, :], -1.0,
                                             ALU.mult, ALU.mult),
         reads=[bmv, brs], writes=[bnm])
    p.op("vector", lambda e: e.tensor_scalar(xout_ap, xin_ap, rstd[:, :], nmr[:, :],
                                             ALU.mult, ALU.add),
         reads=[xin_buf, bnm, brs], writes=[xout_buf])



def emit_ph0(p, ar, aps, dbg_blocks=None, dbg_tiles=None, dbg_notr=False, dbg_noln=False):
    nc = p.nc
    xt = [ar.sbuf(f"p0_x{i}", [128, D], F32) for i in range(2)]
    xln = [ar.sbuf(f"p0_l{i}", [128, D], BF16) for i in range(2)]
    xlnT = ar.sbuf("xlnT", [128, 32 * 1024], BF16)
    bxlnT = [[ar.buf(f"xlnT_{c}_{t}") for t in range(NT)] for c in range(32)]
    wblk = [ar.sbuf(f"p0_w{i}", [128, 16384], BF16) for i in range(2)]
    cosb = ar.sbuf("p0_cos", [128, NT * 128], F32)
    sinb = ar.sbuf("p0_sin", [128, NT * 128], F32)
    cosqb = ar.sbuf("p0_cosq", [128, NT * 128], F32)
    sinqb = ar.sbuf("p0_sinq", [128, NT * 128], F32)
    eye = ar.sbuf("eye", [128, 128], BF16)
    ropa = ar.sbuf("p0_ra", [128, 512], F32)
    ropb = ar.sbuf("p0_rb", [128, 512], F32)
    qout = ar.sbuf("p0_qo", [128, 512], BF16)
    qsq = ar.sbuf("p0_qsq", [128, 512], F32)
    qn2 = ar.sbuf("qn2", [128, NT * 32], F32)
    kn2 = ar.sbuf("p0_kn2", [128, NT * 8], F32)
    stg = [ar.sbuf(f"p0_s{i}", [128, 512], BF16) for i in range(2)]
    pq = [ar.psum(f"p0_pq{i}", [128, 512], F32) for i in range(2)]
    ptr = [ar.psum(f"p0_pt{i}", [128, 512], BF16) for i in range(2)]

    bxt = [ar.b[f"p0_x{i}"] for i in range(2)]
    bxln = [ar.b[f"p0_l{i}"] for i in range(2)]
    bwblk = [ar.b[f"p0_w{i}"] for i in range(2)]
    bcos, bsin, beye = ar.b["p0_cos"], ar.b["p0_sin"], ar.b["eye"]
    bcosq, bsinq = ar.b["p0_cosq"], ar.b["p0_sinq"]
    bra, brb, bqo = ar.b["p0_ra"], ar.b["p0_rb"], ar.b["p0_qo"]
    bqsq, bqn2, bkn2 = ar.b["p0_qsq"], ar.b["qn2"], ar.b["p0_kn2"]
    bstg = [ar.b[f"p0_s{i}"] for i in range(2)]
    bpq = [ar.b[f"p0_pq{i}"] for i in range(2)]
    bptr = [ar.b[f"p0_pt{i}"] for i in range(2)]

    p.op("sync", lambda e: e.dma_start(eye[:, :], aps["eye"]), writes=[beye], dma=True)
    p.op("sync", lambda e: e.dma_start(
        cosb[:, :].rearrange("p (t n) -> p t n", n=128),
        aps["cos"].rearrange("(t p) n -> p t n", p=128)),
        writes=[bcos], dma=True)
    p.op("sync", lambda e: e.dma_start(
        sinb[:, :].rearrange("p (t n) -> p t n", n=128),
        aps["sin"].rearrange("(t p) n -> p t n", p=128)),
        writes=[bsin], dma=True)
    p.op("sync", lambda e: e.dma_start(
        cosqb[:, :].rearrange("p (t n) -> p t n", n=128),
        aps["cosq"].rearrange("(t p) n -> p t n", p=128)),
        writes=[bcosq], dma=True)
    p.op("sync", lambda e: e.dma_start(
        sinqb[:, :].rearrange("p (t n) -> p t n", n=128),
        aps["sinq"].rearrange("(t p) n -> p t n", p=128)),
        writes=[bsinq], dma=True)

    for t in (range(NT) if dbg_tiles is None else dbg_tiles):
        xb, lb = xt[t % 2], xln[t % 2]
        bxb, blb = bxt[t % 2], bxln[t % 2]
        p.op("sync", lambda e, t=t, xb=xb: e.dma_start(
            xb[:, :], aps["x0s"][128 * t:128 * (t + 1), :]),
            writes=[bxb], dma=True)
        if not dbg_noln:
            emit_ln(p, ar, f"p0ln{t}", xb[:, :], bxb, lb[:, :], blb)
        else:
            p.op("vector", lambda e, xb=xb, lb=lb: e.tensor_copy(lb[:, :], xb[:, :]), reads=[bxb], writes=[blb])
        for c in (range(32) if not dbg_notr else []):
            pb, bpb = ptr[c % 2], bptr[c % 2]
            p.op("tensor", lambda e, lb=lb, c=c, pb=pb: e.transpose(
                pb[:, 0:128], lb[:, 128 * c:128 * (c + 1)], eye[:, :]),
                reads=[blb, beye], writes=[bpb])
            eng = "vector" if c % 2 == 0 else "scalar"
            dst = xlnT[:, 1024 * c + 128 * t:1024 * c + 128 * (t + 1)]
            if eng == "vector":
                p.op(eng, lambda e, dst=dst, pb=pb: e.tensor_copy(dst, pb[:, 0:128]),
                     reads=[bpb], writes=[bxlnT[c][t]])
            else:
                p.op(eng, lambda e, dst=dst, pb=pb: e.copy(dst, pb[:, 0:128]),
                     reads=[bpb], writes=[bxlnT[c][t]])

    for bblk in (range(12) if dbg_blocks is None else dbg_blocks):
        wb, bwb = wblk[bblk % 2], bwblk[bblk % 2]
        p.op("gpsimd", lambda e, bblk=bblk, wb=wb: e.dma_start(
            wb[:, :], aps["w0blk"][bblk]),
            writes=[bwb], dma=True)
        is_q = bblk < 8
        is_k = bblk in (8, 9)
        is_v = bblk >= 10
        for t in range(NT):
            pb, bpb = pq[t % 2], bpq[t % 2]
            for c in range(32):
                p.op("tensor", lambda e, c=c, t=t, pb=pb, wb=wb: e.matmul(
                    pb[:, :], xlnT[:, 1024 * c + 128 * t:1024 * c + 128 * (t + 1)],
                    wb[:, 512 * c:512 * (c + 1)],
                    start=(c == 0), stop=(c == 31)),
                    reads=[bxlnT[c][t], bwb], writes=[bpb])
            if is_v:
                sb, bsb = stg[t % 2], bstg[t % 2]
                p.op("scalar", lambda e, sb=sb, pb=pb: e.copy(sb[:, :], pb[:, :]),
                     reads=[bpb], writes=[bsb])
                co = 1024 + 512 * (bblk - 10)
                p.op("sync", lambda e, t=t, co=co, sb=sb: e.dma_start(
                    aps["kvloc"][128 * t:128 * (t + 1), co:co + 512], sb[:, :]),
                    reads=[bsb], dma=True)
            else:
                cs = (cosqb if is_q else cosb)[:, 128 * t:128 * (t + 1)]
                sn = (sinqb if is_q else sinb)[:, 128 * t:128 * (t + 1)]
                p.op("scalar", lambda e, pb=pb: e.activation(qsq[:, :], pb[:, :], AF.Square),
                     reads=[bpb], writes=[bqsq])
                for hh in range(4):
                    sl = slice(128 * hh, 128 * hh + 128)
                    if is_q:
                        h = 4 * bblk + hh
                        dst = qn2[:, 32 * t + h:32 * t + h + 1]
                        p.op("vector", lambda e, sl=sl, dst=dst: e.reduce_sum(
                            dst, qsq[:, sl], axis=AX.X), reads=[bqsq], writes=[bqn2])
                    else:
                        g = 4 * (bblk - 8) + hh
                        dst = kn2[:, 8 * t + g:8 * t + g + 1]
                        p.op("vector", lambda e, sl=sl, dst=dst: e.reduce_sum(
                            dst, qsq[:, sl], axis=AX.X), reads=[bqsq], writes=[bkn2])
                for hh in range(4):
                    sl = slice(128 * hh, 128 * hh + 128)
                    lo = slice(128 * hh, 128 * hh + 64)
                    hi = slice(128 * hh + 64, 128 * hh + 128)
                    p.op("vector", lambda e, sl=sl, cs=cs, pb=pb: e.tensor_tensor(
                        ropa[:, sl], pb[:, sl], cs, ALU.mult),
                        reads=[bpb, bcos, bcosq], writes=[bra])
                    p.op("vector", lambda e, sl=sl, sn=sn, pb=pb: e.tensor_tensor(
                        ropb[:, sl], pb[:, sl], sn, ALU.mult),
                        reads=[bpb, bsin, bsinq], writes=[brb])
                    p.op("vector", lambda e, lo=lo, hi=hi: e.tensor_tensor(
                        qout[:, lo], ropa[:, lo], ropb[:, hi], ALU.subtract),
                        reads=[bra, brb], writes=[bqo])
                    p.op("vector", lambda e, lo=lo, hi=hi: e.tensor_tensor(
                        qout[:, hi], ropa[:, hi], ropb[:, lo], ALU.add),
                        reads=[bra, brb], writes=[bqo])
                if is_k:
                    co = 512 * (bblk - 8)
                    p.op("sync", lambda e, t=t, co=co: e.dma_start(
                        aps["kvloc"][128 * t:128 * (t + 1), co:co + 512], qout[:, :]),
                        reads=[bqo], dma=True)
                else:
                    sb, bsb = stg[t % 2], bstg[t % 2]
                    for hh in range(4):
                        pt, bpt = ptr[hh % 2], bptr[hh % 2]
                        p.op("tensor", lambda e, hh=hh, pt=pt: e.transpose(
                            pt[:, 0:128], qout[:, 128 * hh:128 * (hh + 1)], eye[:, :]),
                            reads=[bqo, beye], writes=[bpt])
                        p.op("scalar", lambda e, hh=hh, pt=pt, sb=sb: e.copy(
                            sb[:, 128 * hh:128 * (hh + 1)], pt[:, 0:128]),
                            reads=[bpt], writes=[bsb])
                    dst = aps["qtT"][512 * bblk:512 * (bblk + 1),
                                     128 * t:128 * (t + 1)].rearrange(
                        "(h p) n -> p h n", p=128)
                    p.op("scalar", lambda e, dst=dst, sb=sb: e.dma_start(
                        dst, sb[:, :].rearrange("p (h n) -> p h n", n=128)),
                        reads=[bsb], dma=True)

    p.op("sync", lambda e: e.dma_start(
        aps["kn2"].rearrange("(t p) g -> p t g", p=128),
        kn2[:, :].rearrange("p (t g) -> p t g", g=8)),
        reads=[bkn2], dma=True)
    return {"xlnT": xlnT, "bxlnT": bxlnT, "qn2": qn2, "bqn2": bqn2}



CBIAS_L0 = 40.0

def kc_rowbase(cc):
    ii = cc // 4
    jj = (cc % 4 - ii) % 4
    return jj * 1024 + ii * 128


def emit_ph1_attn(p, ar, aps, dbg_units=None, dbg_noks=False, dbg_nomask=False, dbg_pt=None, dbg_nonorm=False, dbg_normmode=0, dbg_dump=None, dbg_delay=""):
    eye = ar.t.get("eye")
    if eye is None:
        eye = ar.sbuf("eye", [128, 128], BF16)
        p.op("sync", lambda e: e.dma_start(eye[:, :], aps["eye"]), writes=[ar.b["eye"]], dma=True)
    beye = ar.b["eye"]
    masks = ar.sbuf("p1_masks", [128, 32 * 256], BF16)
    bmasks = ar.b["p1_masks"]
    p.op("sync", lambda e: e.dma_start(
        masks[:, :].rearrange("p (m n) -> p m n", n=256),
        aps["masks"].rearrange("m p n -> p m n")), writes=[bmasks], dma=True)
    scale1 = ar.sbuf("p1_scl", [128, 5 * 256], F32)
    bscale1 = ar.b["p1_scl"]
    p.op("sync", lambda e: e.dma_start(
        scale1[:, :].rearrange("p (m n) -> p m n", n=256),
        aps["scale1"].rearrange("m p n -> p m n")), writes=[bscale1], dma=True)
    ksb = ar.sbuf("p1_ksb", [128, 640], BF16)
    bksb = ar.b["p1_ksb"]
    p.op("sync", lambda e: e.dma_start(ksb[:, :], aps["ksb"]), writes=[bksb], dma=True)
    ones_bf = ar.sbuf("ones_bf", [128, 1], BF16)
    bones_bf = ar.b["ones_bf"]
    p.op("vector", lambda e: e.memset(ones_bf[:, :], 1.0), writes=[bones_bf])
    ones_f = ar.sbuf("ones_f", [1, 128], F32)
    bones_f = ar.b["ones_f"]
    p.op("vector", lambda e: e.memset(ones_f[:, :], 1.0), writes=[bones_f])
    cneg_h, cneg_b = ar.const_col(p, -CBIAS_L0)

    aoT = ar.sbuf("aoT", [128, 32 * 1024], BF16)
    baoT = [[ar.buf(f"aoT_{h}_{pr}") for pr in range(4)] for h in range(32)]

    vsb = ar.sbuf("p1_v", [128, 32 * 128], BF16)
    bvsb = ar.b["p1_v"]
    kT2 = [ar.sbuf(f"p1_kT{i}", [128, 4096], BF16) for i in range(2)]
    bkT2 = [ar.b[f"p1_kT{i}"] for i in range(2)]
    kTs2 = [ar.sbuf(f"p1_kTs{i}", [128, 640], BF16) for i in range(2)]
    bkTs2 = [ar.b[f"p1_kTs{i}"] for i in range(2)]
    qh = [ar.sbuf(f"p1_qh{i}", [128, 1024], BF16) for i in range(2)]
    bqh = [ar.b[f"p1_qh{i}"] for i in range(2)]
    PT = [ar.sbuf(f"p1_PT{i}", [128, 512], BF16) for i in range(2)]
    bPT = [ar.b[f"p1_PT{i}"] for i in range(2)]
    PTm = [ar.sbuf(f"p1_PTm{i}", [128, 512], BF16) for i in range(2)]
    bPTm = [ar.b[f"p1_PTm{i}"] for i in range(2)]
    sscr = [ar.sbuf(f"p1_sscr{i}", [128, 256], F32) for i in range(2)]
    bsscr = [ar.b[f"p1_sscr{i}"] for i in range(2)]
    recb = ar.sbuf("p1_recb", [1, 256], F32)
    brecb = ar.b["p1_recb"]
    recb16 = ar.sbuf("p1_recb16", [1, 256], BF16)
    brecb16 = ar.b["p1_recb16"]
    onesbr2 = ar.t.get("ones_bfr")
    if onesbr2 is None:
        onesbr2 = ar.sbuf("ones_bfr", [1, 128], BF16)
        p.op("vector", lambda e: e.memset(onesbr2[:, :], 1.0), writes=[ar.b["ones_bfr"]])
    bonesbr2 = ar.b["ones_bfr"]
    rbb = ar.sbuf("p1_rbb", [128, 256], F32)
    brbb = ar.b["p1_rbb"]
    if dbg_delay:
        dly1 = ar.sbuf("p1_dly1", [1, 256], F32)
        bdly1 = ar.b["p1_dly1"]
        dly2 = ar.sbuf("p1_dly2", [1, 256], F32)
        bdly2 = ar.b["p1_dly2"]
        bdlyA = ar.buf("p1_dlyA")
        pdly = ar.psum("p1_pdly", [128, 512], F32)
        p.op("vector", lambda e: e.memset(dly1[:, :], 1.0), writes=[bdly1])

    psc = [ar.psum(f"p1_sc{i}", [128, 512], F32) for i in range(2)]
    bpsc = [ar.b[f"p1_sc{i}"] for i in range(2)]
    ptr = [ar.psum(f"p1_tr{i}", [128, 1024], BF16) for i in range(2)]
    bptr = [ar.b[f"p1_tr{i}"] for i in range(2)]
    pout = ar.psum("p1_out", [128, 512], F32)
    bpout = ar.b["p1_out"]
    prs = ar.psum("p1_rs", [128, 512], F32)
    bprs = ar.b["p1_rs"]
    prb = ar.psum("p1_rb", [128, 512], F32)
    bprb = ar.b["p1_rb"]

    for g in range(KV):
        if dbg_units is not None and not any(u[0] == g for u in dbg_units):
            continue
        gb = g % 2
        kT, bkT = kT2[gb], bkT2[gb]
        kTs, bkTs = kTs2[gb], bkTs2[gb]
        kvsr = [aps["kvs_buf"]] if "kvs_buf" in aps else []
        p.op("sync", lambda e, g=g, kT=kT: e.dma_start(
            kT[:, :], aps["kvs"][:, 128 * g:128 * (g + 1)], transpose=True),
            reads=kvsr, writes=[bkT], dma=True)
        p.op("sync", lambda e, g=g, vsb=vsb: e.dma_start(
            vsb[:, :].rearrange("p (cc n) -> p cc n", n=128),
            aps["kvs"][:, 1024 + 128 * g:1024 + 128 * (g + 1)].rearrange(
                "(cc p) n -> p cc n", p=128)),
            reads=kvsr, writes=[bvsb], dma=True)
        for cc in range(5):
            s = kc_rowbase(cc) // 128
            p.op("vector", lambda e, s=s, cc=cc, kTs=kTs, kT=kT: e.tensor_tensor(
                kTs[:, 128 * cc:128 * (cc + 1)], kT[:, 128 * s:128 * (s + 1)],
                ksb[:, 128 * cc:128 * (cc + 1)], ALU.mult),
                reads=[bkT, bksb], writes=[bkTs])

        for h4 in range(4):
            h = 4 * g + h4
            if dbg_units is not None and not any(u[0] == g and u[1] == h4 for u in dbg_units):
                continue
            qb, bqb = qh[h % 2], bqh[h % 2]
            p.op("sync", lambda e, h=h, qb=qb: e.dma_start(
                qb[:, :], aps["qtT"][128 * h:128 * (h + 1), :]),
                writes=[bqb], dma=True)
            for pr in range(4):
                if dbg_units is not None and (g, h4, pr) not in dbg_units:
                    continue
                ncc = 8 * (pr + 1)
                qsl = qb[:, 256 * pr:256 * (pr + 1)]
                first = {"o": True}
                for cp in range(ncc // 2):
                    pb, bpb = psc[cp % 2], bpsc[cp % 2]
                    pt, bpt = PT[cp % 2], bPT[cp % 2]
                    for half in range(2):
                        cc = 2 * cp + half
                        use_ks = (pr >= 1 and cc <= 4) and not dbg_noks
                        s = kc_rowbase(cc) // 128
                        lhs = kTs[:, 128 * cc:128 * (cc + 1)] if use_ks \
                            else kT[:, 128 * s:128 * (s + 1)]
                        lhsr = [bkTs] if use_ks else [bkT]
                        p.op("tensor", lambda e, lhs=lhs, qsl=qsl, pb=pb, half=half: e.matmul(
                            pb[:, 256 * half:256 * (half + 1)], lhs, qsl,
                            start=True, stop=True),
                            reads=lhsr + [bqb], writes=[bpb])
                    if pr == 0 and cp <= 2:
                        for half in range(2):
                            cc = 2 * cp + half
                            hs = slice(256 * half, 256 * (half + 1))
                            if cc <= 4:
                                ss, bss = sscr[half], bsscr[half]
                                p.op("vector", lambda e, cc=cc, pb=pb, hs=hs, ss=ss: e.tensor_tensor(
                                    ss[:, :], pb[:, hs], scale1[:, 256 * cc:256 * (cc + 1)],
                                    ALU.mult), reads=[bpb, bscale1], writes=[bss])
                                p.op("scalar", lambda e, pt=pt, hs=hs, ss=ss: e.activation(
                                    pt[:, hs], ss[:, :], AF.Exp, bias=cneg_h[:, :]),
                                    reads=[bss, cneg_b], writes=[bpt])
                            else:
                                p.op("scalar", lambda e, pb=pb, pt=pt, hs=hs: e.activation(
                                    pt[:, hs], pb[:, hs], AF.Exp, bias=cneg_h[:, :]),
                                    reads=[bpb, cneg_b], writes=[bpt])
                    else:
                        p.op("scalar", lambda e, pb=pb, pt=pt: e.activation(
                            pt[:, :], pb[:, :], AF.Exp, bias=cneg_h[:, :]),
                            reads=[bpb, cneg_b], writes=[bpt])
                    if dbg_pt is not None and dbg_pt.get("unit") == (g, h4, pr, cp):
                        p.op("sync", lambda e, pt=pt: e.dma_start(dbg_pt["ap"], pt[:, :]),
                             reads=[bpt], dma=True)
                    for half in range(2):
                        cc = 2 * cp + half
                        hs = slice(256 * half, 256 * (half + 1))
                        src_t, src_b = pt, bpt
                        if cc >= 8 * pr and not dbg_nomask:
                            ptm, bptm = PTm[cp % 2], bPTm[cp % 2]
                            p.op("vector", lambda e, cc=cc, pt=pt, ptm=ptm, hs=hs: e.tensor_tensor(
                                ptm[:, hs], pt[:, hs], masks[:, 256 * cc:256 * (cc + 1)],
                                ALU.mult), reads=[bpt, bmasks], writes=[bptm])
                            src_t, src_b = ptm, bptm
                        s = kc_rowbase(cc) // 128
                        last = (cc == ncc - 1)
                        st_ = first["o"]
                        p.op("tensor", lambda e, s=s, src_t=src_t, hs=hs, last=last, st_=st_, vsb=vsb: e.matmul(
                            pout[:, 0:256], vsb[:, 128 * s:128 * (s + 1)], src_t[:, hs],
                            start=st_, stop=last),
                            reads=[bvsb, src_b], writes=[bpout])
                        p.op("tensor", lambda e, src_t=src_t, hs=hs, last=last, st_=st_: e.matmul(
                            prs[0:1, 0:256], ones_bf[:, :], src_t[:, hs],
                            start=st_, stop=last),
                            reads=[bones_bf, src_b], writes=[bprs])
                        first["o"] = False
                dst = aoT[:, 1024 * h + 256 * pr:1024 * h + 256 * (pr + 1)]
                if dbg_nonorm:
                    p.op("vector", lambda e, dst=dst: e.tensor_copy(dst, pout[:, 0:256]),
                         reads=[bpout], writes=[baoT[h][pr]])
                elif dbg_normmode == 3:
                    p.op("vector", lambda e: e.reciprocal(recb[:, :], prs[0:1, 0:256]),
                         reads=[bprs], writes=[brecb])
                    p.op("tensor", lambda e: e.matmul(
                        prb[:, 0:256], ones_f[:, :], recb[:, :], start=True, stop=True),
                        reads=[bones_f, brecb], writes=[bprb])
                    p.op("scalar", lambda e: e.copy(rbb[:, :], prb[:, 0:256]),
                         reads=[bprb], writes=[brbb])
                    p.op("vector", lambda e, dst=dst: e.tensor_copy(dst, pout[:, 0:256]),
                         reads=[bpout, brbb], writes=[baoT[h][pr]])
                elif dbg_normmode == 4:
                    p.op("vector", lambda e: e.reciprocal(recb[:, :], prs[0:1, 0:256]),
                         reads=[bprs], writes=[brecb])
                    p.op("vector", lambda e, dst=dst: e.tensor_copy(dst, pout[:, 0:256]),
                         reads=[bpout, brecb], writes=[baoT[h][pr]])
                elif dbg_normmode == 1:
                    p.op("vector", lambda e: e.reciprocal(recb[:, :], prs[0:1, 0:256]),
                         reads=[bprs], writes=[brecb])
                    p.op("vector", lambda e: e.tensor_copy(recb16[:, :], recb[:, :]),
                         reads=[brecb], writes=[brecb16])
                    p.op("tensor", lambda e: e.matmul(
                        prb[:, 0:256], onesbr2[:, :], recb16[:, :], start=True, stop=True),
                        reads=[bonesbr2, brecb16], writes=[bprb])
                    p.op("scalar", lambda e: e.copy(rbb[:, :], prb[:, 0:256]),
                         reads=[bprb], writes=[brbb])
                    p.op("vector", lambda e, dst=dst: e.tensor_tensor(
                        dst, pout[:, 0:256], rbb[:, :], ALU.mult),
                        reads=[bpout, brbb], writes=[baoT[h][pr]])
                else:
                    rd_rs = [bprs]
                    if "A" in dbg_delay:
                        p.op("tensor", lambda e: e.matmul(
                            pdly[:, 0:256], ones_f[:, :], dly1[:, :],
                            start=True, stop=True), reads=[bones_f, bdly1], writes=[bdlyA])
                        rd_rs = [bprs, bdlyA]
                    p.op("vector", lambda e: e.reciprocal(recb[:, :], prs[0:1, 0:256]),
                         reads=rd_rs, writes=[brecb])
                    rd_mm = [bones_f, brecb]
                    if "B" in dbg_delay:
                        p.op("vector", lambda e: e.tensor_copy(recb16[:, :], recb[:, :]),
                             reads=[brecb], writes=[brecb16])
                        rd_mm = [bones_f, brecb, brecb16]
                    p.op("tensor", lambda e: e.matmul(
                        prb[:, 0:256], ones_f[:, :], recb[:, :], start=True, stop=True),
                        reads=rd_mm, writes=[bprb])
                    rd_cp = [bprb]
                    if "C" in dbg_delay:
                        p.op("tensor", lambda e: e.matmul(
                            pdly[:, 0:256], ones_f[:, :], dly1[:, :],
                            start=True, stop=True), reads=[bones_f, bdly1], writes=[bdlyA])
                        rd_cp = [bprb, bdlyA]
                    p.op("scalar", lambda e: e.copy(rbb[:, :], prb[:, 0:256]),
                         reads=rd_cp, writes=[brbb])
                    rd_tt = [bpout, brbb]
                    if "D" in dbg_delay:
                        p.op("scalar", lambda e: e.activation(
                            dly2[:, :], dly1[:, :], AF.Copy), reads=[bdly1], writes=[bdly2])
                        rd_tt = [bpout, brbb, bdly2]
                    if dbg_dump is not None:
                        u = 4 * h + pr
                        p.op("sync", lambda e, u=u: e.dma_start(
                            dbg_dump["recb"][u:u + 1, :], recb[:, :]),
                            reads=[brecb], dma=True)
                        p.op("sync", lambda e, u=u: e.dma_start(
                            dbg_dump["rbb"][u], rbb[:, :]),
                            reads=[brbb], dma=True)
                    p.op("vector", lambda e, dst=dst: e.tensor_tensor(
                        dst, pout[:, 0:256], rbb[:, :], ALU.mult),
                        reads=rd_tt, writes=[baoT[h][pr]])
    flat = [b for row in baoT for b in row]
    return {"aoT": aoT, "baoT": flat}


def emit_ph1_wo(p, ar, aps, aoT, baoT):
    wb = [ar.sbuf(f"p1w_{i}", [128, 32 * 256], BF16) for i in range(2)]
    bwb = [ar.b[f"p1w_{i}"] for i in range(2)]
    stg = [ar.sbuf(f"p1ws_{i}", [128, 256], F32) for i in range(2)]
    bstg = [ar.b[f"p1ws_{i}"] for i in range(2)]
    psc = [ar.t["p1_sc0"], ar.t["p1_sc1"]]
    bpsc = [ar.b["p1_sc0"], ar.b["p1_sc1"]]
    for B in range(16):
        w, bw = wb[B % 2], bwb[B % 2]
        p.op("gpsimd", lambda e, B=B, w=w: e.dma_start(w[:, :], aps["w1blk"][B]),
             writes=[bw], dma=True)
        for t in range(NT):
            pb, bpb = psc[t % 2], bpsc[t % 2]
            for c in range(32):
                p.op("tensor", lambda e, c=c, t=t, pb=pb, w=w: e.matmul(
                    pb[:, 0:256], aoT[:, 1024 * c + 128 * t:1024 * c + 128 * (t + 1)],
                    w[:, 256 * c:256 * (c + 1)],
                    start=(c == 0), stop=(c == 31)),
                    reads=baoT + [bw], writes=[bpb])
            sb, bsb = stg[t % 2], bstg[t % 2]
            p.op("scalar", lambda e, sb=sb, pb=pb: e.copy(sb[:, :], pb[:, 0:256]),
                 reads=[bpb], writes=[bsb])
            p.op("sync", lambda e, t=t, B=B, sb=sb: e.dma_start(
                aps["aout"][128 * t:128 * (t + 1), 256 * B:256 * (B + 1)], sb[:, :]),
                reads=[bsb], dma=True)


def emit_lnres(p, ar, pref, yap, resap, xsap, xltap, bias_bcast=None,
               xlt_buf=None, xs_buf=None):
    eye = ar.t["eye"]
    beye = ar.b["eye"]
    yb = ar.sbuf(pref + "_y", [128, D], F32)
    by = ar.b[pref + "_y"]
    ylb = ar.sbuf(pref + "_yl", [128, D], F32)
    byl = ar.b[pref + "_yl"]
    if xltap is not None:
        l2 = ar.sbuf(pref + "_l2", [128, D], BF16)
        bl2 = ar.b[pref + "_l2"]
        sg = ar.sbuf(pref + "_sg", [128, 512], BF16)
        bsg = ar.b[pref + "_sg"]
        if "p1_tr0" not in ar.t:
            for i in range(2):
                ar.psum(f"p1_tr{i}", [128, 1024], BF16)
        ptr = [ar.t["p1_tr0"], ar.t["p1_tr1"]]
        bptr = [ar.b["p1_tr0"], ar.b["p1_tr1"]]
    for t in range(NT):
        p.op("sync", lambda e, t=t: e.dma_start(yb[:, :], yap[128 * t:128 * (t + 1), :]),
             writes=[by], dma=True)
        if bias_bcast is not None:
            bb, bbb = bias_bcast
            p.op("vector", lambda e: e.tensor_tensor(yb[:, :], yb[:, :], bb[:, :], ALU.add),
                 reads=[by, bbb], writes=[by])
        emit_ln(p, ar, f"{pref}ln1_{t}", yb[:, :], by, ylb[:, :], byl)
        p.op("sync", lambda e, t=t: e.dma_start(yb[:, :], resap[128 * t:128 * (t + 1), :]),
             writes=[by], dma=True)
        p.op("vector", lambda e: e.tensor_tensor(ylb[:, :], ylb[:, :], yb[:, :], ALU.add),
             reads=[byl, by], writes=[byl])
        p.op("sync", lambda e, t=t: e.dma_start(xsap[128 * t:128 * (t + 1), :], ylb[:, :]),
             reads=[byl], writes=([xs_buf] if xs_buf is not None else []), dma=True)
        if xltap is None:
            continue
        emit_ln(p, ar, f"{pref}ln2_{t}", ylb[:, :], byl, l2[:, :], bl2)
        for q4 in range(8):
            for hh in range(4):
                c = 4 * q4 + hh
                pb, bpb = ptr[hh % 2], bptr[hh % 2]
                p.op("tensor", lambda e, c=c, pb=pb: e.transpose(
                    pb[:, 0:128], l2[:, 128 * c:128 * (c + 1)], eye[:, :]),
                    reads=[bl2, beye], writes=[bpb])
                p.op("scalar", lambda e, hh=hh, pb=pb: e.copy(
                    sg[:, 128 * hh:128 * (hh + 1)], pb[:, 0:128]),
                    reads=[bpb], writes=[bsg])
            dst = xltap[512 * q4:512 * (q4 + 1), 128 * t:128 * (t + 1)].rearrange(
                "(h p) n -> p h n", p=128)
            p.op("scalar", lambda e, dst=dst: e.dma_start(
                dst, sg[:, :].rearrange("p (h n) -> p h n", n=128)),
                reads=[bsg], writes=([xlt_buf] if xlt_buf is not None else []), dma=True)





def emit_bcast_row(p, ar, name, row_ap, width, row_buf=None, scale=None):
    nc = p.nc
    if "ones_f" not in ar.t:
        onf = ar.sbuf("ones_f", [1, 128], F32)
        p.op("vector", lambda e: e.memset(onf[:, :], 1.0), writes=[ar.b["ones_f"]])
    onf, bonf = ar.t["ones_f"], ar.b["ones_f"]
    lw = max(width, 128)
    row = ar.sbuf(name + "_r", [1, lw], F32)
    brow = ar.b[name + "_r"]
    out = ar.sbuf(name, [128, width], F32)
    bout = ar.b[name]
    if "pbc" not in ar.t:
        ar.psum("pbc", [128, 512], F32)
    pbc, bpbc = ar.t["pbc"], ar.b["pbc"]
    p.op("sync", lambda e: e.dma_start(row[:, :], row_ap), writes=[brow], dma=True)
    if scale is not None:
        p.op("vector", lambda e: e.tensor_scalar_mul(row[:, :], row[:, :], scale),
             reads=[brow], writes=[brow])
    for c in range((width + 511) // 512):
        w = min(512, width - 512 * c)
        p.op("tensor", lambda e, c=c, w=w: e.matmul(
            pbc[:, 0:w], onf[:, :], row[:, 512 * c:512 * c + w], start=True, stop=True),
            reads=[bonf, brow], writes=[bpbc])
        p.op("scalar", lambda e, c=c, w=w: e.copy(out[:, 512 * c:512 * c + w], pbc[:, 0:w]),
             reads=[bpbc], writes=[bout])
    return out, bout


def emit_ph2(p, ar, aps, dbg_stop=99):
    if "eye" not in ar.t:
        eye = ar.sbuf("eye", [128, 128], BF16)
        p.op("sync", lambda e: e.dma_start(eye[:, :], aps["eye"]), writes=[ar.b["eye"]], dma=True)
    x1T = ar.sbuf("p2_x1T", [128, 32 * 1024], BF16)
    bx1T = ar.b["p2_x1T"]
    p.op("sync", lambda e: e.dma_start(
        x1T[:, :].rearrange("p (c n) -> p c n", n=1024),
        aps["xln1T"].rearrange("(c p) n -> p c n", p=128)), writes=[bx1T], dma=True)
    if "ones_bfr" not in ar.t:
        onesr0 = ar.sbuf("ones_bfr", [1, 128], BF16)
        p.op("vector", lambda e: e.memset(onesr0[:, :], 1.0), writes=[ar.b["ones_bfr"]])
    onesr, bonesr = ar.t["ones_bfr"], ar.b["ones_bfr"]
    brow = ar.sbuf("p2_brow", [1, 4096], BF16)
    bbrow = ar.b["p2_brow"]
    p.op("gpsimd", lambda e: e.dma_start(brow[:, :], aps["bias0"]), writes=[bbrow], dma=True)
    lsi_i = ar.sbuf("p2_lsii", [1, 128], I32)
    blsii = ar.b["p2_lsii"]
    p.op("sync", lambda e: e.dma_start(lsi_i[:, :], aps["lsi"]), writes=[blsii], dma=True)
    lsf = ar.sbuf("p2_lsf", [1, 4], F32)
    blsf = ar.b["p2_lsf"]
    p.op("vector", lambda e: e.tensor_copy(lsf[:, :], lsi_i[:, 0:4]), reads=[blsii], writes=[blsf])
    bidx = ar.sbuf("p2_bidx", [1, 128], F32)
    bbidx = ar.b["p2_bidx"]
    p.op("sync", lambda e: e.dma_start(bidx[:, :], aps["bidx"]), writes=[bbidx], dma=True)
    sm = ar.sbuf("p2_sm", [1, 8], F32)
    bsm = ar.b["p2_sm"]
    p.op("vector", lambda e: e.tensor_scalar(sm[:, 0:4], lsf[:, :], bidx[:, 0:1], None, ALU.is_le),
         reads=[blsf, bbidx], writes=[bsm])
    p.op("vector", lambda e: e.reduce_sum(sm[:, 4:5], sm[:, 0:4], axis=AX.X),
         reads=[bsm], writes=[bsm])
    p.op("vector", lambda e: e.tensor_scalar_add(sm[:, 4:5], sm[:, 4:5], -1.0),
         reads=[bsm], writes=[bsm])
    p.op("vector", lambda e: e.tensor_scalar(sm[:, 5:6], sm[:, 4:5], 0.5, None, ALU.is_gt),
         reads=[bsm], writes=[bsm])
    p.op("vector", lambda e: e.tensor_scalar(sm[:, 6:7], sm[:, 4:5], -1.0, 0.0, ALU.add, ALU.max),
         reads=[bsm], writes=[bsm])
    p.op("vector", lambda e: e.tensor_scalar(sm[:, 6:7], sm[:, 6:7], 2.0, None, ALU.min),
         reads=[bsm], writes=[bsm])
    erow = ar.sbuf("p2_erow", [1, 128], F32)
    berow = ar.b["p2_erow"]
    p.op("sync", lambda e: e.dma_start(erow[:, :], aps["erow3"]), writes=[berow], dma=True)
    alpha = ar.sbuf("p2_alpha", [1, 3], F32)
    balpha = ar.b["p2_alpha"]
    p.op("vector", lambda e: e.tensor_scalar(alpha[:, :], erow[:, 0:3], sm[:, 6:7], None, ALU.is_equal),
         reads=[berow, bsm], writes=[balpha])
    p.op("vector", lambda e: e.tensor_scalar(alpha[:, :], alpha[:, :], sm[:, 5:6], None, ALU.mult),
         reads=[balpha, bsm], writes=[balpha])
    p.op("vector", lambda e: e.tensor_copy(sm[:, 0:3], alpha[:, :]),
         reads=[balpha], writes=[bsm])
    p.barrier()
    if "ones_f" not in ar.t:
        onf0 = ar.sbuf("ones_f", [1, 128], F32)
        p.op("vector", lambda e: e.memset(onf0[:, :], 1.0), writes=[ar.b["ones_f"]])
    onf, bonf = ar.t["ones_f"], ar.b["ones_f"]
    if "pbc" not in ar.t:
        ar.psum("pbc", [128, 512], F32)
    pbc, bpbc = ar.t["pbc"], ar.b["pbc"]
    ab = ar.sbuf("p2_ab", [128, 3], F32)
    bab = ar.b["p2_ab"]
    p.op("tensor", lambda e: e.matmul(pbc[:, 0:3], onf[:, :], alpha[:, :], start=True, stop=True),
         reads=[bonf, balpha], writes=[bpbc])
    p.op("scalar", lambda e: e.copy(ab[:, :], pbc[:, 0:3]), reads=[bpbc], writes=[bab])
    if dbg_stop <= 1:
        return
    a0 = ar.sbuf("p2_a0", [128, 3 * 512], BF16)
    ba0 = ar.b["p2_a0"]
    p.op("gpsimd", lambda e: e.dma_start(
        a0[:, :].rearrange("p (e c r) -> p e c r", e=3, r=16),
        aps["a0t"].rearrange("e (c p) r -> p e c r", p=128)), writes=[ba0], dma=True)
    aeffb = ar.sbuf("p2_aeffb", [128, 512], BF16)
    baeffb = ar.b["p2_aeffb"]
    p.op("vector", lambda e: e.tensor_scalar(aeffb[:, :], a0[:, 0:512], ab[:, 0:1], None, ALU.mult),
         reads=[ba0, bab], writes=[baeffb])
    p.op("vector", lambda e: e.scalar_tensor_tensor(aeffb[:, :], a0[:, 512:1024], ab[:, 1:2],
                                                    aeffb[:, :], ALU.mult, ALU.add),
         reads=[ba0, bab, baeffb], writes=[baeffb])
    p.op("vector", lambda e: e.scalar_tensor_tensor(aeffb[:, :], a0[:, 1024:1536], ab[:, 2:3],
                                                    aeffb[:, :], ALU.mult, ALU.add),
         reads=[ba0, bab, baeffb], writes=[baeffb])
    b0 = ar.sbuf("p2_b0", [16, 3 * 4096], BF16)
    bb0 = ar.b["p2_b0"]
    p.op("gpsimd", lambda e: e.dma_start(
        b0[:, :].rearrange("p (e n) -> p e n", e=3),
        aps["b0t"].rearrange("e p n -> p e n")), writes=[bb0], dma=True)
    beffb = ar.sbuf("p2_beffb", [16, 4096], BF16)
    bbeffb = ar.b["p2_beffb"]
    p.op("vector", lambda e: e.tensor_scalar(beffb[:, :], b0[:, 0:4096], ab[0:16, 0:1], None, ALU.mult),
         reads=[bb0, bab], writes=[bbeffb])
    p.op("vector", lambda e: e.scalar_tensor_tensor(beffb[:, :], b0[:, 4096:8192], ab[0:16, 1:2],
                                                    beffb[:, :], ALU.mult, ALU.add),
         reads=[bb0, bab, bbeffb], writes=[bbeffb])
    p.op("vector", lambda e: e.scalar_tensor_tensor(beffb[:, :], b0[:, 8192:12288], ab[0:16, 2:3],
                                                    beffb[:, :], ALU.mult, ALU.add),
         reads=[bb0, bab, bbeffb], writes=[bbeffb])
    if dbg_stop <= 2:
        return
    eye, beye = ar.t["eye"], ar.b["eye"]
    tT = ar.sbuf("p2_tT", [16, NT * 128], BF16)
    btT = ar.b["p2_tT"]
    tst = ar.sbuf("p2_tst", [128, 16], BF16)
    btst = ar.b["p2_tst"]
    plo = ar.psum("p2_plo", [128, 512], F32)
    bplo = ar.b["p2_plo"]
    ptt = ar.psum("p2_ptt", [128, 1024], BF16)
    bptt = ar.b["p2_ptt"]
    for t in range(NT):
        for c in range(32):
            p.op("tensor", lambda e, c=c, t=t: e.matmul(
                plo[:, 0:16], x1T[:, 1024 * c + 128 * t:1024 * c + 128 * (t + 1)],
                aeffb[:, 16 * c:16 * (c + 1)], start=(c == 0), stop=(c == 31)),
                reads=[bx1T, baeffb], writes=[bplo])
        p.op("scalar", lambda e: e.copy(tst[:, :], plo[:, 0:16]), reads=[bplo], writes=[btst])
        p.op("tensor", lambda e: e.transpose(ptt[0:16, 0:128], tst[:, :], eye[:, :]),
             reads=[btst, beye], writes=[bptt])
        p.op("vector", lambda e, t=t: e.tensor_copy(tT[:, 128 * t:128 * (t + 1)], ptt[0:16, 0:128]),
             reads=[bptt], writes=[btT])
    if dbg_stop <= 3:
        return
    wb = [ar.sbuf(f"p2w_{i}", [128, 32 * 128], BF16) for i in range(2)]
    bwb = [ar.b[f"p2w_{i}"] for i in range(2)]
    stg = [ar.sbuf(f"p2s_{i}", [128, 128], F32) for i in range(2)]
    bstg = [ar.b[f"p2s_{i}"] for i in range(2)]
    pmm = [ar.psum(f"p2_mm{i}", [128, 512], F32) for i in range(2)]
    bpmm = [ar.b[f"p2_mm{i}"] for i in range(2)]
    for B in range(32):
        w, bw = wb[B % 2], bwb[B % 2]
        p.op("gpsimd", lambda e, B=B, w=w: e.dma_start(w[:, :], aps["w2blk"][B]),
             writes=[bw], dma=True)
        for t in range(NT):
            pb, bpb = pmm[t % 2], bpmm[t % 2]
            for c in range(32):
                p.op("tensor", lambda e, c=c, t=t, pb=pb, w=w: e.matmul(
                    pb[:, 0:128], x1T[:, 1024 * c + 128 * t:1024 * c + 128 * (t + 1)],
                    w[:, 128 * c:128 * (c + 1)], start=(c == 0), stop=False),
                    reads=[bx1T, bw], writes=[bpb])
            p.op("tensor", lambda e, t=t, B=B, pb=pb: e.matmul(
                pb[:, 0:128], tT[:, 128 * t:128 * (t + 1)],
                beffb[:, 128 * B:128 * (B + 1)], start=False, stop=False),
                reads=[btT, bbeffb], writes=[bpb])
            p.op("tensor", lambda e, B=B, pb=pb: e.matmul(
                pb[:, 0:128], onesr[:, :], brow[:, 128 * B:128 * (B + 1)],
                start=False, stop=True), reads=[bonesr, bbrow], writes=[bpb])
            sb, bsb = stg[t % 2], bstg[t % 2]
            p.op("scalar", lambda e, sb=sb, pb=pb: e.copy(sb[:, :], pb[:, 0:128]),
                 reads=[bpb], writes=[bsb])
            p.op("sync", lambda e, t=t, B=B, sb=sb: e.dma_start(
                aps["yffn"][128 * t:128 * (t + 1), 128 * B:128 * (B + 1)], sb[:, :]),
                reads=[bsb], dma=True)


def emit_l1pre(p, ar, aps, xlt_ap, xlt_buf):
    eye, beye = ar.t["eye"], ar.b["eye"]
    xl = [ar.sbuf(f"l1p_x{i}", [128, 4096], BF16) for i in range(2)]
    bxl = [ar.b[f"l1p_x{i}"] for i in range(2)]
    wqd = ar.sbuf("l1p_wqd", [128, 32 * 12], BF16)
    bwqd = ar.b["l1p_wqd"]
    p.op("gpsimd", lambda e: e.dma_start(
        wqd[:, :].rearrange("p (c r) -> p c r", r=12),
        aps["wq_d"].rearrange("(c p) r -> p c r", p=128)), writes=[bwqd], dma=True)
    wkvd = ar.sbuf("l1p_wkvd", [128, 32 * 4], BF16)
    bwkvd = ar.b["l1p_wkvd"]
    p.op("gpsimd", lambda e: e.dma_start(
        wkvd[:, :].rearrange("p (c r) -> p c r", r=4),
        aps["wkv_d"].rearrange("(c p) r -> p c r", p=128)), writes=[bwkvd], dma=True)
    wqk = ar.sbuf("l1p_wqk", [12, 128], BF16)
    bwqk = ar.b["l1p_wqk"]
    p.op("gpsimd", lambda e: e.dma_start(wqk[:, :], aps["w_qk"]), writes=[bwqk], dma=True)
    bqd = ar.sbuf("l1p_bqd", [1, 128], BF16)
    bbqd = ar.b["l1p_bqd"]
    p.op("gpsimd", lambda e: e.dma_start(bqd[:, :], aps["bq_d"]), writes=[bbqd], dma=True)
    bkvd = ar.sbuf("l1p_bkvd", [1, 128], BF16)
    bbkvd = ar.b["l1p_bkvd"]
    p.op("gpsimd", lambda e: e.dma_start(bkvd[:, :], aps["bkv_d"]), writes=[bbkvd], dma=True)
    bqk = ar.sbuf("l1p_bqk", [1, 128], BF16)
    bbqk = ar.b["l1p_bqk"]
    p.op("gpsimd", lambda e: e.dma_start(bqk[:, :], aps["b_qk"]), writes=[bbqk], dma=True)
    onesr = ar.t.get("ones_bfr")
    if onesr is None:
        onesr = ar.sbuf("ones_bfr", [1, 128], BF16)
        p.op("vector", lambda e: e.memset(onesr[:, :], 1.0), writes=[ar.b["ones_bfr"]])
    bonesr = ar.b["ones_bfr"]
    cq = ar.sbuf("l1p_cq", [128, 16], F32)
    bcq = ar.b["l1p_cq"]
    cqb = ar.sbuf("l1p_cqb", [128, 16], BF16)
    bcqb = ar.b["l1p_cqb"]
    ckv = ar.sbuf("l1p_ckv", [128, 128], F32)
    bckv = ar.b["l1p_ckv"]
    p.op("vector", lambda e: e.memset(ckv[:, :], 0.0), writes=[bckv])
    pq = ar.psum("l1p_pq", [128, 512], F32)
    bpq = ar.b["l1p_pq"]
    pt2 = ar.psum("l1p_pt", [128, 1024], BF16)
    bpt2 = ar.b["l1p_pt"]
    pqt = ar.psum("l1p_pqt", [128, 512], F32)
    bpqt = ar.b["l1p_pqt"]
    cqT = ar.sbuf("l1p_cqT", [12, 128], BF16)
    bcqT = ar.b["l1p_cqT"]
    q1s = ar.sbuf("l1p_q1s", [4, 128], BF16)
    bq1s = ar.b["l1p_q1s"]
    for t in range(NT):
        x, bx = xl[t % 2], bxl[t % 2]
        p.op("sync", lambda e, t=t, x=x: e.dma_start(
            x[:, :].rearrange("p (c n) -> p c n", n=128),
            xlt_ap[:, 128 * t:128 * (t + 1)].rearrange("(c p) n -> p c n", p=128)),
            reads=[xlt_buf], writes=[bx], dma=True)
        for c in range(32):
            p.op("tensor", lambda e, c=c, x=x: e.matmul(
                pq[:, 0:12], x[:, 128 * c:128 * (c + 1)], wqd[:, 12 * c:12 * (c + 1)],
                start=(c == 0), stop=False), reads=[bx, bwqd], writes=[bpq])
            p.op("tensor", lambda e, c=c, x=x: e.matmul(
                pq[:, 16:20], x[:, 128 * c:128 * (c + 1)], wkvd[:, 4 * c:4 * (c + 1)],
                start=False, stop=(c == 31)), reads=[bx, bwkvd], writes=[bpq])
        p.op("tensor", lambda e: e.matmul(pq[:, 0:12], onesr[:, :], bqd[:, 0:12],
                                          start=False, stop=False),
             reads=[bonesr, bbqd], writes=[bpq])
        p.op("tensor", lambda e: e.matmul(pq[:, 16:20], onesr[:, :], bkvd[:, 0:4],
                                          start=False, stop=True),
             reads=[bonesr, bbkvd], writes=[bpq])
        p.op("vector", lambda e: e.tensor_copy(ckv[:, 0:4], pq[:, 16:20]),
             reads=[bpq], writes=[bckv])
        ckvw = [aps["ckvloc_buf"]] if "ckvloc_buf" in aps else []
        p.op("sync", lambda e, t=t: e.dma_start(
            aps["ckvloc"][128 * t:128 * (t + 1), :], ckv[:, :]), reads=[bckv],
            writes=ckvw, dma=True)
        p.op("scalar", lambda e: e.copy(cqb[:, 0:12], pq[:, 0:12]), reads=[bpq], writes=[bcqb])
        p.op("tensor", lambda e: e.transpose(pt2[0:16, 0:128], cqb[:, 0:16], eye[:, :]),
             reads=[bcqb, beye], writes=[bpt2])
        p.op("vector", lambda e: e.tensor_copy(cqT[:, :], pt2[0:12, 0:128]),
             reads=[bpt2], writes=[bcqT])
        for h in range(H):
            p.op("tensor", lambda e, h=h: e.matmul(
                pqt[0:4, 0:128], wqk[:, 4 * h:4 * (h + 1)], cqT[:, :],
                start=True, stop=False), reads=[bwqk, bcqT], writes=[bpqt])
            p.op("tensor", lambda e, h=h: e.matmul(
                pqt[0:4, 0:128], bqk[:, 4 * h:4 * (h + 1)], onesr[:, :],
                start=False, stop=True), reads=[bbqk, bonesr], writes=[bpqt])
            p.op("vector", lambda e: e.tensor_copy(q1s[:, :], pqt[0:4, 0:128]),
                 reads=[bpqt], writes=[bq1s])
            p.op("sync", lambda e, h=h, t=t: e.dma_start(
                aps["q1T"][:, 1024 * h + 128 * t:1024 * h + 128 * (t + 1)], q1s[:, :]),
                reads=[bq1s], dma=True)


CBIAS_L1 = 30.0

def emit_ph3(p, ar, aps):
    if "eye" not in ar.t:
        eye = ar.sbuf("eye", [128, 128], BF16)
        p.op("sync", lambda e: e.dma_start(eye[:, :], aps["eye"]), writes=[ar.b["eye"]], dma=True)
    eye, beye = ar.t["eye"], ar.b["eye"]
    if "ones_f" not in ar.t:
        onf = ar.sbuf("ones_f", [1, 128], F32)
        p.op("vector", lambda e: e.memset(onf[:, :], 1.0), writes=[ar.b["ones_f"]])
    onf, bonf = ar.t["ones_f"], ar.b["ones_f"]
    qhb = [ar.sbuf(f"p3_qh{i}", [4, 1024], BF16) for i in range(2)]
    bqhb = [ar.b[f"p3_qh{i}"] for i in range(2)]
    ckvf = ar.sbuf("p3_ckvf", [128, 32 * 128], F32)
    bckvf = ar.b["p3_ckvf"]
    ckvgr = [aps["ckvg_buf"]] if "ckvg_buf" in aps else []
    p.op("sync", lambda e: e.dma_start(
        ckvf[:, :].rearrange("p (c r) -> p c r", r=128),
        aps["ckvg"].rearrange("(c p) r -> p c r", p=128)), reads=ckvgr, writes=[bckvf], dma=True)
    wv5 = ar.sbuf("p3_wv5", [5, 4096], BF16)
    bwv5 = ar.b["p3_wv5"]
    p.op("gpsimd", lambda e: e.dma_start(wv5[:, :], aps["wv5"]), writes=[bwv5], dma=True)
    ckv5 = ar.sbuf("p3_ckv5", [128, 32 * 5], BF16)
    bckv5 = ar.b["p3_ckv5"]
    p.op("vector", lambda e: e.memset(ckv5[:, :], 1.0), writes=[bckv5])
    for c in range(32):
        p.op("vector", lambda e, c=c: e.tensor_copy(ckv5[:, 5 * c:5 * c + 4],
                                                    ckvf[:, 128 * c:128 * c + 4]),
             reads=[bckvf], writes=[bckv5])
    ckvb = ar.sbuf("p3_ckvb", [128, 4], BF16)
    bckvb = ar.b["p3_ckvb"]
    ckvT = ar.sbuf("p3_ckvT", [4, 4096], BF16)
    bckvT = ar.b["p3_ckvT"]
    ptc = ar.psum("p3_ptc", [128, 1024], BF16)
    bptc = ar.b["p3_ptc"]
    for c in range(32):
        p.op("vector", lambda e, c=c: e.tensor_copy(ckvb[:, :], ckvf[:, 128 * c:128 * c + 4]),
             reads=[bckvf], writes=[bckvb])
        p.op("tensor", lambda e, c=c: e.transpose(ptc[0:4, 0:128], ckvb[:, :], eye[:, :]),
             reads=[bckvb, beye], writes=[bptc])
        p.op("scalar", lambda e, c=c: e.copy(ckvT[:, 128 * c:128 * (c + 1)], ptc[0:4, 0:128]),
             reads=[bptc], writes=[bckvT])
    cneg_h, cneg_b = ar.const_col(p, -CBIAS_L1)
    aoT2 = ar.sbuf("aoT2", [128, 32 * 1024], BF16)
    baoT2 = [[ar.buf(f"aoT2_{h}_{gq}") for gq in range(2)] for h in range(H)]
    PT = [ar.sbuf(f"p3_PT{i}", [128, 512], BF16) for i in range(2)]
    bPT = [ar.b[f"p3_PT{i}"] for i in range(2)]
    ctx5 = ar.sbuf("p3_ctx5", [5, 512], BF16)
    bctx5 = ar.b["p3_ctx5"]
    sel5 = ar.sbuf("p3_sel5", [5, 128], BF16)
    bsel5 = ar.b["p3_sel5"]
    p.op("gpsimd", lambda e: e.dma_start(sel5[:, :], aps["sel5"]), writes=[bsel5], dma=True)
    recb = ar.sbuf("p3_recb", [1, 512], F32)
    brecb = ar.b["p3_recb"]
    rbb = ar.sbuf("p3_rbb", [128, 512], F32)
    brbb = ar.b["p3_rbb"]
    psc = [ar.psum(f"p3_sc{i}", [128, 512], F32) for i in range(2)]
    bpsc = [ar.b[f"p3_sc{i}"] for i in range(2)]
    pctx = ar.psum("p3_ctx", [128, 512], F32)
    bpctx = ar.b["p3_ctx"]
    pov = ar.psum("p3_ov", [128, 512], F32)
    bpov = ar.b["p3_ov"]
    prb = ar.psum("p3_rb", [128, 512], F32)
    bprb = ar.b["p3_rb"]
    pdn = ar.psum("p3_dn", [128, 512], F32)
    bpdn = ar.b["p3_dn"]
    for h in range(H):
        qhc, bqhc = qhb[h % 2], bqhb[h % 2]
        p.op("sync", lambda e, h=h, qhc=qhc: e.dma_start(
            qhc[:, :], aps["q1T"][:, 1024 * h:1024 * (h + 1)]),
            writes=[bqhc], dma=True)
        for gq in range(2):
            qsl = qhc[:, 512 * gq:512 * (gq + 1)]
            first = {"c": True}
            for c in range(32):
                pb, bpb = psc[c % 2], bpsc[c % 2]
                pt, bpt = PT[c % 2], bPT[c % 2]
                p.op("tensor", lambda e, c=c, qsl=qsl, pb=pb: e.matmul(
                    pb[:, :], ckvT[:, 128 * c:128 * (c + 1)], qsl, start=True, stop=True),
                    reads=[bckvT, bqhc], writes=[bpb])
                p.op("scalar", lambda e, pb=pb, pt=pt: e.activation(
                    pt[:, :], pb[:, :], AF.Exp, bias=cneg_h[:, :]),
                    reads=[bpb, cneg_b], writes=[bpt])
                st_ = first["c"]
                p.op("tensor", lambda e, c=c, pt=pt, st_=st_: e.matmul(
                    pctx[0:5, :], ckv5[:, 5 * c:5 * (c + 1)], pt[:, :],
                    start=st_, stop=(c == 31)), reads=[bckv5, bpt], writes=[bpctx])
                first["c"] = False
            p.op("scalar", lambda e: e.copy(ctx5[:, :], pctx[0:5, :]),
                 reads=[bpctx], writes=[bctx5])
            p.op("tensor", lambda e: e.matmul(pdn[0:1, :], sel5[:, 0:1], ctx5[:, :],
                                              start=True, stop=True),
                 reads=[bsel5, bctx5], writes=[bpdn])
            p.op("vector", lambda e: e.reciprocal(recb[:, :], pdn[0:1, :]),
                 reads=[bpdn], writes=[brecb])
            p.op("tensor", lambda e: e.matmul(prb[:, :], onf[:, :], recb[:, :],
                                              start=True, stop=True),
                 reads=[bonf, brecb], writes=[bprb])
            p.op("scalar", lambda e: e.copy(rbb[:, :], prb[:, :]),
                 reads=[bprb], writes=[brbb])
            p.op("tensor", lambda e, h=h: e.matmul(
                pov[:, :], wv5[:, 128 * h:128 * (h + 1)], ctx5[:, :],
                start=True, stop=True), reads=[bwv5, bctx5], writes=[bpov])
            dst = aoT2[:, 1024 * h + 512 * gq:1024 * h + 512 * (gq + 1)]
            p.op("vector", lambda e, dst=dst: e.tensor_tensor(
                dst, pov[:, :], rbb[:, :], ALU.mult),
                reads=[bpov, brbb], writes=[baoT2[h][gq]])
    flat = [b for row in baoT2 for b in row]
    return {"aoT2": aoT2, "baoT2": flat}


def emit_wo_generic(p, ar, pref, aps_wblk, aoT, baoT, yout_ap, nblk=16,
                    bias_row=None):
    if bias_row is not None:
        if "ones_bfr" not in ar.t:
            onesr0 = ar.sbuf("ones_bfr", [1, 128], BF16)
            p.op("vector", lambda e: e.memset(onesr0[:, :], 1.0), writes=[ar.b["ones_bfr"]])
        onesr, bonesr = ar.t["ones_bfr"], ar.b["ones_bfr"]
        brw = ar.sbuf(pref + "_brw", [1, 4096], BF16)
        bbrw = ar.b[pref + "_brw"]
        p.op("gpsimd", lambda e: e.dma_start(brw[:, :], bias_row), writes=[bbrw], dma=True)
    wb = [ar.sbuf(f"{pref}w_{i}", [128, 32 * 256], BF16) for i in range(2)]
    bwb = [ar.b[f"{pref}w_{i}"] for i in range(2)]
    stg = [ar.sbuf(f"{pref}s_{i}", [128, 256], F32) for i in range(2)]
    bstg = [ar.b[f"{pref}s_{i}"] for i in range(2)]
    pmm = [ar.psum(f"{pref}_mm{i}", [128, 512], F32) for i in range(2)]
    bpmm = [ar.b[f"{pref}_mm{i}"] for i in range(2)]
    for B in range(nblk):
        w, bw = wb[B % 2], bwb[B % 2]
        p.op("gpsimd", lambda e, B=B, w=w: e.dma_start(w[:, :], aps_wblk[B]),
             writes=[bw], dma=True)
        for t in range(NT):
            pb, bpb = pmm[t % 2], bpmm[t % 2]
            for c in range(32):
                p.op("tensor", lambda e, c=c, t=t, pb=pb, w=w: e.matmul(
                    pb[:, 0:256], aoT[:, 1024 * c + 128 * t:1024 * c + 128 * (t + 1)],
                    w[:, 256 * c:256 * (c + 1)], start=(c == 0),
                    stop=(c == 31 and bias_row is None)),
                    reads=baoT + [bw], writes=[bpb])
            if bias_row is not None:
                p.op("tensor", lambda e, B=B, pb=pb: e.matmul(
                    pb[:, 0:256], onesr[:, :], brw[:, 256 * B:256 * (B + 1)],
                    start=False, stop=True), reads=[bonesr, bbrw], writes=[bpb])
            sb, bsb = stg[t % 2], bstg[t % 2]
            p.op("scalar", lambda e, sb=sb, pb=pb: e.copy(sb[:, :], pb[:, 0:256]),
                 reads=[bpb], writes=[bsb])
            p.op("sync", lambda e, t=t, B=B, sb=sb: e.dma_start(
                yout_ap[128 * t:128 * (t + 1), 256 * B:256 * (B + 1)], sb[:, :]),
                reads=[bsb], dma=True)





def emit_ph4(p, ar, aps, with_counts_out, with_main, fused_cc=False):
    if "eye" not in ar.t:
        eye = ar.sbuf("eye", [128, 128], BF16)
        p.op("sync", lambda e: e.dma_start(eye[:, :], aps["eye"]), writes=[ar.b["eye"]], dma=True)
    eye, beye = ar.t["eye"], ar.b["eye"]
    if "ones_f" not in ar.t:
        onf = ar.sbuf("ones_f", [1, 128], F32)
        p.op("vector", lambda e: e.memset(onf[:, :], 1.0), writes=[ar.b["ones_f"]])
    onf, bonf = ar.t["ones_f"], ar.b["ones_f"]
    if "ones_bfr" not in ar.t:
        onesr = ar.sbuf("ones_bfr", [1, 128], BF16)
        p.op("vector", lambda e: e.memset(onesr[:, :], 1.0), writes=[ar.b["ones_bfr"]])
    onesr, bonesr = ar.t["ones_bfr"], ar.b["ones_bfr"]
    if "ones_bf" not in ar.t:
        oc = ar.sbuf("ones_bf", [128, 1], BF16)
        p.op("vector", lambda e: e.memset(oc[:, :], 1.0), writes=[ar.b["ones_bf"]])
    onesc, bonesc = ar.t["ones_bf"], ar.b["ones_bf"]
    xlt_buf = aps.get("xlt3_buf") or ar.buf("p4_xlt3")

    gw1 = ar.sbuf("p4_gw1", [128, 32 * 32], BF16)
    bgw1 = ar.b["p4_gw1"]
    p.op("gpsimd", lambda e: e.dma_start(
        gw1[:, :].rearrange("p (c r) -> p c r", r=32),
        aps["gw1t"].rearrange("(c p) r -> p c r", p=128)), writes=[bgw1], dma=True)
    gw2 = ar.sbuf("p4_gw2", [32, 128], BF16)
    bgw2 = ar.b["p4_gw2"]
    p.op("gpsimd", lambda e: e.dma_start(gw2[:, :], aps["gw2t"]), writes=[bgw2], dma=True)
    gb1 = ar.sbuf("p4_gb1", [1, 128], BF16)
    bgb1 = ar.b["p4_gb1"]
    p.op("gpsimd", lambda e: e.dma_start(gb1[:, :], aps["gb1"]), writes=[bgb1], dma=True)
    gb2 = ar.sbuf("p4_gb2", [1, 128], BF16)
    bgb2 = ar.b["p4_gb2"]
    p.op("gpsimd", lambda e: e.dma_start(gb2[:, :], aps["gb2"]), writes=[bgb2], dma=True)
    lt8 = ar.sbuf("p4_lt8", [8, 128], BF16)
    blt8 = ar.b["p4_lt8"]
    p.op("gpsimd", lambda e: e.dma_start(lt8[:, :], aps["lt8"]), writes=[blt8], dma=True)
    a1c = ar.sbuf("p4_a1c", [128, 32 * 128], BF16)
    ba1c = ar.b["p4_a1c"]
    p.op("gpsimd", lambda e: e.dma_start(
        a1c[:, :].rearrange("p (c r) -> p c r", r=128),
        aps["a1cat"].rearrange("(c p) r -> p c r", p=128)), writes=[ba1c], dma=True)
    gmb, bgmb = emit_bcast_row(p, ar, "p4_gmb", aps["lng"], 32)
    btb, bbtb = emit_bcast_row(p, ar, "p4_btb", aps["lnb"], 32)

    xall = ar.sbuf("p4_xall", [128, 32 * 1024], BF16)
    bxall = ar.b["p4_xall"]
    p.op("sync", lambda e: e.dma_start(
        xall[:, :].rearrange("p (c n) -> p c n", n=1024),
        aps["xln3T"].rearrange("(c p) n -> p c n", p=128)),
        reads=[xlt_buf], writes=[bxall], dma=True)
    hg = ar.sbuf("p4_hg", [128, 32], F32)
    bhg = ar.b["p4_hg"]
    hln = ar.sbuf("p4_hln", [128, 32], F32)
    bhln = ar.b["p4_hln"]
    h2b = ar.sbuf("p4_h2b", [128, 32], BF16)
    bh2b = ar.b["p4_h2b"]
    h2T = ar.sbuf("p4_h2T", [32, 128], BF16)
    bh2T = ar.b["p4_h2T"]
    gts = ar.sbuf("p4_gts", [128, 8], F32)
    bgts = ar.b["p4_gts"]
    mcol = ar.sbuf("p4_mcol", [128, 4], F32)
    bmcol = ar.b["p4_mcol"]
    dsc = ar.sbuf("p4_dsc", [128, 8], F32)
    bdsc = ar.b["p4_dsc"]
    eqf = ar.sbuf("p4_eqf", [128, 8], F32)
    beqf = ar.b["p4_eqf"]
    eqb = ar.sbuf("p4_eqb", [128, 8], BF16)
    beqb = ar.b["p4_eqb"]
    eqT = ar.sbuf("p4_eqT", [8, 128], BF16)
    beqT = ar.b["p4_eqT"]
    pfz = ar.sbuf("p4_pfz", [128, 8], F32)
    bpfz = ar.b["p4_pfz"]
    ohf = ar.sbuf("p4_ohf", [128, 8], F32)
    bohf = ar.b["p4_ohf"]
    ohs = ar.sbuf("p4_ohs", [128, 2 * 64], BF16)
    bohs = ar.b["p4_ohs"]
    gg = ar.sbuf("p4_gg", [128, 16], F32)
    bgg = ar.b["p4_gg"]
    tcat = ar.sbuf("p4_tcat", [128, NT * 128], BF16)
    btcat = ar.b["p4_tcat"]
    cnts = ar.sbuf("p4_cnts", [8, 128], F32)
    bcnts = ar.b["p4_cnts"]
    p.op("vector", lambda e: e.memset(cnts[:, :], 0.0), writes=[bcnts])
    pg = ar.psum("p4_pg", [128, 512], F32)
    bpg = ar.b["p4_pg"]
    ptr2 = ar.psum("p4_ptr", [128, 1024], BF16)
    bptr2 = ar.b["p4_ptr"]
    pcnt = ar.psum("p4_pcnt", [128, 512], F32)
    bpcnt = ar.b["p4_pcnt"]
    ptc = ar.psum("p4_ptc", [128, 512], F32)
    bptc = ar.b["p4_ptc"]

    for t in range(NT):
        for c in range(32):
            p.op("tensor", lambda e, c=c, t=t: e.matmul(
                pg[:, 0:32], xall[:, 1024 * c + 128 * t:1024 * c + 128 * (t + 1)],
                gw1[:, 32 * c:32 * (c + 1)],
                start=(c == 0), stop=False), reads=[bxall, bgw1], writes=[bpg])
        p.op("tensor", lambda e: e.matmul(pg[:, 0:32], onesr[:, :], gb1[:, 0:32],
                                          start=False, stop=True),
             reads=[bonesr, bgb1], writes=[bpg])
        p.op("vector", lambda e: e.tensor_copy(hg[:, :], pg[:, 0:32]), reads=[bpg], writes=[bhg])
        emit_ln(p, ar, f"p4ln_{t}", hg[:, :], bhg, hln[:, :], bhln, width=32)
        p.op("vector", lambda e: e.tensor_tensor(hln[:, :], hln[:, :], gmb[:, :], ALU.mult),
             reads=[bhln, bgmb], writes=[bhln])
        p.op("vector", lambda e: e.tensor_tensor(hln[:, :], hln[:, :], btb[:, :], ALU.add),
             reads=[bhln, bbtb], writes=[bhln])
        p.op("vector", lambda e: e.tensor_relu(h2b[:, :], hln[:, :]),
             reads=[bhln], writes=[bh2b])
        p.op("tensor", lambda e: e.transpose(ptr2[0:32, 0:128], h2b[:, :], eye[:, :]),
             reads=[bh2b, beye], writes=[bptr2])
        p.op("vector", lambda e: e.tensor_copy(h2T[:, :], ptr2[0:32, 0:128]),
             reads=[bptr2], writes=[bh2T])
        p.op("tensor", lambda e: e.matmul(pg[:, 64:72], h2T[:, :], gw2[:, 0:8],
                                          start=True, stop=False),
             reads=[bh2T, bgw2], writes=[bpg])
        p.op("tensor", lambda e: e.matmul(pg[:, 64:72], onesr[:, :], gb2[:, 0:8],
                                          start=False, stop=True),
             reads=[bonesr, bgb2], writes=[bpg])
        p.op("vector", lambda e: e.tensor_copy(gts[:, :], pg[:, 64:72]), reads=[bpg], writes=[bgts])
        for k in range(2):
            gsrc, bgsrc = (gts, bgts) if k == 0 else (dsc, bdsc)
            p.op("vector", lambda e, gsrc=gsrc, k=k: e.reduce_max(
                mcol[:, k:k + 1], gsrc[:, :], axis=AX.X), reads=[bgsrc], writes=[bmcol])
            p.op("vector", lambda e, gsrc=gsrc, k=k: e.tensor_scalar(
                eqf[:, :], gsrc[:, :], mcol[:, k:k + 1], None, ALU.is_ge),
                reads=[bgsrc, bmcol], writes=[beqf])
            p.op("vector", lambda e: e.tensor_copy(eqb[:, :], eqf[:, :]),
                 reads=[beqf], writes=[beqb])
            p.op("tensor", lambda e: e.transpose(ptr2[0:8, 0:128], eqb[:, :], eye[:, :]),
                 reads=[beqb, beye], writes=[bptr2])
            p.op("vector", lambda e: e.tensor_copy(eqT[:, :], ptr2[0:8, 0:128]),
                 reads=[bptr2], writes=[beqT])
            p.op("tensor", lambda e: e.matmul(pg[:, 96:104], eqT[:, :], lt8[:, 0:8],
                                              start=True, stop=True),
                 reads=[beqT, blt8], writes=[bpg])
            p.op("vector", lambda e: e.tensor_scalar(pfz[:, :], pg[:, 96:104], 0.5, None, ALU.is_le),
                 reads=[bpg], writes=[bpfz])
            p.op("vector", lambda e: e.tensor_tensor(ohf[:, :], eqf[:, :], pfz[:, :], ALU.mult),
                 reads=[beqf, bpfz], writes=[bohf])
            p.op("vector", lambda e, k=k, t=t: e.tensor_copy(
                ohs[:, 64 * k + 8 * t:64 * k + 8 * (t + 1)], ohf[:, :]),
                reads=[bohf], writes=[bohs])
            p.op("tensor", lambda e, k=k, t=t: e.matmul(
                pcnt[0:8, k:k + 1], ohs[:, 64 * k + 8 * t:64 * k + 8 * (t + 1)],
                onesc[:, :], start=(k == 0 and t == 0), stop=(k == 1 and t == NT - 1)),
                reads=[bohs, bonesc], writes=[bpcnt])
            if k == 0:
                p.op("vector", lambda e: e.scalar_tensor_tensor(
                    dsc[:, :], ohf[:, :], -1e30, gts[:, :], ALU.mult, ALU.add),
                    reads=[bohf, bgts], writes=[bdsc])
        p.op("vector", lambda e: e.tensor_tensor(mcol[:, 2:3], mcol[:, 1:2], mcol[:, 0:1],
                                                 ALU.subtract), reads=[bmcol], writes=[bmcol])
        p.op("scalar", lambda e: e.activation(mcol[:, 2:3], mcol[:, 2:3], AF.Exp),
             reads=[bmcol], writes=[bmcol])
        p.op("vector", lambda e: e.tensor_scalar_add(mcol[:, 2:3], mcol[:, 2:3], 1.0),
             reads=[bmcol], writes=[bmcol])
        p.op("vector", lambda e: e.reciprocal(mcol[:, 3:4], mcol[:, 2:3]),
             reads=[bmcol], writes=[bmcol])
        p.op("vector", lambda e, t=t: e.tensor_copy(gg[:, t:t + 1], mcol[:, 3:4]),
             reads=[bmcol], writes=[bgg])
        p.op("vector", lambda e, t=t: e.tensor_scalar(gg[:, 8 + t:9 + t], mcol[:, 3:4],
                                                      -1.0, 1.0, ALU.mult, ALU.add),
             reads=[bmcol], writes=[bgg])
        for c in range(32):
            p.op("tensor", lambda e, c=c, t=t: e.matmul(
                ptc[:, 0:128], xall[:, 1024 * c + 128 * t:1024 * c + 128 * (t + 1)],
                a1c[:, 128 * c:128 * (c + 1)],
                start=(c == 0), stop=(c == 31)), reads=[bxall, ba1c], writes=[bptc])
        p.op("vector", lambda e, t=t: e.tensor_copy(tcat[:, 128 * t:128 * (t + 1)], ptc[:, 0:128]),
             reads=[bptc], writes=[btcat])
    p.op("vector", lambda e: e.tensor_copy(cnts[:, 0:2], pcnt[0:8, 0:2]),
         reads=[bpcnt], writes=[bcnts])
    if with_counts_out:
        cloc_w = [aps["cntl_buf"]] if "cntl_buf" in aps else []
        p.op("sync", lambda e: e.dma_start(aps["cnt_loc"], cnts[:, :]),
             reads=[bcnts], writes=cloc_w, dma=True)
    if fused_cc:
        p.op("gpsimd", lambda e: e.collective_compute(
            "AllReduce", ALU.add, [list(range(NCORES))],
            [aps["cnt_loc"].opt()], [aps["cnts_all"].opt()]),
            reads=[aps["cntl_buf"]], writes=[aps["cnts_buf"]])
    if not with_main:
        return

    call = ar.sbuf("p4_call", [8, 128], F32)
    bcall = ar.b["p4_call"]
    car = [aps["cnts_buf"]] if "cnts_buf" in aps else []
    p.op("sync", lambda e: e.dma_start(call[:, :], aps["cnts_all"]), reads=car,
         writes=[bcall], dma=True)
    cg = ar.sbuf("p4_cg", [8, 2], F32)
    bcg = ar.b["p4_cg"]
    if fused_cc:
        p.op("vector", lambda e: e.tensor_copy(cg[:, :], call[:, 0:2]),
             reads=[bcall], writes=[bcg])
    else:
        p.op("vector", lambda e: e.reduce_sum(
            cg[:, :], call[:, 0:16].rearrange("p (c k) -> p k c", k=2), axis=AX.X),
            reads=[bcall], writes=[bcg])
    alw = ar.sbuf("p4_alw", [8, 2], BF16)
    balw = ar.b["p4_alw"]
    p.op("vector", lambda e: e.tensor_scalar(alw[:, :], cg[:, :], 3072.5, None, ALU.is_le),
         reads=[bcg], writes=[balw])
    alwT = ar.sbuf("p4_alwT", [1, 8], F32)
    balwT = ar.b["p4_alwT"]
    aw = ar.sbuf("p4_aw", [128, 16], F32)
    baw = ar.b["p4_aw"]
    pbc, bpbc = ar.t["pbc"], ar.b["pbc"]
    for k in range(2):
        p.op("tensor", lambda e, k=k: e.transpose(ptr2[0:1, 0:8], alw[:, k:k + 1],
                                                  eye[0:8, 0:8]),
             reads=[balw, beye], writes=[bptr2])
        p.op("scalar", lambda e: e.copy(alwT[:, :], ptr2[0:1, 0:8]), reads=[bptr2], writes=[balwT])
        p.op("tensor", lambda e, k=k: e.matmul(pbc[:, 0:8], onf[:, :], alwT[:, :],
                                               start=True, stop=True),
             reads=[bonf, balwT], writes=[bpbc])
        p.op("vector", lambda e, k=k: e.tensor_scalar(aw[:, 8 * k:8 * (k + 1)], pbc[:, 0:8],
                                                      1.0 / 16.0, None, ALU.mult),
             reads=[bpbc], writes=[baw])

    wfin = ar.sbuf("p4_wfin", [128, 8], F32)
    bwfin = ar.b["p4_wfin"]
    wtmp = ar.sbuf("p4_wtmp", [128, 8], F32)
    bwtmp = ar.b["p4_wtmp"]
    tT = ar.sbuf("p4_tT", [128, NT * 128], BF16)
    btT = ar.b["p4_tT"]
    tsc = ar.sbuf("p4_tsc", [128, 128], BF16)
    btsc = ar.b["p4_tsc"]
    b1s = ar.sbuf("p4_b1s", [128, 4096], BF16)
    bb1s = ar.b["p4_b1s"]
    p.op("gpsimd", lambda e: e.dma_start(b1s[:, :], aps["b1stack"]), writes=[bb1s], dma=True)
    for t in range(NT):
        p.op("vector", lambda e, t=t: e.tensor_scalar(
            wfin[:, :], ohs[:, 8 * t:8 * (t + 1)], gg[:, t:t + 1], None, ALU.mult),
            reads=[bohs, bgg], writes=[bwfin])
        p.op("vector", lambda e: e.tensor_tensor(wfin[:, :], wfin[:, :], aw[:, 0:8], ALU.mult),
             reads=[bwfin, baw], writes=[bwfin])
        p.op("vector", lambda e, t=t: e.tensor_scalar(
            wtmp[:, :], ohs[:, 64 + 8 * t:64 + 8 * (t + 1)], gg[:, 8 + t:9 + t], None, ALU.mult),
            reads=[bohs, bgg], writes=[bwtmp])
        p.op("vector", lambda e: e.tensor_tensor(wtmp[:, :], wtmp[:, :], aw[:, 8:16], ALU.mult),
             reads=[bwtmp, baw], writes=[bwtmp])
        p.op("vector", lambda e: e.tensor_tensor(wfin[:, :], wfin[:, :], wtmp[:, :], ALU.add),
             reads=[bwfin, bwtmp], writes=[bwfin])
        for e_ in range(8):
            p.op("vector", lambda e, t=t, e_=e_: e.tensor_scalar(
                tsc[:, 16 * e_:16 * (e_ + 1)], tcat[:, 128 * t + 16 * e_:128 * t + 16 * (e_ + 1)],
                wfin[:, e_:e_ + 1], None, ALU.mult),
                reads=[btcat, bwfin], writes=[btsc])
        p.op("tensor", lambda e: e.transpose(ptr2[:, 0:128], tsc[:, :], eye[:, :]),
             reads=[btsc, beye], writes=[bptr2])
        p.op("vector", lambda e, t=t: e.tensor_copy(tT[:, 128 * t:128 * (t + 1)], ptr2[:, 0:128]),
             reads=[bptr2], writes=[btT])
    wb = [ar.sbuf(f"p4w_{i}", [128, 32 * 256], BF16) for i in range(2)]
    bwb = [ar.b[f"p4w_{i}"] for i in range(2)]
    stg = [ar.sbuf(f"p4s_{i}", [128, 256], F32) for i in range(2)]
    bstg = [ar.b[f"p4s_{i}"] for i in range(2)]
    pmm = [ar.psum(f"p4_mm{i}", [128, 512], F32) for i in range(2)]
    bpmm = [ar.b[f"p4_mm{i}"] for i in range(2)]
    for B in range(16):
        w, bw = wb[B % 2], bwb[B % 2]
        p.op("gpsimd", lambda e, B=B, w=w: e.dma_start(w[:, :], aps["w4blk"][B]),
             writes=[bw], dma=True)
        for t in range(NT):
            pb, bpb = pmm[t % 2], bpmm[t % 2]
            for c in range(32):
                p.op("tensor", lambda e, c=c, t=t, pb=pb, w=w: e.matmul(
                    pb[:, 0:256], xall[:, 1024 * c + 128 * t:1024 * c + 128 * (t + 1)],
                    w[:, 256 * c:256 * (c + 1)], start=(c == 0), stop=False),
                    reads=[bxall, bw], writes=[bpb])
            p.op("tensor", lambda e, t=t, B=B, pb=pb: e.matmul(
                pb[:, 0:256], tT[:, 128 * t:128 * (t + 1)], b1s[:, 256 * B:256 * (B + 1)],
                start=False, stop=True), reads=[btT, bb1s], writes=[bpb])
            sb, bsb = stg[t % 2], bstg[t % 2]
            p.op("scalar", lambda e, sb=sb, pb=pb: e.copy(sb[:, :], pb[:, 0:256]),
                 reads=[bpb], writes=[bsb])
            p.op("sync", lambda e, t=t, B=B, sb=sb: e.dma_start(
                aps["yffn2"][128 * t:128 * (t + 1), 256 * B:256 * (B + 1)], sb[:, :]),
                reads=[bsb], dma=True)
    emit_lnres(p, ar, "p4f", aps["yffn2"], aps["xs3"], aps["out"], None)



def host_slot_rows(core):
    b, j = divmod(core, 4)
    rows = []
    for i in range(NSLOT):
        T = tile_of(j, i)
        base = b * SEQ + T * 128
        rows.extend(range(base, base + 128))
    return np.array(rows)


def host_rope_tables():
    inv_freq = 1.0 / ROPE_BASE ** (np.arange(0, HD, 2, dtype=np.float64) / HD)
    pos = np.arange(SEQ, dtype=np.float64)
    freqs = pos[:, None] * inv_freq[None, :]
    emb = np.concatenate([freqs, freqs], axis=-1)
    return np.cos(emb).astype(np.float32), np.sin(emb).astype(np.float32)


def host_wblk(wT, bw=512):
    ink, out = wT.shape
    assert ink == 4096 and out % bw == 0
    nb = out // bw
    v = wT.reshape(32, 128, nb, bw)
    return np.ascontiguousarray(
        v.transpose(2, 1, 0, 3).reshape(nb, 128, 32 * bw)).astype(BF)


def host_w0blk(wq, wk, wv):
    return host_wblk(np.concatenate([wq.T, wk.T, wv.T], axis=1))


def host_key_scale_row():
    ks = np.ones(SEQ, np.float32)
    ks[:SYS_LEN] = SUP
    ks[SYS_LEN:SYS_LEN + IMG_LEN] = ENH
    return ks


def host_masks(core):
    j = core % 4
    out = np.zeros((32, 128, 256), np.float32)
    for cc in range(32):
        pr = cc // 8
        kpos = cc * 128 + np.arange(128)
        for hs in range(2):
            T = tile_of(j, 2 * pr + hs)
            qpos = T * 128 + np.arange(128)
            out[cc][:, 128 * hs:128 * (hs + 1)] = (kpos[:, None] <= qpos[None, :])
    return out.astype(BF)


def host_scale1(core):
    j = core % 4
    ks = host_key_scale_row()
    out = np.ones((5, 128, 256), np.float32)
    for cc in range(5):
        kpos = cc * 128 + np.arange(128)
        for hs in range(2):
            T = tile_of(j, hs)
            qpos = T * 128 + np.arange(128)
            row_sel = (qpos >= SYS_LEN + IMG_LEN)
            out[cc][:, 128 * hs:128 * (hs + 1)] = np.where(
                row_sel[None, :], ks[kpos][:, None], 1.0)
    return out


def host_ksb():
    return np.ascontiguousarray(
        np.broadcast_to(host_key_scale_row()[:640], (128, 640))).astype(BF)


def host_prep(inputs):
    pr = {}
    x0 = np.ascontiguousarray(np.asarray(inputs["L0_attn_hidden_states"]).reshape(8192, D))
    pr["x0"] = x0
    cos_t, sin_t = host_rope_tables()
    sq = 1.0 / np.sqrt(128.0)
    pr["cos_t"], pr["sin_t"], pr["sq"] = cos_t, sin_t, sq
    pr["w0blk"] = host_w0blk(np.asarray(inputs["L0_attn_wq"]),
                             np.asarray(inputs["L0_attn_wk"]),
                             np.asarray(inputs["L0_attn_wv"]))
    pr["w1blk"] = host_wblk(np.ascontiguousarray(np.asarray(inputs["L0_attn_wo"]).T), 256)
    pr["w2blk"] = host_wblk(np.ascontiguousarray(np.asarray(inputs["L0_ffn_weight"]).T), 128)
    pr["w3blk"] = host_wblk(np.ascontiguousarray(np.asarray(inputs["L1_attn_Wo"])), 256)
    pr["w4blk"] = host_wblk(np.ascontiguousarray(np.asarray(inputs["L1_ffn_weight"]).T), 256)
    pr["eye"] = np.eye(128, dtype=np.float32).astype(BF)
    pr["ksb"] = host_ksb()
    pr["bias0"] = np.asarray(inputs["L0_ffn_bias"]).reshape(1, D).astype(np.float32)
    pr["a0t"] = np.ascontiguousarray(np.asarray(inputs["L0_ffn_lora_A"]).transpose(0, 2, 1))
    pr["b0t"] = np.ascontiguousarray(np.asarray(inputs["L0_ffn_lora_B"]).transpose(0, 2, 1))
    lsi = np.zeros((1, 128), np.int32)
    lsi[0, :4] = np.asarray(inputs["L0_ffn_lora_start_indices"]).reshape(4)
    pr["lsi"] = lsi
    er = np.zeros((1, 128), np.float32)
    er[0, :3] = [0.0, 1.0, 2.0]
    pr["erow3"] = er
    pr["wq_d"] = np.ascontiguousarray(np.asarray(inputs["L1_attn_Wq_d"]).astype(np.float32))
    pr["wkv_d"] = np.ascontiguousarray(np.asarray(inputs["L1_attn_Wkv_d"]).astype(np.float32))
    pr["w_qk"] = np.ascontiguousarray(np.asarray(inputs["L1_attn_W_qk"]).astype(np.float32) * 0.5)
    t = np.zeros((1, 128), np.float32); t[0, :12] = np.asarray(inputs["L1_attn_bq_d"])
    pr["bq_d"] = t
    t = np.zeros((1, 128), np.float32); t[0, :4] = np.asarray(inputs["L1_attn_bkv_d"])
    pr["bkv_d"] = t
    pr["b_qk"] = (np.asarray(inputs["L1_attn_b_qk"]).reshape(1, 128).astype(np.float32) * 0.5)
    pr["wv5"] = np.ascontiguousarray(np.concatenate(
        [np.asarray(inputs["L1_attn_Wv_u"]).astype(np.float32),
         np.asarray(inputs["L1_attn_bv_u"]).reshape(1, D).astype(np.float32)], axis=0))
    t = np.zeros((5, 128), np.float32); t[4, 0] = 1.0
    pr["sel5"] = t
    pr["bo"] = np.asarray(inputs["L1_attn_bo"]).reshape(1, D).astype(np.float32)
    pr["gw1t"] = np.ascontiguousarray(np.asarray(inputs["L1_ffn_gw1"]).T.astype(np.float32))
    t = np.zeros((1, 128), np.float32); t[0, :32] = np.asarray(inputs["L1_ffn_gb1"])
    pr["gb1"] = t
    t = np.zeros((1, 128), np.float32); t[0, :32] = np.asarray(inputs["L1_ffn_ln_gamma"])
    pr["lng"] = t
    t = np.zeros((1, 128), np.float32); t[0, :32] = np.asarray(inputs["L1_ffn_ln_beta"])
    pr["lnb"] = t
    t = np.zeros((32, 128), np.float32); t[:, :8] = np.asarray(inputs["L1_ffn_gw2"]).T
    pr["gw2t"] = t
    t = np.zeros((1, 128), np.float32); t[0, :8] = np.asarray(inputs["L1_ffn_gb2"])
    pr["gb2"] = t
    t = np.zeros((8, 128), np.float32); t[:, :8] = np.triu(np.ones((8, 8)), 1)
    pr["lt8"] = t.astype(BF)
    pr["a1cat"] = np.ascontiguousarray(
        np.asarray(inputs["L1_ffn_lora_A"]).transpose(1, 0, 2).reshape(D, 128).astype(np.float32))
    pr["b1stack"] = np.ascontiguousarray(
        np.asarray(inputs["L1_ffn_lora_B"]).reshape(128, D).astype(np.float32))
    return pr


def _din(nc, name, shape, dt):
    return nc.dram_tensor(name, list(shape), dt, kind="ExternalInput").ap()


def _dout(nc, name, shape, dt):
    return nc.dram_tensor(name, list(shape), dt, kind="ExternalOutput").ap()


from contextlib import ExitStack


def build_A():
    nc = bass.Bass("TRN2", target_bir_lowering=False)
    aps = {}
    aps["x0s"] = _din(nc, "x0s", [1024, 4096], F32)
    aps["w0blk"] = _din(nc, "w0blk", [12, 128, 16384], BF16)
    for n in ("cos", "sin", "cosq", "sinq"):
        aps[n] = _din(nc, n, [1024, 128], F32)
    aps["eye"] = _din(nc, "eye", [128, 128], BF16)
    aps["qtT"] = _dout(nc, "qtT", [4096, 1024], BF16)
    aps["kvloc"] = _dout(nc, "kvloc", [1024, 2048], BF16)
    aps["kn2"] = _dout(nc, "kn2", [1024, 8], F32)
    with ExitStack() as ctx:
        ar = Arena(nc, ctx)
        p = Prog(nc)
        emit_ph0(p, ar, aps)
        p.finalize(ctx)
    return nc


def build_B():
    nc = bass.Bass("TRN2", target_bir_lowering=False)
    aps = {}
    aps["x0s"] = _din(nc, "x0s", [1024, 4096], F32)
    aps["qtT"] = _din(nc, "qtT", [4096, 1024], BF16)
    aps["kvs"] = _din(nc, "kvs", [4096, 2048], BF16)
    aps["masks"] = _din(nc, "masks", [32, 128, 256], BF16)
    aps["scale1"] = _din(nc, "scale1", [5, 128, 256], F32)
    aps["ksb"] = _din(nc, "ksb", [128, 640], BF16)
    aps["eye"] = _din(nc, "eye", [128, 128], BF16)
    aps["w1blk"] = _din(nc, "w1blk", [16, 128, 32 * 256], BF16)
    aps["aout"] = nc.dram_tensor("aout", [1024, 4096], F32, kind="Internal").ap()
    aps["xs1"] = _dout(nc, "xs1", [1024, 4096], F32)
    aps["xln1T"] = _dout(nc, "xln1T", [4096, 1024], BF16)
    with ExitStack() as ctx:
        ar = Arena(nc, ctx)
        p = Prog(nc)
        st = emit_ph1_attn(p, ar, aps)
        emit_ph1_wo(p, ar, aps, st["aoT"], st["baoT"])
        emit_lnres(p, ar, "p1c", aps["aout"], aps["x0s"], aps["xs1"], aps["xln1T"])
        p.finalize(ctx)
    return nc


def build_C():
    nc = bass.Bass("TRN2", target_bir_lowering=False)
    aps = {}
    aps["xln1T"] = _din(nc, "xln1T", [4096, 1024], BF16)
    aps["xs1"] = _din(nc, "xs1", [1024, 4096], F32)
    aps["w2blk"] = _din(nc, "w2blk", [32, 128, 32 * 128], BF16)
    aps["bias0"] = _din(nc, "bias0", [1, 4096], F32)
    aps["a0t"] = _din(nc, "a0t", [3, 4096, 16], F32)
    aps["b0t"] = _din(nc, "b0t", [3, 16, 4096], F32)
    aps["lsi"] = _din(nc, "lsi", [1, 128], I32)
    aps["bidx"] = _din(nc, "bidx", [1, 128], F32)
    aps["erow3"] = _din(nc, "erow3", [1, 128], F32)
    aps["eye"] = _din(nc, "eye", [128, 128], BF16)
    aps["wq_d"] = _din(nc, "wq_d", [4096, 12], F32)
    aps["wkv_d"] = _din(nc, "wkv_d", [4096, 4], F32)
    aps["w_qk"] = _din(nc, "w_qk", [12, 128], F32)
    aps["bq_d"] = _din(nc, "bq_d", [1, 128], F32)
    aps["bkv_d"] = _din(nc, "bkv_d", [1, 128], F32)
    aps["b_qk"] = _din(nc, "b_qk", [1, 128], F32)
    aps["yffn"] = nc.dram_tensor("yffn", [1024, 4096], F32, kind="Internal").ap()
    aps["xln2T"] = nc.dram_tensor("xln2T", [4096, 1024], BF16, kind="Internal").ap()
    aps["xs2"] = _dout(nc, "xs2", [1024, 4096], F32)
    aps["ckvloc"] = _dout(nc, "ckvloc", [1024, 128], F32)
    aps["q1T"] = _dout(nc, "q1T", [4, 32 * 1024], BF16)
    with ExitStack() as ctx:
        ar = Arena(nc, ctx)
        p = Prog(nc)
        emit_ph2(p, ar, aps)
        ar.alias("p1_tr0", "p2_ptt")
        ar.alias("p1_tr1", "p2_ptt")
        xlt_buf = ar.buf("xlt2")
        emit_lnres(p, ar, "p2c", aps["yffn"], aps["xs1"], aps["xs2"], aps["xln2T"],
                   xlt_buf=xlt_buf)
        emit_l1pre(p, ar, aps, aps["xln2T"], xlt_buf)
        p.finalize(ctx)
    return nc


def build_D():
    nc = bass.Bass("TRN2", target_bir_lowering=False)
    aps = {}
    aps["q1T"] = _din(nc, "q1T", [4, 32 * 1024], BF16)
    aps["ckvg"] = _din(nc, "ckvg", [4096, 128], F32)
    aps["wv5"] = _din(nc, "wv5", [5, 4096], F32)
    aps["sel5"] = _din(nc, "sel5", [5, 128], F32)
    aps["eye"] = _din(nc, "eye", [128, 128], BF16)
    aps["w3blk"] = _din(nc, "w3blk", [16, 128, 32 * 256], BF16)
    aps["bo"] = _din(nc, "bo", [1, 4096], F32)
    aps["xs2"] = _din(nc, "xs2", [1024, 4096], F32)
    aps["aout2"] = nc.dram_tensor("aout2", [1024, 4096], F32, kind="Internal").ap()
    aps["xs3"] = _dout(nc, "xs3", [1024, 4096], F32)
    aps["xln3T"] = _dout(nc, "xln3T", [4096, 1024], BF16)
    with ExitStack() as ctx:
        ar = Arena(nc, ctx)
        p = Prog(nc)
        st = emit_ph3(p, ar, aps)
        ar.alias("p1_tr0", "p3_ptc")
        ar.alias("p1_tr1", "p3_ptc")
        ar.alias("pbc", "p3_sc0")
        ar.alias("p3o_mm0", "p3_sc0")
        ar.alias("p3o_mm1", "p3_sc1")
        emit_wo_generic(p, ar, "p3o", aps["w3blk"], st["aoT2"], st["baoT2"], aps["aout2"],
                        bias_row=aps["bo"])
        emit_lnres(p, ar, "p3c", aps["aout2"], aps["xs2"], aps["xs3"], aps["xln3T"])
        p.finalize(ctx)
    return nc


def _ph4_aps(nc, with_main):
    aps = {}
    aps["xln3T"] = _din(nc, "xln3T", [4096, 1024], BF16)
    aps["gw1t"] = _din(nc, "gw1t", [4096, 32], F32)
    aps["gb1"] = _din(nc, "gb1", [1, 128], F32)
    aps["lng"] = _din(nc, "lng", [1, 128], F32)
    aps["lnb"] = _din(nc, "lnb", [1, 128], F32)
    aps["gw2t"] = _din(nc, "gw2t", [32, 128], F32)
    aps["gb2"] = _din(nc, "gb2", [1, 128], F32)
    aps["lt8"] = _din(nc, "lt8", [8, 128], BF16)
    aps["a1cat"] = _din(nc, "a1cat", [4096, 128], F32)
    aps["eye"] = _din(nc, "eye", [128, 128], BF16)
    if with_main:
        aps["b1stack"] = _din(nc, "b1stack", [128, 4096], F32)
        aps["w4blk"] = _din(nc, "w4blk", [16, 128, 32 * 256], BF16)
        aps["cnts_all"] = _din(nc, "cnts_all", [8, 128], F32)
        aps["xs3"] = _din(nc, "xs3", [1024, 4096], F32)
        aps["yffn2"] = nc.dram_tensor("yffn2", [1024, 4096], F32, kind="Internal").ap()
        aps["out"] = _dout(nc, "out", [1024, 4096], F32)
    else:
        aps["cnt_loc"] = _dout(nc, "cnt_loc", [8, 128], F32)
    return aps


def build_E():
    nc = bass.Bass("TRN2", target_bir_lowering=False)
    aps = _ph4_aps(nc, False)
    with ExitStack() as ctx:
        ar = Arena(nc, ctx)
        p = Prog(nc)
        emit_ph4(p, ar, aps, with_counts_out=True, with_main=False)
        p.finalize(ctx)
    return nc


def build_F():
    nc = bass.Bass("TRN2", target_bir_lowering=False)
    aps = _ph4_aps(nc, True)
    with ExitStack() as ctx:
        ar = Arena(nc, ctx)
        p = Prog(nc)
        emit_ph4(p, ar, aps, with_counts_out=False, with_main=True)
        p.finalize(ctx)
    return nc





def build_fused():
    nc = bass.Bass("TRN2", target_bir_lowering=False)
    aps = {}
    aps["x0s"] = _din(nc, "x0s", [1024, 4096], F32)
    aps["w0blk"] = _din(nc, "w0blk", [12, 128, 16384], BF16)
    for n in ("cos", "sin", "cosq", "sinq"):
        aps[n] = _din(nc, n, [1024, 128], F32)
    aps["eye"] = _din(nc, "eye", [128, 128], BF16)
    aps["masks"] = _din(nc, "masks", [32, 128, 256], BF16)
    aps["scale1"] = _din(nc, "scale1", [5, 128, 256], F32)
    aps["ksb"] = _din(nc, "ksb", [128, 640], BF16)
    aps["w1blk"] = _din(nc, "w1blk", [16, 128, 32 * 256], BF16)
    aps["w2blk"] = _din(nc, "w2blk", [32, 128, 32 * 128], BF16)
    aps["bias0"] = _din(nc, "bias0", [1, 4096], F32)
    aps["a0t"] = _din(nc, "a0t", [3, 4096, 16], F32)
    aps["b0t"] = _din(nc, "b0t", [3, 16, 4096], F32)
    aps["lsi"] = _din(nc, "lsi", [1, 128], I32)
    aps["bidx"] = _din(nc, "bidx", [1, 128], F32)
    aps["erow3"] = _din(nc, "erow3", [1, 128], F32)
    aps["wq_d"] = _din(nc, "wq_d", [4096, 12], F32)
    aps["wkv_d"] = _din(nc, "wkv_d", [4096, 4], F32)
    aps["w_qk"] = _din(nc, "w_qk", [12, 128], F32)
    aps["bq_d"] = _din(nc, "bq_d", [1, 128], F32)
    aps["bkv_d"] = _din(nc, "bkv_d", [1, 128], F32)
    aps["b_qk"] = _din(nc, "b_qk", [1, 128], F32)
    aps["wv5"] = _din(nc, "wv5", [5, 4096], F32)
    aps["sel5"] = _din(nc, "sel5", [5, 128], F32)
    aps["w3blk"] = _din(nc, "w3blk", [16, 128, 32 * 256], BF16)
    aps["bo"] = _din(nc, "bo", [1, 4096], F32)
    aps["gw1t"] = _din(nc, "gw1t", [4096, 32], F32)
    aps["gb1"] = _din(nc, "gb1", [1, 128], F32)
    aps["lng"] = _din(nc, "lng", [1, 128], F32)
    aps["lnb"] = _din(nc, "lnb", [1, 128], F32)
    aps["gw2t"] = _din(nc, "gw2t", [32, 128], F32)
    aps["gb2"] = _din(nc, "gb2", [1, 128], F32)
    aps["lt8"] = _din(nc, "lt8", [8, 128], BF16)
    aps["a1cat"] = _din(nc, "a1cat", [4096, 128], F32)
    aps["b1stack"] = _din(nc, "b1stack", [128, 4096], F32)
    aps["w4blk"] = _din(nc, "w4blk", [16, 128, 32 * 256], BF16)
    def _int(name, shape, dt):
        return nc.dram_tensor(name, list(shape), dt, kind="Internal").ap()
    aps["qtT"] = _int("qtT", [4096, 1024], BF16)
    aps["kvloc"] = _int("kvloc", [1024, 2048], BF16)
    aps["kn2"] = _int("kn2", [1024, 8], F32)
    aps["kvs"] = _int("kvs", [4096, 2048], BF16)
    aps["aout"] = _int("aout", [1024, 4096], F32)
    aps["xs1"] = _int("xs1", [1024, 4096], F32)
    aps["xln1T"] = _int("xln1T", [4096, 1024], BF16)
    aps["yffn"] = _int("yffn", [1024, 4096], F32)
    aps["xs2"] = _int("xs2", [1024, 4096], F32)
    aps["xln2T"] = _int("xln2T", [4096, 1024], BF16)
    aps["ckvloc"] = _int("ckvloc", [1024, 128], F32)
    aps["ckvg"] = _int("ckvg", [4096, 128], F32)
    aps["q1T"] = _int("q1T", [4, 32 * 1024], BF16)
    aps["aout2"] = _int("aout2", [1024, 4096], F32)
    aps["xs3"] = _int("xs3", [1024, 4096], F32)
    aps["xln3T"] = _int("xln3T", [4096, 1024], BF16)
    aps["yffn2"] = _int("yffn2", [1024, 4096], F32)
    aps["cnt_loc"] = _int("cnt_loc", [8, 128], F32)
    aps["cnts_all"] = _int("cnt_sum", [8, 128], F32)
    aps["out"] = _dout(nc, "out", [1024, 4096], F32)

    RG2 = [[0, 1, 2, 3], [4, 5, 6, 7]]
    with ExitStack() as ctx:
        ar = Arena(nc, ctx)
        p = Prog(nc)
        eye = ar.sbuf("eye", [128, 128], BF16)
        p.op("sync", lambda e: e.dma_start(eye[:, :], aps["eye"]), writes=[ar.b["eye"]], dma=True)
        onf = ar.sbuf("ones_f", [1, 128], F32)
        p.op("vector", lambda e: e.memset(onf[:, :], 1.0), writes=[ar.b["ones_f"]])
        onesr = ar.sbuf("ones_bfr", [1, 128], BF16)
        p.op("vector", lambda e: e.memset(onesr[:, :], 1.0), writes=[ar.b["ones_bfr"]])
        onesc = ar.sbuf("ones_bf", [128, 1], BF16)
        p.op("vector", lambda e: e.memset(onesc[:, :], 1.0), writes=[ar.b["ones_bf"]])
        for v in (-CBIAS_L0, -CBIAS_L1, LN_EPS, -0.5):
            ar.const_col(p, v)
        kvloc_buf = ar.buf("dram_kvloc")
        kvs_buf = ar.buf("dram_kvs")
        aps["kvloc_buf"] = kvloc_buf
        aps["kvs_buf"] = kvs_buf
        ckvl_buf = ar.buf("dram_ckvloc")
        ckvg_buf = ar.buf("dram_ckvg")
        aps["ckvloc_buf"] = ckvl_buf
        aps["ckvg_buf"] = ckvg_buf
        aps["cntl_buf"] = ar.buf("dram_cntl")
        aps["cnts_buf"] = ar.buf("dram_cnts")

        with ExitStack() as s:
            ar.push(s)
            emit_ph0(p, ar, aps)
            p.barrier()
            ar.pop()
        p.op("gpsimd", lambda e: e.collective_compute(
            "AllGather", ALU.bypass, RG2, [aps["kvloc"].opt()], [aps["kvs"].opt()]),
            writes=[kvs_buf])
        with ExitStack() as s:
            ar.push(s)
            st = emit_ph1_attn(p, ar, aps)
            emit_ph1_wo(p, ar, aps, st["aoT"], st["baoT"])
            emit_lnres(p, ar, "p1c", aps["aout"], aps["x0s"], aps["xs1"], aps["xln1T"])
            p.barrier()
            ar.pop()
        with ExitStack() as s:
            ar.push(s)
            emit_ph2(p, ar, aps)
            ar.alias("p1_tr0", "p2_ptt")
            ar.alias("p1_tr1", "p2_ptt")
            xlt_buf = ar.buf("xlt2")
            emit_lnres(p, ar, "p2c", aps["yffn"], aps["xs1"], aps["xs2"], aps["xln2T"],
                       xlt_buf=xlt_buf)
            emit_l1pre(p, ar, aps, aps["xln2T"], xlt_buf)
            p.barrier()
            ar.pop()
        p.op("gpsimd", lambda e: e.collective_compute(
            "AllGather", ALU.bypass, RG2, [aps["ckvloc"].opt()], [aps["ckvg"].opt()]),
            writes=[ckvg_buf])
        with ExitStack() as s:
            ar.push(s)
            st = emit_ph3(p, ar, aps)
            ar.alias("p1_tr0", "p3_ptc")
            ar.alias("p1_tr1", "p3_ptc")
            ar.alias("pbc", "p3_sc0")
            ar.alias("p3o_mm0", "p3_sc0")
            ar.alias("p3o_mm1", "p3_sc1")
            emit_wo_generic(p, ar, "p3o", aps["w3blk"], st["aoT2"], st["baoT2"],
                            aps["aout2"], bias_row=aps["bo"])
            emit_lnres(p, ar, "p3c", aps["aout2"], aps["xs2"], aps["xs3"], aps["xln3T"])
            p.barrier()
            ar.pop()
        with ExitStack() as s:
            ar.push(s)
            emit_ph4(p, ar, aps, with_counts_out=True, with_main=True, fused_cc=True)
            p.finalize(ctx)
            ar.pop()
    return nc


def kernel_fused(**inputs):
    pr = host_prep(inputs)
    x0 = pr["x0"]
    cos_t, sin_t, sq = pr["cos_t"], pr["sin_t"], pr["sq"]
    rows_all = [host_slot_rows(c) for c in range(NCORES)]
    nc = build_fused()
    in_maps = []
    for c in range(NCORES):
        rows = rows_all[c]
        pos = rows % SEQ
        in_maps.append({
            "x0s": np.ascontiguousarray(x0[rows]), "w0blk": pr["w0blk"],
            "cos": np.ascontiguousarray(cos_t[pos]),
            "sin": np.ascontiguousarray(sin_t[pos]),
            "cosq": np.ascontiguousarray(cos_t[pos]) * sq,
            "sinq": np.ascontiguousarray(sin_t[pos]) * sq,
            "eye": pr["eye"], "masks": host_masks(c), "scale1": host_scale1(c),
            "ksb": pr["ksb"], "w1blk": pr["w1blk"], "w2blk": pr["w2blk"],
            "bias0": pr["bias0"], "a0t": pr["a0t"], "b0t": pr["b0t"],
            "lsi": pr["lsi"], "bidx": np.full((1, 128), float(c // 4), dtype=np.float32),
            "erow3": pr["erow3"], "wq_d": pr["wq_d"], "wkv_d": pr["wkv_d"],
            "w_qk": pr["w_qk"], "bq_d": pr["bq_d"], "bkv_d": pr["bkv_d"],
            "b_qk": pr["b_qk"], "wv5": pr["wv5"], "sel5": pr["sel5"],
            "w3blk": pr["w3blk"], "bo": pr["bo"], "gw1t": pr["gw1t"],
            "gb1": pr["gb1"], "lng": pr["lng"], "lnb": pr["lnb"],
            "gw2t": pr["gw2t"], "gb2": pr["gb2"], "lt8": pr["lt8"],
            "a1cat": pr["a1cat"], "b1stack": pr["b1stack"], "w4blk": pr["w4blk"],
        })
    res = run_bass_kernel_spmd(nc, in_maps, core_ids=list(range(NCORES)))
    out = np.empty((8192, D), dtype=np.float32)
    for c in range(NCORES):
        out[rows_all[c]] = np.asarray(res.results[c]["out"])
    return out.reshape(2, SEQ, D)


def kernel_unfused(**inputs):
    pr = host_prep(inputs)
    x0 = pr["x0"]
    cos_t, sin_t, sq = pr["cos_t"], pr["sin_t"], pr["sq"]
    rows_all = [host_slot_rows(c) for c in range(NCORES)]

    ncA = build_A()
    inA = []
    for c in range(NCORES):
        rows = rows_all[c]
        pos = rows % SEQ
        inA.append({"x0s": np.ascontiguousarray(x0[rows]), "w0blk": pr["w0blk"],
                    "cos": np.ascontiguousarray(cos_t[pos]),
                    "sin": np.ascontiguousarray(sin_t[pos]),
                    "cosq": np.ascontiguousarray(cos_t[pos]) * sq,
                    "sinq": np.ascontiguousarray(sin_t[pos]) * sq,
                    "eye": pr["eye"]})
    rA = run_bass_kernel_spmd(ncA, inA, core_ids=list(range(NCORES)))

    ncB = build_B()
    inB = []
    for c in range(NCORES):
        b = c // 4
        kvs = np.ascontiguousarray(np.concatenate(
            [np.asarray(rA.results[4 * b + j]["kvloc"]) for j in range(4)], axis=0))
        inB.append({"x0s": inA[c]["x0s"], "qtT": np.asarray(rA.results[c]["qtT"]),
                    "kvs": kvs, "masks": host_masks(c), "scale1": host_scale1(c),
                    "ksb": pr["ksb"], "eye": pr["eye"], "w1blk": pr["w1blk"]})
    rB = run_bass_kernel_spmd(ncB, inB, core_ids=list(range(NCORES)))

    ncC = build_C()
    inC = []
    for c in range(NCORES):
        inC.append({"xln1T": np.asarray(rB.results[c]["xln1T"]),
                    "xs1": np.asarray(rB.results[c]["xs1"]),
                    "w2blk": pr["w2blk"], "bias0": pr["bias0"], "a0t": pr["a0t"],
                    "b0t": pr["b0t"], "lsi": pr["lsi"],
                    "bidx": np.full((1, 128), float(c // 4), dtype=np.float32),
                    "erow3": pr["erow3"], "eye": pr["eye"],
                    "wq_d": pr["wq_d"], "wkv_d": pr["wkv_d"], "w_qk": pr["w_qk"],
                    "bq_d": pr["bq_d"], "bkv_d": pr["bkv_d"], "b_qk": pr["b_qk"]})
    rC = run_bass_kernel_spmd(ncC, inC, core_ids=list(range(NCORES)))

    ncD = build_D()
    inD = []
    for c in range(NCORES):
        b = c // 4
        ckvg = np.ascontiguousarray(np.concatenate(
            [np.asarray(rC.results[4 * b + j]["ckvloc"]) for j in range(4)], axis=0))
        inD.append({"q1T": np.asarray(rC.results[c]["q1T"]), "ckvg": ckvg,
                    "wv5": pr["wv5"], "sel5": pr["sel5"], "eye": pr["eye"],
                    "w3blk": pr["w3blk"], "bo": pr["bo"],
                    "xs2": np.asarray(rC.results[c]["xs2"])})
    rD = run_bass_kernel_spmd(ncD, inD, core_ids=list(range(NCORES)))

    ncE = build_E()
    inE = []
    for c in range(NCORES):
        inE.append({"xln3T": np.asarray(rD.results[c]["xln3T"]),
                    "gw1t": pr["gw1t"], "gb1": pr["gb1"], "lng": pr["lng"],
                    "lnb": pr["lnb"], "gw2t": pr["gw2t"], "gb2": pr["gb2"],
                    "lt8": pr["lt8"], "a1cat": pr["a1cat"], "eye": pr["eye"]})
    rE = run_bass_kernel_spmd(ncE, inE, core_ids=list(range(NCORES)))
    cnts_all = np.zeros((8, 128), np.float32)
    cnts_all[:, 0:16] = np.concatenate(
        [np.asarray(rE.results[c]["cnt_loc"])[:, 0:2] for c in range(NCORES)], axis=1)

    ncF = build_F()
    inF = []
    for c in range(NCORES):
        inF.append({"xln3T": np.asarray(rD.results[c]["xln3T"]),
                    "gw1t": pr["gw1t"], "gb1": pr["gb1"], "lng": pr["lng"],
                    "lnb": pr["lnb"], "gw2t": pr["gw2t"], "gb2": pr["gb2"],
                    "lt8": pr["lt8"], "a1cat": pr["a1cat"], "eye": pr["eye"],
                    "b1stack": pr["b1stack"], "w4blk": pr["w4blk"],
                    "cnts_all": cnts_all,
                    "xs3": np.asarray(rD.results[c]["xs3"])})
    rF = run_bass_kernel_spmd(ncF, inF, core_ids=list(range(NCORES)))

    out = np.empty((8192, D), dtype=np.float32)
    for c in range(NCORES):
        out[rows_all[c]] = np.asarray(rF.results[c]["out"])
    return out.reshape(2, SEQ, D)


def kernel(**inputs):
    return kernel_fused(**inputs)
